# version 05
# speedup vs baseline: 1.2597x; 1.0249x over previous
import numpy as np
from contextlib import ExitStack

import concourse.bass as bass
import concourse.mybir as mybir
from concourse.bass_utils import run_bass_kernel_spmd

F32 = mybir.dt.float32
BF16 = mybir.dt.bfloat16
AF = mybir.ActivationFunctionType
ALU = mybir.AluOpType
AX = mybir.AxisListType.X

NCORES = 8
BPC = 2
T = 2048
TOK = BPC * T
D = 1024
DT = D // 128
FC = 512
NFC = TOK // FC
LN_EPS = 1e-5



class _Buf:
    __slots__ = ("name", "last_w", "readers")

    def __init__(self, name):
        self.name = name
        self.last_w = None
        self.readers = []


class _Sem:
    __slots__ = ("h", "v")

    def __init__(self, h):
        self.h = h
        self.v = 0


class Prog:

    ENGINES = ("sync", "gpsimd", "tensor", "vector", "scalar")

    def __init__(self, nc, es):
        self.nc = nc
        self.es = es
        self.ops = {e: [] for e in self.ENGINES}
        self._semn = 0
        self.esem = {e: self.new_sem() for e in self.ENGINES}
        self.dsem = {e: self.new_sem() for e in ("sync", "gpsimd", "scalar", "vector")}
        self.nbuf = 0

    def new_sem(self):
        h = self.es.enter_context(self.nc.semaphore(f"S{self._semn}"))
        self._semn += 1
        return _Sem(h)

    def buf(self, name=None):
        self.nbuf += 1
        return _Buf(name or f"b{self.nbuf}")

    def barrier(self):
        snap = []
        for s in list(self.esem.values()) + list(self.dsem.values()):
            if s.v > 0:
                snap.append((s, s.v))
        self._pending_barrier = {e: list(snap) for e in self.ENGINES}

    def op(self, eng, fn, reads=(), writes=(), dma=False, tag="", extra_waits=()):
        waits = {}

        def add_wait(sv):
            if sv is None:
                return
            s, v = sv
            if waits.get(s, 0) < v:
                waits[s] = v

        pend = getattr(self, "_pending_barrier", None)
        if pend and pend.get(eng):
            for sv in pend[eng]:
                add_wait(sv)
            pend[eng] = []
        for b in reads:
            add_wait(b.last_w)
        for b in writes:
            add_wait(b.last_w)
            for r in b.readers:
                add_wait(r)
        for w in extra_waits:
            add_wait(w)
        if dma:
            sem = self.dsem[eng]
            n = 16
        else:
            sem = self.esem[eng]
            n = 1
        sem.v += n
        tok = (sem, sem.v)
        self.ops[eng].append((list(waits.items()), fn, (sem, n), tag))
        for b in reads:
            b.readers.append(tok)
        for b in writes:
            b.last_w = tok
            b.readers = []
        return tok

    def emit(self):
        waited = {e: {} for e in self.ENGINES}

        def mk(ename):
            def body(engobj):
                w8 = waited[ename]
                for waits, fn, (sem, n), tag in self.ops[ename]:
                    for s, v in waits:
                        if w8.get(s, 0) >= v:
                            continue
                        engobj.wait_ge(s.h, v)
                        w8[s] = v
                    instr = fn(engobj)
                    if instr is None:
                        raise RuntimeError(f"op '{tag}' on {ename} returned None")
                    instr.then_inc(sem.h, n)
            return body

        with self.nc.Block() as block:
            block.sync(mk("sync"))
            block.gpsimd(mk("gpsimd"))
            block.tensor(mk("tensor"))
            block.vector(mk("vector"))
            block.scalar(mk("scalar"))



def ceil_div(a, b):
    return (a + b - 1) // b


class Env:
    def __init__(self, nc, es, P):
        self.nc = nc
        self.es = es
        self.P = P
        self._uid = 0
        self.psum = []
        for i in range(6):
            t = es.enter_context(nc.psum_tensor(f"pb{i}", [128, FC], F32))
            self.psum.append((t, P.buf(f"pb{i}")))
        self._psum_rr = 0
        self.psum_bf = []
        for i in range(2):
            t = es.enter_context(nc.psum_tensor(f"pbf{i}", [128, FC], BF16))
            self.psum_bf.append((t, P.buf(f"pbf{i}")))
        self._psum_bf_rr = 0

    def uid(self):
        self._uid += 1
        return self._uid

    def sb(self, name, shape, dtype):
        scope = getattr(self, "_scope", None) or self.es
        t = scope.enter_context(self.nc.sbuf_tensor(f"{name}{self.uid()}", shape, dtype))
        return t, self.P.buf(name)

    def sb_shared(self, name, shape, dtype):
        t = self.es.enter_context(self.nc.sbuf_tensor(f"{name}{self.uid()}", shape, dtype))
        return t, self.P.buf(name)

    def bank(self):
        t, b = self.psum[self._psum_rr]
        self._psum_rr = (self._psum_rr + 1) % len(self.psum)
        return t, b

    def bank_bf(self):
        t, b = self.psum_bf[self._psum_bf_rr]
        self._psum_bf_rr = (self._psum_bf_rr + 1) % len(self.psum_bf)
        return t, b

    def const_col(self, val):
        if not hasattr(self, "_ccol"):
            self._ccol = {}
        if val not in self._ccol:
            t, b = self.sb_shared(f"cc{len(self._ccol)}_", [128, 1], F32)
            self.P.op("vector", lambda e: e.memset(t[:, :], float(val)), writes=[b], tag="const")
            self._ccol[val] = (t, b)
        return self._ccol[val]

    def ln_scratch(self, fc):
        if not hasattr(self, "_ln_scr"):
            self._ln_scr = {}
        if fc not in self._ln_scr:
            key = f"lnS{fc}_"
            ones, ones_b = self.sb_shared(key + "o", [128, 1], F32)
            onesr, onesr_b = self.sb_shared(key + "r", [1, 128], F32)
            self.P.op("vector", lambda e: e.memset(ones[:, :], 1.0), writes=[ones_b], tag="lno")
            self.P.op("vector", lambda e: e.memset(onesr[:, :], 1.0), writes=[onesr_b], tag="lnr")
            self._ln_scr[fc] = {
                "ones": (ones, ones_b), "onesr": (onesr, onesr_b),
                "xsq": self.sb_shared(key + "q", [128, fc], F32),
                "mean": self.sb_shared(key + "m", [1, fc], F32),
                "msq": self.sb_shared(key + "s", [1, fc], F32),
                "rstd": self.sb_shared(key + "d", [1, fc], F32),
                "shift": self.sb_shared(key + "h", [1, fc], F32),
                "rep": self.sb_shared(key + "p", [128, 2 * fc], F32),
            }
        return self._ln_scr[fc]


def load_wt_bf16(env, name, w_dram, rows, cols):
    P = env.P
    kt = ceil_div(rows, 128)
    slab, buf = env.sb(name, [128, kt * cols], BF16)
    src = w_dram.rearrange("(a p) c -> p a c", p=128)
    P.op("gpsimd", lambda e: e.dma_start(out=slab[:, :], in_=src), writes=[buf],
         dma=True, tag=f"load_{name}")
    return (slab, buf, kt, cols)


def wt_tile(slabinfo, k, m0, m1):
    slab, buf, kt, cols = slabinfo
    return slab[:, k * cols + m0: k * cols + m1]



def emit_ln_chunk(env, x_sb, x_buf, dtiles, fc):
    P = env.P
    scr = env.ln_scratch(fc)
    ones, ones_b = scr["ones"]
    onesr, onesr_b = scr["onesr"]
    xsq, xsq_b = scr["xsq"]
    mean, mean_b = scr["mean"]
    msq, msq_b = scr["msq"]
    rstd, rstd_b = scr["rstd"]
    shift, shift_b = scr["shift"]
    rep, rep_b = scr["rep"]

    bank_s, bs_b = env.bank()
    bank_q, bq_b = env.bank()

    def mm_sum(e):
        last = None
        for dt in range(dtiles):
            last = e.matmul(bank_s[0:1, 0:fc], ones[:, :], x_sb[:, dt * fc:(dt + 1) * fc],
                            start=(dt == 0), stop=(dt == dtiles - 1))
        return last
    P.op("tensor", mm_sum, reads=[ones_b, x_buf], writes=[bs_b], tag="ln_sum")

    for dt in range(dtiles):
        P.op("scalar", lambda e, dt=dt: e.activation(
            xsq[:, 0:fc], x_sb[:, dt * fc:(dt + 1) * fc], AF.Square),
            reads=[x_buf], writes=[xsq_b], tag="ln_sq")
        P.op("tensor", lambda e, dt=dt: e.matmul(
            bank_q[0:1, 0:fc], ones[:, :], xsq[:, 0:fc],
            start=(dt == 0), stop=(dt == dtiles - 1)),
            reads=[ones_b, xsq_b], writes=[bq_b], tag="ln_sumsq")

    ndiv = 1.0 / (dtiles * 128)
    P.op("scalar", lambda e: e.activation(mean[0:1, :], bank_s[0:1, 0:fc], AF.Copy, scale=ndiv),
         reads=[bs_b], writes=[mean_b], tag="ln_mean")
    P.op("scalar", lambda e: e.activation(msq[0:1, :], bank_q[0:1, 0:fc], AF.Copy, scale=ndiv),
         reads=[bq_b], writes=[msq_b], tag="ln_msq")
    P.op("scalar", lambda e: e.activation(rstd[0:1, :], mean[0:1, :], AF.Square),
         reads=[mean_b], writes=[rstd_b], tag="ln_m2")
    P.op("vector", lambda e: e.tensor_sub(rstd[0:1, :], msq[0:1, :], rstd[0:1, :]),
         reads=[msq_b, rstd_b], writes=[rstd_b], tag="ln_var")
    epsc, epsc_b = env.const_col(LN_EPS)
    P.op("scalar", lambda e: e.activation(rstd[0:1, :], rstd[0:1, :], AF.Ln, bias=epsc[0:1, 0:1]),
         reads=[rstd_b, epsc_b], writes=[rstd_b], tag="ln_ln")
    P.op("scalar", lambda e: e.activation(rstd[0:1, :], rstd[0:1, :], AF.Exp, scale=-0.5),
         reads=[rstd_b], writes=[rstd_b], tag="ln_rstd")
    P.op("vector", lambda e: e.tensor_mul(shift[0:1, :], mean[0:1, :], rstd[0:1, :]),
         reads=[mean_b, rstd_b], writes=[shift_b], tag="ln_mr")
    P.op("scalar", lambda e: e.activation(shift[0:1, :], shift[0:1, :], AF.Copy, scale=-1.0),
         reads=[shift_b], writes=[shift_b], tag="ln_neg")
    for r, (src, src_b) in enumerate(((rstd, rstd_b), (shift, shift_b))):
        bank_r, br_b = env.bank()
        P.op("tensor", lambda e, src=src, bank_r=bank_r: e.matmul(
            bank_r[:, 0:fc], onesr[:, :], src[0:1, :], start=True, stop=True),
            reads=[onesr_b, src_b], writes=[br_b], tag="ln_rep")
        P.op("scalar", lambda e, r=r, bank_r=bank_r: e.activation(
            rep[:, r * fc:(r + 1) * fc], bank_r[:, 0:fc], AF.Copy),
            reads=[br_b], writes=[rep_b], tag="ln_repc")
    for dt in range(dtiles):
        sl = slice(dt * fc, (dt + 1) * fc)
        P.op("vector", lambda e, sl=sl: e.tensor_mul(x_sb[:, sl], x_sb[:, sl], rep[:, 0:fc]),
             reads=[x_buf, rep_b], writes=[x_buf], tag="ln_scl")
        P.op("vector", lambda e, sl=sl: e.tensor_add(x_sb[:, sl], x_sb[:, sl], rep[:, fc:2 * fc]),
             reads=[x_buf, rep_b], writes=[x_buf], tag="ln_sft")



def x_chunk_src(x_dram, f0, fc):
    return x_dram.rearrange("(a p) c -> p a c", p=128)[:, :, f0:f0 + fc]


def emit_l3_ffn(env, x_in, x_out, drams):
    P = env.P
    key = f"l3f{env.uid()}"
    w1 = load_wt_bf16(env, key + "w1", drams["l3f_w1t"], 1024, 4096)
    w2 = load_wt_bf16(env, key + "w2", drams["l3f_w2t"], 4096, 1024)
    b1, b1_b = env.sb(key + "b1", [128, 32], F32)
    b2, b2_b = env.sb(key + "b2", [128, 8], F32)
    P.op("sync", lambda e: e.dma_start(out=b1[:, :], in_=drams["l3f_b1"]), writes=[b1_b], dma=True, tag="b1")
    P.op("sync", lambda e: e.dma_start(out=b2[:, :], in_=drams["l3f_b2"]), writes=[b2_b], dma=True, tag="b2")

    xc, xc_b = env.sb(key + "xc", [128, DT * FC], F32)
    xb, xb_b = env.sb(key + "xb", [128, DT * FC], BF16)
    hb, hb_b = env.sb(key + "hb", [128, 32 * FC], BF16)
    x_in_b = drams["_xin_buf"]
    x_out_b = drams["_xout_buf"]

    for f0 in range(0, TOK, FC):
        P.op("sync", lambda e, f0=f0: e.dma_start(out=xc[:, :], in_=x_chunk_src(x_in, f0, FC)),
             reads=[x_in_b], writes=[xc_b], dma=True, tag="xload")
        P.op("gpsimd", lambda e: e.dma_start(out=xb[:, :], in_=xc[:, :]),
             reads=[xc_b], writes=[xb_b], dma=True, tag="xcast")
        for mt in range(32):
            bank, bk_b = env.bank()

            def mm1(e, mt=mt, bank=bank):
                last = None
                for k in range(DT):
                    last = e.matmul(bank[:, 0:FC], wt_tile(w1, k, mt * 128, (mt + 1) * 128),
                                    xb[:, k * FC:(k + 1) * FC], start=(k == 0), stop=(k == DT - 1))
                return last
            P.op("tensor", mm1, reads=[w1[1], xb_b], writes=[bk_b], tag="mm1")
            P.op("scalar", lambda e, mt=mt, bank=bank: e.activation(
                hb[:, mt * FC:(mt + 1) * FC], bank[:, 0:FC], AF.Relu, bias=b1[:, mt:mt + 1]),
                reads=[bk_b, b1_b], writes=[hb_b], tag="relu")
        for mt in range(DT):
            bank, bk_b = env.bank()

            def mm2(e, mt=mt, bank=bank):
                last = None
                for k in range(32):
                    last = e.matmul(bank[:, 0:FC], wt_tile(w2, k, mt * 128, (mt + 1) * 128),
                                    hb[:, k * FC:(k + 1) * FC], start=(k == 0), stop=(k == 31))
                return last
            P.op("tensor", mm2, reads=[w2[1], hb_b], writes=[bk_b], tag="mm2")
            P.op("vector", lambda e, mt=mt, bank=bank: e.scalar_tensor_tensor(
                xc[:, mt * FC:(mt + 1) * FC], bank[:, 0:FC], b2[:, mt:mt + 1],
                xc[:, mt * FC:(mt + 1) * FC], ALU.add, ALU.add),
                reads=[bk_b, b2_b, xc_b], writes=[xc_b], tag="residadd")
        emit_ln_chunk(env, xc, xc_b, DT, FC)
        P.op("sync", lambda e, f0=f0: e.dma_start(out=x_chunk_src(x_out, f0, FC), in_=xc[:, :]),
             reads=[xc_b], writes=[x_out_b], dma=True, tag="xstore")


def emit_l0_ffn(env, x_in, x_out, drams):
    P = env.P
    key = f"l0f{env.uid()}"
    u1 = load_wt_bf16(env, key + "u1", drams["l0f_u1"], 1024, 256)
    v1 = load_wt_bf16(env, key + "v1", drams["l0f_v1"], 256, 4096)
    u2 = load_wt_bf16(env, key + "u2", drams["l0f_u2"], 4096, 256)
    v2 = load_wt_bf16(env, key + "v2", drams["l0f_v2"], 256, 1024)
    b1, b1_b = env.sb(key + "b1", [128, 32], F32)
    b2, b2_b = env.sb(key + "b2", [128, 8], F32)
    P.op("sync", lambda e: e.dma_start(out=b1[:, :], in_=drams["l0f_b1"]), writes=[b1_b], dma=True, tag="b1")
    P.op("sync", lambda e: e.dma_start(out=b2[:, :], in_=drams["l0f_b2"]), writes=[b2_b], dma=True, tag="b2")

    xc, xc_b = env.sb(key + "xc", [128, DT * FC], F32)
    xb, xb_b = env.sb(key + "xb", [128, DT * FC], BF16)
    pb, pb_b = env.sb(key + "pb", [128, 2 * FC], BF16)
    z, z_b = env.sb(key + "z", [128, FC], F32)
    z2, z2_b = env.sb(key + "z2", [128, FC], F32)
    hbuf, hb_b = env.sb(key + "hbuf", [128, 32 * FC], BF16)
    qb, qb_b = env.sb(key + "qb", [128, 2 * FC], BF16)
    x_in_b = drams["_xin_buf"]
    x_out_b = drams["_xout_buf"]

    for f0 in range(0, TOK, FC):
        P.op("sync", lambda e, f0=f0: e.dma_start(out=xc[:, :], in_=x_chunk_src(x_in, f0, FC)),
             reads=[x_in_b], writes=[xc_b], dma=True, tag="xload")
        P.op("gpsimd", lambda e: e.dma_start(out=xb[:, :], in_=xc[:, :]),
             reads=[xc_b], writes=[xb_b], dma=True, tag="xcast")
        for mt in range(2):
            bank, bk_b = env.bank()

            def mmp(e, mt=mt, bank=bank):
                last = None
                for k in range(DT):
                    last = e.matmul(bank[:, 0:FC], wt_tile(u1, k, mt * 128, (mt + 1) * 128),
                                    xb[:, k * FC:(k + 1) * FC], start=(k == 0), stop=(k == DT - 1))
                return last
            P.op("tensor", mmp, reads=[u1[1], xb_b], writes=[bk_b], tag="u1mm")
            P.op("scalar", lambda e, mt=mt, bank=bank: e.activation(
                pb[:, mt * FC:(mt + 1) * FC], bank[:, 0:FC], AF.Copy),
                reads=[bk_b], writes=[pb_b], tag="pcast")
        for mt in range(32):
            bank, bk_b = env.bank()

            def mmh(e, mt=mt, bank=bank):
                last = None
                for k in range(2):
                    last = e.matmul(bank[:, 0:FC], wt_tile(v1, k, mt * 128, (mt + 1) * 128),
                                    pb[:, k * FC:(k + 1) * FC], start=(k == 0), stop=(k == 1))
                return last
            P.op("tensor", mmh, reads=[v1[1], pb_b], writes=[bk_b], tag="v1mm")
            P.op("vector", lambda e, mt=mt, bank=bank: e.tensor_scalar(
                z[:, :], bank[:, 0:FC], b1[:, mt:mt + 1], None, ALU.add),
                reads=[bk_b, b1_b], writes=[z_b], tag="gz")
            P.op("scalar", lambda e: e.activation(z2[:, :], z[:, :], AF.Square),
                 reads=[z_b], writes=[z2_b], tag="gz2")
            P.op("vector", lambda e: e.tensor_scalar(z2[:, :], z2[:, :], 0.044715, 1.0, ALU.mult, ALU.add),
                 reads=[z2_b], writes=[z2_b], tag="gz2b")
            P.op("vector", lambda e: e.tensor_mul(z2[:, :], z[:, :], z2[:, :]),
                 reads=[z_b, z2_b], writes=[z2_b], tag="gz3")
            P.op("scalar", lambda e: e.activation(z2[:, :], z2[:, :], AF.Tanh, scale=0.7978845608028654),
                 reads=[z2_b], writes=[z2_b], tag="gth")
            P.op("vector", lambda e: e.scalar_tensor_tensor(z2[:, :], z2[:, :], 1.0, z[:, :], ALU.add, ALU.mult),
                 reads=[z2_b, z_b], writes=[z2_b], tag="gmul")
            P.op("scalar", lambda e, mt=mt: e.activation(
                hbuf[:, mt * FC:(mt + 1) * FC], z2[:, :], AF.Copy, scale=0.5),
                reads=[z2_b], writes=[hb_b], tag="gh")
        for mt in range(2):
            bank, bk_b = env.bank()

            def mmq(e, mt=mt, bank=bank):
                last = None
                for k in range(32):
                    last = e.matmul(bank[:, 0:FC], wt_tile(u2, k, mt * 128, (mt + 1) * 128),
                                    hbuf[:, k * FC:(k + 1) * FC], start=(k == 0), stop=(k == 31))
                return last
            P.op("tensor", mmq, reads=[u2[1], hb_b], writes=[bk_b], tag="u2mm")
            P.op("scalar", lambda e, mt=mt, bank=bank: e.activation(
                qb[:, mt * FC:(mt + 1) * FC], bank[:, 0:FC], AF.Copy),
                reads=[bk_b], writes=[qb_b], tag="qcast")
        for mt in range(DT):
            bank, bk_b = env.bank()

            def mmy(e, mt=mt, bank=bank):
                last = None
                for k in range(2):
                    last = e.matmul(bank[:, 0:FC], wt_tile(v2, k, mt * 128, (mt + 1) * 128),
                                    qb[:, k * FC:(k + 1) * FC], start=(k == 0), stop=(k == 1))
                return last
            P.op("tensor", mmy, reads=[v2[1], qb_b], writes=[bk_b], tag="v2mm")
            P.op("vector", lambda e, mt=mt, bank=bank: e.scalar_tensor_tensor(
                xc[:, mt * FC:(mt + 1) * FC], bank[:, 0:FC], b2[:, mt:mt + 1],
                xc[:, mt * FC:(mt + 1) * FC], ALU.add, ALU.add),
                reads=[bk_b, b2_b, xc_b], writes=[xc_b], tag="residadd")
        emit_ln_chunk(env, xc, xc_b, DT, FC)
        P.op("sync", lambda e, f0=f0: e.dma_start(out=x_chunk_src(x_out, f0, FC), in_=xc[:, :]),
             reads=[xc_b], writes=[x_out_b], dma=True, tag="xstore")


def emit_l3_attn(env, x_in, x_out, drams):
    P = env.P
    key = f"l3a{env.uid()}"
    wq = load_wt_bf16(env, key + "wq", drams["l3a_wqt"], 1024, 1024)
    wk = load_wt_bf16(env, key + "wk", drams["l3a_wkt"], 1024, 1024)
    wv = load_wt_bf16(env, key + "wv", drams["l3a_wvt"], 1024, 1024)
    identf, identf_b = env.sb(key + "idf", [128, 128], F32)
    P.op("sync", lambda e: e.dma_start(out=identf[:, :], in_=drams["ident_f32"]),
         writes=[identf_b], dma=True, tag="idf")
    identb, identb_b = env.sb(key + "idb", [128, 128], BF16)
    P.op("gpsimd", lambda e: e.dma_start(out=identb[:, :], in_=drams["ident_f32"]),
         writes=[identb_b], dma=True, tag="idb")
    triu, triu_b = env.sb(key + "tri", [128, 128], BF16)
    P.op("gpsimd", lambda e: e.dma_start(out=triu[:, :], in_=drams["triu01"]),
         writes=[triu_b], dma=True, tag="tri")
    onesc, onesc_b = env.sb(key + "onc", [128, 1], BF16)
    P.op("vector", lambda e: e.memset(onesc[:, :], 1.0), writes=[onesc_b], tag="onc")

    xc, xc_b = env.sb(key + "xc", [128, DT * FC], F32)
    xb, xb_b = env.sb(key + "xb", [128, DT * FC], BF16)
    NT = T // 128
    q_sb, q_b = env.sb(key + "q", [128, DT * T], BF16)
    k_sb, k_b = env.sb(key + "k", [128, DT * T], BF16)
    vc, vc_b = env.sb(key + "v", [128, DT * FC], BF16)
    vT, vT_b = env.sb(key + "vt", [128, NT * D], BF16)
    att_tok, att_tok_b = env.sb(key + "at", [128, D], F32)
    recip, recip_b = env.sb(key + "rc", [128, 1], F32)
    xq, xq_b = env.sb(key + "xq", [128, DT * 128], F32)
    attnT, attnT_b = env.sb(key + "an", [128, 128], BF16)
    x_in_b = drams["_xin_buf"]
    x_out_b = drams["_xout_buf"]

    for b in range(BPC):
        toff = b * T
        for ci in range(T // FC):
            f0 = toff + ci * FC
            P.op("sync", lambda e, f0=f0: e.dma_start(out=xc[:, :], in_=x_chunk_src(x_in, f0, FC)),
                 reads=[x_in_b], writes=[xc_b], dma=True, tag="xload")
            P.op("gpsimd", lambda e: e.dma_start(out=xb[:, :], in_=xc[:, :]),
                 reads=[xc_b], writes=[xb_b], dma=True, tag="xcast")
            for wslab, dst, dst_b, scl in ((wq, q_sb, q_b, None), (wk, k_sb, k_b, None), (wv, vc, vc_b, None)):
                for mt in range(DT):
                    bank, bk_b = env.bank()

                    def mm(e, wslab=wslab, mt=mt, bank=bank):
                        last = None
                        for k in range(DT):
                            last = e.matmul(bank[:, 0:FC], wt_tile(wslab, k, mt * 128, (mt + 1) * 128),
                                            xb[:, k * FC:(k + 1) * FC], start=(k == 0), stop=(k == DT - 1))
                        return last
                    P.op("tensor", mm, reads=[wslab[1], xb_b], writes=[bk_b], tag="proj")
                    if dst is vc:
                        sl = slice(mt * FC, (mt + 1) * FC)
                    else:
                        sl = slice(mt * T + ci * FC, mt * T + (ci + 1) * FC)
                    P.op("scalar", lambda e, sl=sl, dst=dst, bank=bank: e.activation(
                        dst[:, sl], bank[:, 0:FC], AF.Copy),
                        reads=[bk_b], writes=[dst_b], tag="pevac")
            for dt in range(DT):
                for ts in range(FC // 128):
                    tt = ci * (FC // 128) + ts
                    bank, bk_b = env.bank_bf()
                    P.op("tensor", lambda e, dt=dt, ts=ts, bank=bank: e.transpose(
                        bank[:, 0:128], vc[:, dt * FC + ts * 128: dt * FC + (ts + 1) * 128],
                        identb[:, :]), reads=[vc_b, identb_b], writes=[bk_b], tag="vtr")
                    P.op("scalar", lambda e, tt=tt, dt=dt, bank=bank: e.activation(
                        vT[:, tt * D + dt * 128: tt * D + (dt + 1) * 128],
                        bank[:, 0:128], AF.Copy),
                        reads=[bk_b], writes=[vT_b], tag="vtevac")
        for qc in range(NT):
            bank_av0, bav0_b = env.bank()
            bank_av1, bav1_b = env.bank()
            bank_nrm, bnrm_b = env.bank()
            bank_s, bs_b = env.bank()
            for kt in range(qc + 1):
                def mms(e, kt=kt, qc=qc, bank_s=bank_s):
                    last = None
                    for dt in range(DT):
                        last = e.matmul(bank_s[:, 0:128],
                                        k_sb[:, dt * T + kt * 128: dt * T + (kt + 1) * 128],
                                        q_sb[:, dt * T + qc * 128: dt * T + (qc + 1) * 128],
                                        start=(dt == 0), stop=(dt == DT - 1))
                    return last
                P.op("tensor", mms, reads=[k_b, q_b], writes=[bs_b], tag="smm")
                P.op("scalar", lambda e, bank_s=bank_s: e.activation(
                    attnT[:, :], bank_s[:, 0:128], AF.Exp, scale=float(1024 ** -0.5)),
                    reads=[bs_b], writes=[attnT_b], tag="sexp")
                if kt == qc:
                    P.op("vector", lambda e: e.tensor_mul(attnT[:, :], attnT[:, :], triu[:, :]),
                         reads=[attnT_b, triu_b], writes=[attnT_b], tag="smask")
                for half, (bk, bkb) in enumerate(((bank_av0, bav0_b), (bank_av1, bav1_b))):
                    P.op("tensor", lambda e, kt=kt, qc=qc, half=half, bk=bk: e.matmul(
                        bk[:, 0:FC], attnT[:, :],
                        vT[:, kt * D + half * FC: kt * D + (half + 1) * FC],
                        start=(kt == 0), stop=(kt == qc)),
                        reads=[attnT_b, vT_b], writes=[bkb], tag="avmm")
                P.op("tensor", lambda e, kt=kt, qc=qc, bank_nrm=bank_nrm: e.matmul(
                    bank_nrm[:, 0:1], attnT[:, :], onesc[:, :],
                    start=(kt == 0), stop=(kt == qc)),
                    reads=[attnT_b, onesc_b], writes=[bnrm_b], tag="nrmm")
            P.op("vector", lambda e, bank_nrm=bank_nrm: e.reciprocal(recip[:, 0:1], bank_nrm[:, 0:1]),
                 reads=[bnrm_b], writes=[recip_b], tag="recip")
            for half, (bk, bkb) in enumerate(((bank_av0, bav0_b), (bank_av1, bav1_b))):
                P.op("scalar", lambda e, half=half, bk=bk: e.activation(
                    att_tok[:, half * FC:(half + 1) * FC], bk[:, 0:FC], AF.Copy, scale=recip[:, 0:1]),
                    reads=[bkb, recip_b], writes=[att_tok_b], tag="avnorm")
            f0 = toff + qc * 128
            P.op("sync", lambda e, f0=f0: e.dma_start(out=xq[:, :], in_=x_chunk_src(x_in, f0, 128)),
                 reads=[x_in_b], writes=[xq_b], dma=True, tag="xload2")
            for dt in range(DT):
                bank_t, bt_b = env.bank()
                P.op("tensor", lambda e, dt=dt, bank_t=bank_t: e.transpose(
                    bank_t[:, 0:128], att_tok[:, dt * 128:(dt + 1) * 128], identf[:, :]),
                    reads=[att_tok_b, identf_b], writes=[bt_b], tag="otr")
                P.op("vector", lambda e, dt=dt, bank_t=bank_t: e.tensor_add(
                    xq[:, dt * 128:(dt + 1) * 128], xq[:, dt * 128:(dt + 1) * 128], bank_t[:, 0:128]),
                    reads=[bt_b, xq_b], writes=[xq_b], tag="rad")
            emit_ln_chunk(env, xq, xq_b, DT, 128)
            P.op("sync", lambda e, f0=f0: e.dma_start(out=x_chunk_src(x_out, f0, 128), in_=xq[:, :]),
                 reads=[xq_b], writes=[x_out_b], dma=True, tag="xstore")


def _emit_mha(env, x_in, x_out, drams, alibi):
    P = env.P
    H, DH = 16, 64
    NT = T // 128
    pre = "l0a" if alibi else "l1a"
    key = f"{pre}{env.uid()}"
    if alibi:
        wq = load_wt_bf16(env, key + "wq", drams[pre + "_wqt"], 1024, 1024)
        wk = load_wt_bf16(env, key + "wk", drams[pre + "_wkt"], 1024, 1024)
        wv = load_wt_bf16(env, key + "wv", drams[pre + "_wvt"], 1024, 1024)
    else:
        wqkv = load_wt_bf16(env, key + "wqkv", drams[pre + "_wqkvt"], 1024, 3072)
    wo = load_wt_bf16(env, key + "wo", drams[pre + "_wot"], 1024, 1024)
    if not alibi:
        bout, bout_b = env.sb(key + "bo", [128, 8], F32)
        P.op("sync", lambda e: e.dma_start(out=bout[:, :], in_=drams[pre + "_bout"]),
             writes=[bout_b], dma=True, tag="bout")

    identf, identf_b = env.sb(key + "idf", [128, 128], F32)
    P.op("sync", lambda e: e.dma_start(out=identf[:, :], in_=drams["ident_f32"]),
         writes=[identf_b], dma=True, tag="idf")
    identb, identb_b = env.sb(key + "idb", [128, 128], BF16)
    P.op("gpsimd", lambda e: e.dma_start(out=identb[:, :], in_=drams["ident_f32"]),
         writes=[identb_b], dma=True, tag="idb")
    triu, triu_b = env.sb(key + "tri", [128, 128], BF16)
    P.op("gpsimd", lambda e: e.dma_start(out=triu[:, :], in_=drams["triu01"]),
         writes=[triu_b], dma=True, tag="tri")
    if alibi:
        trineg, trineg_b = env.sb(key + "trn", [128, 128], F32)
        P.op("sync", lambda e: e.dma_start(out=trineg[:, :], in_=drams["trineg"]),
             writes=[trineg_b], dma=True, tag="trn")

    xb, xb_b = env.sb(key + "xb", [128, DT * T], BF16)
    NAUG = 68 if alibi else 64
    ksl = [env.sb(key + "ks", [NAUG, T], BF16) for _ in range(2)]
    qsl = [env.sb(key + "qs", [NAUG, T], BF16) for _ in range(2)]
    vhl = [env.sb(key + "vh", [64, T], BF16) for _ in range(2)]
    vTl = [env.sb(key + "vt", [128, NT * 65], BF16) for _ in range(2)]
    atl = [env.sb(key + "an", [128, 512], BF16) for _ in range(2)]
    att_tok, att_tok_b = env.sb(key + "at", [128, 64], F32)
    recip, recip_b = env.sb(key + "rc", [128, 1], F32)
    aall, aall_b = env.sb(key + "aa", [128, DT * T], BF16)
    xc, xc_b = env.sb(key + "xc", [128, DT * FC], F32)
    x_in_b = drams["_xin_buf"]
    x_out_b = drams["_xout_buf"]

    slopes = [2.0 ** (-(h + 1) / 2.0) for h in range(H)]

    for b in range(BPC):
        toff = b * T
        P.op("gpsimd", lambda e, toff=toff: e.dma_start(
            out=xb.ap().rearrange("p (a c) -> p a c", a=DT),
            in_=x_in.rearrange("(a p) c -> p a c", p=128)[:, :, toff:toff + T]),
            reads=[x_in_b], writes=[xb_b], dma=True, tag="xbcast")
        for h in range(H):
            kslab, kslab_b = ksl[h % 2]
            qslab, qslab_b = qsl[h % 2]
            vh, vh_b = vhl[h % 2]
            vT, vT_b = vTl[h % 2]
            if alibi:
                plist = ((wq, qslab, qslab_b), (wk, kslab, kslab_b), (wv, vh, vh_b))
                offs = (h * 64, h * 64, h * 64)
            else:
                plist = ((wqkv, qslab, qslab_b), (wqkv, kslab, kslab_b), (wqkv, vh, vh_b))
                offs = (h * 64, 1024 + h * 64, 2048 + h * 64)
            for (slab, dst, dst_b), m0 in zip(plist, offs):
                for ci in range(T // FC):
                    bank, bk_b = env.bank()

                    def mmp(e, slab=slab, m0=m0, ci=ci, bank=bank):
                        last = None
                        for k in range(DT):
                            last = e.matmul(bank[0:64, 0:FC], wt_tile(slab, k, m0, m0 + 64),
                                            xb[:, k * T + ci * FC: k * T + (ci + 1) * FC],
                                            start=(k == 0), stop=(k == DT - 1))
                        return last
                    P.op("tensor", mmp, reads=[slab[1], xb_b], writes=[bk_b], tag="hproj")
                    P.op("scalar", lambda e, dst=dst, ci=ci, bank=bank: e.activation(
                        dst[0:64, ci * FC:(ci + 1) * FC], bank[0:64, 0:FC], AF.Copy),
                        reads=[bk_b], writes=[dst_b], tag="hevac")
            if alibi:
                P.op("gpsimd", lambda e, h=h, kslab=kslab: e.dma_start(
                    out=kslab[64:68, :], in_=drams["alibi_k"][h]),
                    writes=[kslab_b], dma=True, tag="aug_k")
                P.op("gpsimd", lambda e, h=h, qslab=qslab: e.dma_start(
                    out=qslab[64:68, :], in_=drams["alibi_q"][h]),
                    writes=[qslab_b], dma=True, tag="aug_q")
            P.op("vector", lambda e, vT=vT: e.memset(vT[:, :], 1.0), writes=[vT_b], tag="vt_on")
            for tt in range(NT):
                bank, bk_b = env.bank_bf()
                P.op("tensor", lambda e, tt=tt, bank=bank, vh=vh: e.transpose(
                    bank[0:128, 0:64], vh[0:64, tt * 128:(tt + 1) * 128], identb[0:64, 0:64]),
                    reads=[vh_b, identb_b], writes=[bk_b], tag="vtr")
                P.op("scalar", lambda e, tt=tt, bank=bank, vT=vT: e.activation(
                    vT[:, tt * 65: tt * 65 + 64], bank[0:128, 0:64], AF.Copy),
                    reads=[bk_b], writes=[vT_b], tag="vtev")
            for qc in range(NT):
                bank_av, bav_b = env.bank()
                nkt = qc + 1
                pend = None
                for k0 in range(0, nkt, 4):
                    kn = min(4, nkt - k0)
                    bank_s, bs_b = env.bank()

                    def mms(e, k0=k0, kn=kn, qc=qc, bank_s=bank_s, kslab=kslab, qslab=qslab):
                        last = None
                        for j in range(kn):
                            last = e.matmul(bank_s[:, j * 128:(j + 1) * 128],
                                            kslab[0:NAUG, (k0 + j) * 128:(k0 + j + 1) * 128],
                                            qslab[0:NAUG, qc * 128:(qc + 1) * 128],
                                            start=True, stop=True)
                        return last
                    P.op("tensor", mms, reads=[kslab_b, qslab_b], writes=[bs_b], tag="smm")
                    diag_j = qc - k0 if (k0 <= qc < k0 + kn) else None
                    if alibi and diag_j is not None:
                        P.op("vector", lambda e, diag_j=diag_j, bank_s=bank_s: e.tensor_add(
                            bank_s[:, diag_j * 128:(diag_j + 1) * 128],
                            bank_s[:, diag_j * 128:(diag_j + 1) * 128], trineg[:, :]),
                            reads=[bs_b, trineg_b], writes=[bs_b], tag="spremask")
                    attnT, attnT_b = atl[(k0 // 4) % 2]
                    P.op("scalar", lambda e, kn=kn, bank_s=bank_s, attnT=attnT: e.activation(
                        attnT[:, 0:kn * 128], bank_s[:, 0:kn * 128], AF.Exp, scale=0.125),
                        reads=[bs_b], writes=[attnT_b], tag="sexp")
                    if (not alibi) and diag_j is not None:
                        P.op("vector", lambda e, diag_j=diag_j, attnT=attnT: e.tensor_mul(
                            attnT[:, diag_j * 128:(diag_j + 1) * 128],
                            attnT[:, diag_j * 128:(diag_j + 1) * 128], triu[:, :]),
                            reads=[attnT_b, triu_b], writes=[attnT_b], tag="smask")

                    if pend is not None:
                        pk0, pkn, pat, pat_b = pend

                        def mmavp(e, k0=pk0, kn=pkn, qc=qc, bank_av=bank_av, attnT=pat, vT=vT):
                            last = None
                            for j in range(kn):
                                last = e.matmul(bank_av[:, 0:65], attnT[:, j * 128:(j + 1) * 128],
                                                vT[:, (k0 + j) * 65:(k0 + j + 1) * 65],
                                                start=(k0 + j == 0), stop=(k0 + j == qc))
                            return last
                        P.op("tensor", mmavp, reads=[pat_b, vT_b], writes=[bav_b], tag="avmm")
                    pend = (k0, kn, attnT, attnT_b)
                pk0, pkn, pat, pat_b = pend

                def mmavl(e, k0=pk0, kn=pkn, qc=qc, bank_av=bank_av, attnT=pat, vT=vT):
                    last = None
                    for j in range(kn):
                        last = e.matmul(bank_av[:, 0:65], attnT[:, j * 128:(j + 1) * 128],
                                        vT[:, (k0 + j) * 65:(k0 + j + 1) * 65],
                                        start=(k0 + j == 0), stop=(k0 + j == qc))
                    return last
                P.op("tensor", mmavl, reads=[pat_b, vT_b], writes=[bav_b], tag="avmm")
                P.op("vector", lambda e, bank_av=bank_av: e.reciprocal(recip[:, 0:1], bank_av[:, 64:65]),
                     reads=[bav_b], writes=[recip_b], tag="recip")
                P.op("scalar", lambda e, bank_av=bank_av: e.activation(
                    att_tok[:, 0:64], bank_av[:, 0:64], AF.Copy, scale=recip[:, 0:1]),
                    reads=[bav_b, recip_b], writes=[att_tok_b], tag="avnorm")
                bank_t, bt_b = env.bank()
                P.op("tensor", lambda e, bank_t=bank_t: e.transpose(
                    bank_t[0:64, 0:128], att_tok[:, 0:64], identf[:, :]),
                    reads=[att_tok_b, identf_b], writes=[bt_b], tag="otr")
                dt0 = h // 2
                pbase = (h % 2) * 64
                P.op("scalar", lambda e, dt0=dt0, pbase=pbase, qc=qc, bank_t=bank_t: e.activation(
                    aall[pbase:pbase + 64, dt0 * T + qc * 128: dt0 * T + (qc + 1) * 128],
                    bank_t[0:64, 0:128], AF.Copy),
                    reads=[bt_b], writes=[aall_b], tag="oev")
        for ci in range(T // FC):
            f0 = toff + ci * FC
            P.op("sync", lambda e, f0=f0: e.dma_start(out=xc[:, :], in_=x_chunk_src(x_in, f0, FC)),
                 reads=[x_in_b], writes=[xc_b], dma=True, tag="xload2")
            for mt in range(DT):
                bank, bk_b = env.bank()

                def mmo(e, mt=mt, ci=ci, bank=bank):
                    last = None
                    for k in range(DT):
                        last = e.matmul(bank[:, 0:FC], wt_tile(wo, k, mt * 128, (mt + 1) * 128),
                                        aall[:, k * T + ci * FC: k * T + (ci + 1) * FC],
                                        start=(k == 0), stop=(k == DT - 1))
                    return last
                P.op("tensor", mmo, reads=[wo[1], aall_b], writes=[bk_b], tag="womm")
                if alibi:
                    P.op("vector", lambda e, mt=mt, bank=bank: e.tensor_add(
                        xc[:, mt * FC:(mt + 1) * FC], xc[:, mt * FC:(mt + 1) * FC], bank[:, 0:FC]),
                        reads=[bk_b, xc_b], writes=[xc_b], tag="rad")
                else:
                    P.op("vector", lambda e, mt=mt, bank=bank: e.scalar_tensor_tensor(
                        xc[:, mt * FC:(mt + 1) * FC], bank[:, 0:FC], bout[:, mt:mt + 1],
                        xc[:, mt * FC:(mt + 1) * FC], ALU.add, ALU.add),
                        reads=[bk_b, bout_b, xc_b], writes=[xc_b], tag="radb")
            emit_ln_chunk(env, xc, xc_b, DT, FC)
            P.op("sync", lambda e, f0=f0: e.dma_start(out=x_chunk_src(x_out, f0, FC), in_=xc[:, :]),
                 reads=[xc_b], writes=[x_out_b], dma=True, tag="xstore")


def emit_l0_attn(env, x_in, x_out, drams):
    _emit_mha(env, x_in, x_out, drams, alibi=True)


def emit_l1_attn(env, x_in, x_out, drams):
    _emit_mha(env, x_in, x_out, drams, alibi=False)


def emit_l1_ffn(env, x_in, x_out, drams):
    P = env.P
    E, I = 8, 2816
    NKI = I // 128
    GRP = 1024
    NG = TOK // GRP
    key = f"l1f{env.uid()}"

    rw, rw_b = env.sb(key + "rw", [128, DT * E], F32)
    P.op("sync", lambda e: e.dma_start(out=rw[:, :], in_=drams["l1f_routert"].rearrange("(a p) c -> p a c", p=128)),
         writes=[rw_b], dma=True, tag="rw")
    identf, identf_b = env.sb(key + "idf", [128, 128], F32)
    P.op("sync", lambda e: e.dma_start(out=identf[:, :], in_=drams["ident_f32"]),
         writes=[identf_b], dma=True, tag="idf")

    xc, xc_b = env.sb(key + "xc", [128, DT * FC], F32)
    xq, xq_b = env.sb(key + "xq", [128, DT * GRP], BF16)
    lg, lg_b = env.sb(key + "lg", [8, FC], F32)
    ptok, ptok_b = env.sb(key + "pt", [128, E], F32)
    ssc, ssc_b = env.sb(key + "ss", [128, 6], F32)
    eqm, eqm_b = env.sb(key + "eq", [128, E], F32)
    wte, wte_b = env.sb(key + "wt", [128, E], F32)
    wrep, wrep_b = env.sb(key + "wr", [128, GRP], F32)
    sg = [env.sb(key + "sg", [128, FC], F32) for _ in range(2)]
    hh = [env.sb(key + "hh", [128, FC], F32) for _ in range(2)]
    hbuf, hbuf_b = env.sb(key + "hb", [128, NKI * GRP], BF16)
    yacc, yacc_b = env.sb(key + "ya", [128, DT * GRP], F32)
    wall, wall_b = yacc, yacc_b
    wsl = [env.sb(key + "ws", [128, DT * 512], BF16) for _ in range(2)]
    dsl, dsl_b = env.sb(key + "ds", [128, NKI * 1024], BF16)
    wdram = env.nc.dram_tensor(key + "wdram", [8, TOK], F32, kind="Internal").ap()
    wdram_b = P.buf(key + "wdram")
    x_in_b = drams["_xin_buf"]
    x_out_b = drams["_xout_buf"]

    for ci in range(NFC):
        f0 = ci * FC
        P.op("sync", lambda e, f0=f0: e.dma_start(out=xc[:, :], in_=x_chunk_src(x_in, f0, FC)),
             reads=[x_in_b], writes=[xc_b], dma=True, tag="xload")
        bank, bk_b = env.bank()

        def mmr(e, bank=bank):
            last = None
            for k in range(DT):
                last = e.matmul(bank[0:8, 0:FC], rw[:, k * E:(k + 1) * E],
                                xc[:, k * FC:(k + 1) * FC], start=(k == 0), stop=(k == DT - 1))
            return last
        P.op("tensor", mmr, reads=[rw_b, xc_b], writes=[bk_b], tag="rmm")
        P.op("scalar", lambda e, bank=bank: e.activation(lg[:, :], bank[0:8, 0:FC], AF.Copy),
             reads=[bk_b], writes=[lg_b], tag="rev")
        for ts in range(FC // 128):
            bank_t, bt_b = env.bank()
            P.op("tensor", lambda e, ts=ts, bank_t=bank_t: e.transpose(
                bank_t[0:128, 0:8], lg[0:8, ts * 128:(ts + 1) * 128], identf[0:8, 0:8]),
                reads=[lg_b, identf_b], writes=[bt_b], tag="rtr")
            P.op("scalar", lambda e, bank_t=bank_t: e.activation(
                ptok[:, :], bank_t[0:128, 0:8], AF.Copy),
                reads=[bt_b], writes=[ptok_b], tag="rtev")
            P.op("vector", lambda e: e.reduce_max(ssc[:, 0:1], ptok[:, :], axis=AX, negate=True),
                 reads=[ptok_b], writes=[ssc_b], tag="rmax")
            P.op("scalar", lambda e: e.activation(ptok[:, :], ptok[:, :], AF.Exp, bias=ssc[:, 0:1]),
                 reads=[ptok_b, ssc_b], writes=[ptok_b], tag="rexp")
            P.op("vector", lambda e: e.reduce_sum(ssc[:, 1:2], ptok[:, :], axis=AX),
                 reads=[ptok_b], writes=[ssc_b], tag="rsum")
            P.op("vector", lambda e: e.reciprocal(ssc[:, 2:3], ssc[:, 1:2]),
                 reads=[ssc_b], writes=[ssc_b], tag="rrec")
            P.op("vector", lambda e: e.tensor_scalar(ptok[:, :], ptok[:, :], ssc[:, 2:3], None, ALU.mult),
                 reads=[ptok_b, ssc_b], writes=[ptok_b], tag="rnorm")
            P.op("vector", lambda e: e.reduce_max(ssc[:, 3:4], ptok[:, :], axis=AX),
                 reads=[ptok_b], writes=[ssc_b], tag="t2m1")
            P.op("vector", lambda e: e.tensor_scalar(eqm[:, :], ptok[:, :], ssc[:, 3:4], None, ALU.is_lt),
                 reads=[ptok_b, ssc_b], writes=[eqm_b], tag="t2lt")
            P.op("vector", lambda e: e.tensor_mul(eqm[:, :], eqm[:, :], ptok[:, :]),
                 reads=[eqm_b, ptok_b], writes=[eqm_b], tag="t2p2")
            P.op("vector", lambda e: e.reduce_max(ssc[:, 4:5], eqm[:, :], axis=AX),
                 reads=[eqm_b], writes=[ssc_b], tag="t2m2")
            P.op("vector", lambda e: e.tensor_scalar(wte[:, :], ptok[:, :], ssc[:, 4:5], None, ALU.is_ge),
                 reads=[ptok_b, ssc_b], writes=[wte_b], tag="t2ge")
            P.op("vector", lambda e: e.tensor_mul(wte[:, :], wte[:, :], ptok[:, :]),
                 reads=[wte_b, ptok_b], writes=[wte_b], tag="t2w")
            bank_w, bw_b = env.bank()
            P.op("tensor", lambda e, bank_w=bank_w: e.transpose(
                bank_w[0:8, 0:128], wte[:, :], identf[:, :]),
                reads=[wte_b, identf_b], writes=[bw_b], tag="wtr")
            P.op("scalar", lambda e, f0=f0, ts=ts, bank_w=bank_w: e.activation(
                wall[0:8, f0 + ts * 128: f0 + (ts + 1) * 128], bank_w[0:8, 0:128], AF.Copy),
                reads=[bw_b], writes=[wall_b], tag="wev")
    P.op("sync", lambda e: e.dma_start(out=wdram, in_=wall[0:8, 0:TOK]),
         reads=[wall_b], writes=[wdram_b], dma=True, tag="wst")

    IC = [(i * 256, 256) for i in range(11)]
    li = 0
    for g in range(NG):
        t0 = g * GRP
        P.op("vector", lambda e: e.memset(yacc[:, :], 0.0), writes=[yacc_b], tag="yz")
        P.op("gpsimd", lambda e, t0=t0: e.dma_start(
            out=xq.ap().rearrange("p (a c) -> p a c", a=DT),
            in_=x_in.rearrange("(a p) c -> p a c", p=128)[:, :, t0:t0 + GRP]),
            reads=[x_in_b], writes=[xq_b], dma=True, tag="xqcast")
        for ei in range(E):
            P.op("sync", lambda e, ei=ei, t0=t0: e.dma_start(
                out=wrep[:, :],
                in_=wdram[ei:ei + 1, t0:t0 + GRP].broadcast_to([128, GRP])),
                reads=[wdram_b], writes=[wrep_b], dma=True, tag="wrep")
            P.op("gpsimd", lambda e, ei=ei: e.dma_start(
                out=dsl.ap().rearrange("p (a c) -> p a c", a=NKI),
                in_=drams["l1f_downt"][ei].rearrange("(a p) c -> p a c", p=128)),
                writes=[dsl_b], dma=True, tag="dld")
            for ici, (i0, ilen) in enumerate(IC):
                nmt = ilen // 128
                wslab, wslab_b = wsl[li % 2]
                li += 1
                P.op("gpsimd", lambda e, ei=ei, i0=i0, ilen=ilen, wslab=wslab: e.dma_start(
                    out=wslab.ap()[:, 0:DT * ilen].rearrange("p (a c) -> p a c", a=DT),
                    in_=drams["l1f_gatet"][ei].rearrange("(a p) c -> p a c", p=128)[:, :, i0:i0 + ilen]),
                    writes=[wslab_b], dma=True, tag="gld")
                P.op("gpsimd", lambda e, ei=ei, i0=i0, ilen=ilen, wslab=wslab: e.dma_start(
                    out=wslab.ap()[:, DT * ilen:2 * DT * ilen].rearrange("p (a c) -> p a c", a=DT),
                    in_=drams["l1f_upt"][ei].rearrange("(a p) c -> p a c", p=128)[:, :, i0:i0 + ilen]),
                    writes=[wslab_b], dma=True, tag="uld")
                for tc in range(GRP // FC):
                    for mt in range(nmt):
                        bank_g, bg_b = env.bank()
                        bank_u, bu_b = env.bank()
                        sgt, sg_b = sg[mt % 2]
                        hht, hh_b = hh[mt % 2]

                        def mmg(e, i0=i0, ilen=ilen, mt=mt, tc=tc, bank_g=bank_g, wslab=wslab):
                            last = None
                            for k in range(DT):
                                last = e.matmul(bank_g[:, 0:FC],
                                                wslab[:, k * ilen + mt * 128: k * ilen + (mt + 1) * 128],
                                                xq[:, k * GRP + tc * FC: k * GRP + (tc + 1) * FC],
                                                start=(k == 0), stop=(k == DT - 1))
                            return last
                        P.op("tensor", mmg, reads=[wslab_b, xq_b], writes=[bg_b], tag="gmm")
                        P.op("scalar", lambda e, bank_g=bank_g, sgt=sgt: e.activation(
                            sgt[:, :], bank_g[:, 0:FC], AF.Silu),
                            reads=[bg_b], writes=[sg_b], tag="silu")

                        def mmu(e, i0=i0, ilen=ilen, mt=mt, tc=tc, bank_u=bank_u, wslab=wslab):
                            last = None
                            for k in range(DT):
                                last = e.matmul(bank_u[:, 0:FC],
                                                wslab[:, (DT + k) * ilen + mt * 128: (DT + k) * ilen + (mt + 1) * 128],
                                                xq[:, k * GRP + tc * FC: k * GRP + (tc + 1) * FC],
                                                start=(k == 0), stop=(k == DT - 1))
                            return last
                        P.op("tensor", mmu, reads=[wslab_b, xq_b], writes=[bu_b], tag="umm")
                        P.op("vector", lambda e, bank_u=bank_u, sgt=sgt, hht=hht: e.tensor_mul(
                            hht[:, :], sgt[:, :], bank_u[:, 0:FC]),
                            reads=[sg_b, bu_b], writes=[hh_b], tag="hmul")
                        it = (i0 // 128) + mt
                        P.op("vector", lambda e, it=it, tc=tc, hht=hht: e.tensor_mul(
                            hbuf[:, it * GRP + tc * FC: it * GRP + (tc + 1) * FC], hht[:, :],
                            wrep[:, tc * FC:(tc + 1) * FC]),
                            reads=[hh_b, wrep_b], writes=[hbuf_b], tag="hw")
            for tc in range(GRP // FC):
                for mt in range(DT):
                    bank_d, bd_b = env.bank()

                    def mmd(e, mt=mt, tc=tc, bank_d=bank_d):
                        last = None
                        for k in range(NKI):
                            last = e.matmul(bank_d[:, 0:FC],
                                            dsl[:, k * D + mt * 128:k * D + (mt + 1) * 128],
                                            hbuf[:, k * GRP + tc * FC: k * GRP + (tc + 1) * FC],
                                            start=(k == 0), stop=(k == NKI - 1))
                        return last
                    P.op("tensor", mmd, reads=[dsl_b, hbuf_b], writes=[bd_b], tag="dmm")
                    sl = slice(mt * GRP + tc * FC, mt * GRP + (tc + 1) * FC)
                    P.op("vector", lambda e, sl=sl, bank_d=bank_d: e.tensor_add(
                        yacc[:, sl], yacc[:, sl], bank_d[:, 0:FC]),
                        reads=[bd_b, yacc_b], writes=[yacc_b], tag="yadd")
        for tc in range(GRP // FC):
            f0 = t0 + tc * FC
            P.op("sync", lambda e, f0=f0: e.dma_start(out=xc[:, :], in_=x_chunk_src(x_in, f0, FC)),
                 reads=[x_in_b], writes=[xc_b], dma=True, tag="xload2")
            for dt in range(DT):
                sl = slice(dt * FC, (dt + 1) * FC)
                sla = slice(dt * GRP + tc * FC, dt * GRP + (tc + 1) * FC)
                P.op("vector", lambda e, sl=sl, sla=sla: e.tensor_add(xc[:, sl], xc[:, sl], yacc[:, sla]),
                     reads=[xc_b, yacc_b], writes=[xc_b], tag="rad")
            emit_ln_chunk(env, xc, xc_b, DT, FC)
            P.op("sync", lambda e, f0=f0: e.dma_start(out=x_chunk_src(x_out, f0, FC), in_=xc[:, :]),
                 reads=[xc_b], writes=[x_out_b], dma=True, tag="xstore")


def emit_l2_attn(env, x_in, x_out, drams):
    P = env.P
    nc = env.nc
    DI = 2048
    NDT = 16
    NST = 64
    CW = 515
    key = f"l2m{env.uid()}"

    convw, convw_b = env.sb(key + "cw", [128, NDT * 4], F32)
    convb, convb_b = env.sb(key + "cb", [128, NDT], F32)
    dtb, dtb_b = env.sb(key + "db", [128, NDT], F32)
    alog, alog_b = env.sb(key + "al", [128, NDT * NST], F32)
    P.op("sync", lambda e: e.dma_start(out=convw[:, :], in_=drams["l2a_convw"]), writes=[convw_b], dma=True, tag="cw")
    P.op("sync", lambda e: e.dma_start(out=convb[:, :], in_=drams["l2a_convb"]), writes=[convb_b], dma=True, tag="cb")
    P.op("sync", lambda e: e.dma_start(out=dtb[:, :], in_=drams["l2a_dtb"]), writes=[dtb_b], dma=True, tag="db")
    P.op("sync", lambda e: e.dma_start(out=alog[:, :], in_=drams["l2a_alog"]), writes=[alog_b], dma=True, tag="al")
    P.op("scalar", lambda e: e.activation(alog[:, :], alog[:, :], AF.Exp),
         reads=[alog_b], writes=[alog_b], tag="aexp")
    P.op("vector", lambda e: e.tensor_scalar(alog[:, :], alog[:, :], -1.0, None, ALU.mult),
         reads=[alog_b], writes=[alog_b], tag="aneg")

    wsl2 = [env.sb(key + "ws", [128, 4096], BF16) for _ in range(2)]

    delta, delta_b = env.sb(key + "de", [128, NDT * FC], F32)
    xsw, xsw_b = env.sb(key + "xw", [128, NDT * FC], F32)
    stx, stx_b = env.sb(key + "sx", [128, NDT * CW], F32)
    cvd, cvd_b = stx, stx_b
    halo, halo_b = env.sb(key + "hl", [128, NDT * 3], F32)
    yacc, yacc_b = env.sb(key + "ya", [128, NDT * FC], F32)
    xsb, xsb_b = env.sb(key + "xb", [128, NDT * FC], BF16)
    bcr2 = [env.sb(key + "bc", [128, 2 * FC], F32) for _ in range(2)]
    carry, carry_b = env.sb(key + "ca", [128, NST * NDT], F32)
    sp1, sp1_b = env.sb(key + "s1", [128, FC], F32)
    sp2, sp2_b = env.sb(key + "s2", [128, FC], F32)
    sp3, sp3_b = env.sb(key + "s3", [128, FC], F32)
    sp4, sp4_b = env.sb(key + "s4", [128, FC], F32)
    dabs = [(sp1, sp1_b), (sp3, sp3_b)]
    dxbs = [(sp2, sp2_b), (sp4, sp4_b)]
    xcb, xcb_b = xsb, xsb_b

    zd = nc.dram_tensor(key + "zd", [128, 8 * NDT * FC], F32, kind="Internal").ap()
    zd_b = P.buf(key + "zd")
    bcd = nc.dram_tensor(key + "bcd", [128, FC], F32, kind="Internal").ap()
    bcd_b = P.buf(key + "bcd")
    one_c, one_c_b = env.const_col(1.0)
    x_in_b = drams["_xin_buf"]
    x_out_b = drams["_xout_buf"]

    NTC = T // FC
    for b in range(BPC):
        P.op("vector", lambda e: e.memset(carry[:, :], 0.0), writes=[carry_b], tag="car0")
        for tc in range(NTC):
            cidx = b * NTC + tc
            f0 = b * T + tc * FC
            P.op("gpsimd", lambda e, f0=f0: e.dma_start(
                out=xcb.ap()[:, 0:DT * FC].rearrange("p (a c) -> p a c", a=DT),
                in_=x_in.rearrange("(a p) c -> p a c", p=128)[:, :, f0:f0 + FC]),
                reads=[x_in_b], writes=[xcb_b], dma=True, tag="xcast")
            if tc == 0:
                P.op("vector", lambda e: e.memset(halo[:, :], 0.0), writes=[halo_b], tag="halo0")
            for mt in range(NDT):
                P.op("vector", lambda e, mt=mt: e.tensor_copy(
                    cvd[:, mt * CW:mt * CW + 3], halo[:, mt * 3:(mt + 1) * 3]),
                    reads=[halo_b], writes=[cvd_b], tag="halocp")
            for q in range(8):
                wslab, wslab_b = wsl2[q % 2]
                P.op("gpsimd", lambda e, q=q, wslab=wslab: e.dma_start(
                    out=wslab.ap().rearrange("p (a c) -> p a c", a=DT),
                    in_=drams["l2a_inprojt"].rearrange("(a p) c -> p a c", p=128)[:, :, q * 512:(q + 1) * 512]),
                    writes=[wslab_b], dma=True, tag="ipld")
                for m in range(4):
                    mt = q * 4 + m
                    bank, bk_b = env.bank()

                    def mmi(e, m=m, bank=bank, wslab=wslab):
                        last = None
                        for k in range(DT):
                            last = e.matmul(bank[:, 0:FC], wslab[:, k * 512 + m * 128:(k * 512) + (m + 1) * 128],
                                            xcb[:, k * FC:(k + 1) * FC], start=(k == 0), stop=(k == DT - 1))
                        return last
                    P.op("tensor", mmi, reads=[wslab_b, xcb_b], writes=[bk_b], tag="ipmm")
                    if mt < NDT:
                        P.op("scalar", lambda e, mt=mt, bank=bank: e.activation(
                            cvd[:, mt * CW + 3:mt * CW + CW], bank[:, 0:FC], AF.Copy),
                            reads=[bk_b], writes=[cvd_b], tag="ipev")
                    else:
                        zt = mt - NDT
                        P.op("scalar", lambda e, zt=zt, bank=bank: e.activation(
                            delta[:, zt * FC:(zt + 1) * FC], bank[:, 0:FC], AF.Silu),
                            reads=[bk_b], writes=[delta_b], tag="zsilu")
            P.op("sync", lambda e, cidx=cidx: e.dma_start(
                out=zd[:, cidx * NDT * FC:(cidx + 1) * NDT * FC], in_=delta[:, :]),
                reads=[delta_b], writes=[zd_b], dma=True, tag="zspill")
            for mt in range(NDT):
                sl = slice(mt * FC, (mt + 1) * FC)
                P.op("vector", lambda e, mt=mt, sl=sl: e.tensor_scalar(
                    xsw[:, sl], cvd[:, mt * CW + 0:mt * CW + FC], convw[:, mt * 4:mt * 4 + 1], None, ALU.mult),
                    reads=[cvd_b, convw_b], writes=[xsw_b], tag="cv0")
                for j in range(1, 4):
                    P.op("vector", lambda e, mt=mt, sl=sl, j=j: e.scalar_tensor_tensor(
                        xsw[:, sl], cvd[:, mt * CW + j:mt * CW + j + FC], convw[:, mt * 4 + j:mt * 4 + j + 1],
                        xsw[:, sl], ALU.mult, ALU.add),
                        reads=[cvd_b, convw_b, xsw_b], writes=[xsw_b], tag="cvj")
                P.op("scalar", lambda e, mt=mt, sl=sl: e.activation(
                    xsw[:, sl], xsw[:, sl], AF.Silu, bias=convb[:, mt:mt + 1]),
                    reads=[xsw_b, convb_b], writes=[xsw_b], tag="cvsilu")
            for mt in range(NDT):
                P.op("vector", lambda e, mt=mt: e.tensor_copy(
                    halo[:, mt * 3:(mt + 1) * 3], cvd[:, mt * CW + FC:mt * CW + FC + 3]),
                    reads=[cvd_b], writes=[halo_b], tag="halosv")
            P.op("gpsimd", lambda e: e.dma_start(out=xsb[:, :], in_=xsw[:, :]),
                 reads=[xsw_b], writes=[xsb_b], dma=True, tag="xscast")
            wslab, wslab_b = wsl2[0]
            P.op("gpsimd", lambda e, wslab=wslab: e.dma_start(
                out=wslab.ap()[:, 0:NDT * 128].rearrange("p (a c) -> p a c", a=NDT),
                in_=drams["l2a_xprojt"].rearrange("(a p) c -> p a c", p=128)),
                writes=[wslab_b], dma=True, tag="xpld")
            bank, bk_b = env.bank()

            def mmx(e, bank=bank, wslab=wslab):
                last = None
                for k in range(NDT):
                    last = e.matmul(bank[:, 0:FC], wslab[:, k * 128:(k + 1) * 128],
                                    xsb[:, k * FC:(k + 1) * FC], start=(k == 0), stop=(k == NDT - 1))
                return last
            P.op("tensor", mmx, reads=[wslab_b, xsb_b], writes=[bk_b], tag="xpmm")
            P.op("scalar", lambda e, bank=bank: e.activation(yacc[:, 0:FC], bank[:, 0:FC], AF.Copy),
                 reads=[bk_b], writes=[yacc_b], tag="xpev")
            P.op("sync", lambda e: e.dma_start(out=bcd, in_=yacc[:, 0:FC]),
                 reads=[yacc_b], writes=[bcd_b], dma=True, tag="bcspill")
            for q8 in range(8):
                wslab, wslab_b = wsl2[(q8 + 1) % 2]
                P.op("gpsimd", lambda e, q8=q8, wslab=wslab: e.dma_start(
                    out=wslab.ap().rearrange("p (a c) -> p a c", a=NDT),
                    in_=drams["l2a_dtprojt"].rearrange("(a p) c -> p a c", p=128)[:, :, q8 * 256:(q8 + 1) * 256]),
                    writes=[wslab_b], dma=True, tag="dtld")
                for m in range(2):
                    mt = q8 * 2 + m
                    bank, bk_b = env.bank()

                    def mmd(e, m=m, bank=bank, wslab=wslab):
                        last = None
                        for k in range(NDT):
                            last = e.matmul(bank[:, 0:FC], wslab[:, k * 256 + m * 128:k * 256 + (m + 1) * 128],
                                            xsb[:, k * FC:(k + 1) * FC], start=(k == 0), stop=(k == NDT - 1))
                        return last
                    P.op("tensor", mmd, reads=[wslab_b, xsb_b], writes=[bk_b], tag="dtmm")
                    P.op("scalar", lambda e, mt=mt, bank=bank: e.activation(
                        sp1[:, :], bank[:, 0:FC], AF.Abs, bias=dtb[:, mt:mt + 1]),
                        reads=[bk_b, dtb_b], writes=[sp1_b], tag="spabs")
                    P.op("scalar", lambda e: e.activation(sp1[:, :], sp1[:, :], AF.Exp, scale=-1.0),
                         reads=[sp1_b], writes=[sp1_b], tag="spexp")
                    P.op("scalar", lambda e: e.activation(sp1[:, :], sp1[:, :], AF.Ln, bias=one_c[:, 0:1]),
                         reads=[sp1_b, one_c_b], writes=[sp1_b], tag="spln")
                    P.op("scalar", lambda e, mt=mt, bank=bank: e.activation(
                        sp2[:, :], bank[:, 0:FC], AF.Relu, bias=dtb[:, mt:mt + 1]),
                        reads=[bk_b, dtb_b], writes=[sp2_b], tag="sprelu")
                    P.op("vector", lambda e, mt=mt: e.tensor_add(
                        delta[:, mt * FC:(mt + 1) * FC], sp2[:, :], sp1[:, :]),
                        reads=[sp1_b, sp2_b], writes=[delta_b], tag="spadd")
            for mt in range(NDT):
                sl = slice(mt * FC, (mt + 1) * FC)
                P.op("vector", lambda e, sl=sl: e.tensor_mul(xsw[:, sl], delta[:, sl], xsw[:, sl]),
                     reads=[delta_b, xsw_b], writes=[xsw_b], tag="wmul")
            P.op("vector", lambda e: e.memset(yacc[:, :], 0.0), writes=[yacc_b], tag="yz")
            for n in range(NST):
                bcrep, bcrep_b = bcr2[n % 2]
                P.op("sync", lambda e, n=n, bcrep=bcrep: e.dma_start(
                    out=bcrep.ap().rearrange("p (a c) -> p a c", a=2),
                    in_=bcd.rearrange("(j r) c -> j r c", j=2)[:, n:n + 1, :].transpose([1, 0, 2])
                        .rearrange("o j c -> (o j) c")[None].broadcast_to([128, 2, FC])),
                    reads=[bcd_b], writes=[bcrep_b], dma=True, tag="bcrep")
                if True:
                    for dt in range(NDT):
                        dab, dab_b = dabs[dt % 2]
                        dxb, dxb_b = dxbs[dt % 2]
                        P.op("scalar", lambda e, dt=dt, n=n, dab=dab: e.activation(
                            dab[:, :], delta[:, dt * FC:(dt + 1) * FC], AF.Exp,
                            scale=alog[:, dt * NST + n:dt * NST + n + 1]),
                            reads=[delta_b, alog_b], writes=[dab_b], tag="dA")
                        P.op("vector", lambda e, dt=dt, dxb=dxb, bcrep=bcrep: e.tensor_mul(
                            dxb[:, :], xsw[:, dt * FC:(dt + 1) * FC], bcrep[:, 0:FC]),
                            reads=[xsw_b, bcrep_b], writes=[dxb_b], tag="dBx")
                        P.op("vector", lambda e, dt=dt, n=n, dab=dab, dxb=dxb: e.tensor_tensor_scan(
                            stx[:, dt * FC:(dt + 1) * FC], dab[:, :], dxb[:, :],
                            carry[:, n * NDT + dt:n * NDT + dt + 1], ALU.mult, ALU.add),
                            reads=[dab_b, dxb_b, carry_b], writes=[stx_b], tag="scan")
                    P.op("vector", lambda e, n=n: e.tensor_copy(
                        carry[:, n * NDT:(n + 1) * NDT],
                        stx.ap()[:, 0:NDT * FC].rearrange("p (d t) -> p d t", d=NDT)[:, :, FC - 1:FC]),
                        reads=[stx_b], writes=[carry_b], tag="carry")
                    P.op("vector", lambda e, bcrep=bcrep: e.tensor_mul(
                        stx.ap()[:, 0:NDT * FC].rearrange("p (d t) -> p d t", d=NDT),
                        stx.ap()[:, 0:NDT * FC].rearrange("p (d t) -> p d t", d=NDT),
                        bcrep[:, FC:2 * FC].rearrange(
                            "p (o t) -> p o t", o=1).broadcast_to([128, NDT, FC])),
                        reads=[stx_b, bcrep_b], writes=[stx_b], tag="zC")
                    P.op("vector", lambda e: e.tensor_add(yacc[:, :], yacc[:, :], stx[:, 0:NDT * FC]),
                         reads=[yacc_b, stx_b], writes=[yacc_b], tag="yadd")
            P.op("sync", lambda e, cidx=cidx: e.dma_start(
                out=stx[:, 0:NDT * FC], in_=zd[:, cidx * NDT * FC:(cidx + 1) * NDT * FC]),
                reads=[zd_b], writes=[stx_b], dma=True, tag="zload")
            for mt in range(NDT):
                sl = slice(mt * FC, (mt + 1) * FC)
                P.op("vector", lambda e, sl=sl: e.tensor_mul(xsw[:, sl], yacc[:, sl], stx[:, sl]),
                     reads=[yacc_b, stx_b], writes=[xsw_b], tag="y2")
            P.op("gpsimd", lambda e: e.dma_start(out=xsb[:, :], in_=xsw[:, :]),
                 reads=[xsw_b], writes=[xsb_b], dma=True, tag="y2cast")
            P.op("sync", lambda e, f0=f0: e.dma_start(
                out=stx.ap()[:, 0:DT * FC].rearrange("p (a c) -> p a c", a=DT),
                in_=x_in.rearrange("(a p) c -> p a c", p=128)[:, :, f0:f0 + FC]),
                reads=[x_in_b], writes=[stx_b], dma=True, tag="xload2")
            for q4 in range(4):
                wslab, wslab_b = wsl2[q4 % 2]
                P.op("gpsimd", lambda e, q4=q4, wslab=wslab: e.dma_start(
                    out=wslab.ap()[:, 0:NDT * 256].rearrange("p (a c) -> p a c", a=NDT),
                    in_=drams["l2a_outprojt"].rearrange("(a p) c -> p a c", p=128)[:, :, q4 * 256:(q4 + 1) * 256]),
                    writes=[wslab_b], dma=True, tag="opld")
                for m in range(2):
                    mt = q4 * 2 + m
                    bank, bk_b = env.bank()

                    def mmo(e, m=m, bank=bank, wslab=wslab):
                        last = None
                        for k in range(NDT):
                            last = e.matmul(bank[:, 0:FC], wslab[:, k * 256 + m * 128:k * 256 + (m + 1) * 128],
                                            xsb[:, k * FC:(k + 1) * FC], start=(k == 0), stop=(k == NDT - 1))
                        return last
                    P.op("tensor", mmo, reads=[wslab_b, xsb_b], writes=[bk_b], tag="opmm")
                    P.op("vector", lambda e, mt=mt, bank=bank: e.tensor_add(
                        stx[:, mt * FC:(mt + 1) * FC], stx[:, mt * FC:(mt + 1) * FC], bank[:, 0:FC]),
                        reads=[bk_b, stx_b], writes=[stx_b], tag="rad")
            emit_ln_chunk(env, stx, stx_b, DT, FC)
            P.op("sync", lambda e, f0=f0: e.dma_start(
                out=x_chunk_src(x_out, f0, FC), in_=stx.ap()[:, 0:DT * FC].rearrange("p (a c) -> p a c", a=DT)),
                reads=[stx_b], writes=[x_out_b], dma=True, tag="xstore")


MAGIC = 12582912.0
QMAXI = 32767.0


def _emit_amax_update(env, acc, acc_b, src_ap, src_bufs, tag):
    P = env.P
    P.op("vector", lambda e: e.reduce_max(acc[:, 1:2], src_ap, axis=AX, apply_absolute_value=True),
         reads=src_bufs + [acc_b], writes=[acc_b], tag=tag)
    P.op("vector", lambda e: e.tensor_max(acc[:, 0:1], acc[:, 0:1], acc[:, 1:2]),
         reads=[acc_b], writes=[acc_b], tag=tag + "m")


def _emit_amax_final(env, acc, acc_b, identf, identf_b, out_sb, out_sb_b, col):
    P = env.P
    bank, bk_b = env.bank()
    P.op("tensor", lambda e, bank=bank: e.transpose(bank[0:1, 0:128], acc[:, 0:1], identf[:, :]),
         reads=[acc_b, identf_b], writes=[bk_b], tag="amtr")
    P.op("scalar", lambda e, bank=bank: e.activation(acc[0:1, 4:4 + 128], bank[0:1, 0:128], AF.Copy),
         reads=[bk_b], writes=[acc_b], tag="amev")
    P.op("vector", lambda e, col=col: e.reduce_max(out_sb[0:1, col:col + 1], acc[0:1, 4:4 + 128], axis=AX),
         reads=[acc_b], writes=[out_sb_b], tag="amred")


def emit_l2_ffn_phase(env, x_in, x_out, drams, phase):
    P = env.P
    nc = env.nc
    II = 3072
    NIT = II // 128
    key = f"l2f{env.uid()}p{phase}"

    identf, identf_b = env.sb(key + "idf", [128, 128], F32)
    P.op("sync", lambda e: e.dma_start(out=identf[:, :], in_=drams["ident_f32"]),
         writes=[identf_b], dma=True, tag="idf")
    identb, identb_b = env.sb(key + "idb", [128, 128], BF16)
    P.op("gpsimd", lambda e: e.dma_start(out=identb[:, :], in_=drams["ident_f32"]),
         writes=[identb_b], dma=True, tag="idb")

    acc, acc_b = env.sb(key + "ac", [128, 134], F32)
    aout, aout_b = env.sb(key + "ao", [1, 2], F32)
    P.op("vector", lambda e: e.memset(acc[:, :], 0.0), writes=[acc_b], tag="acc0")
    P.op("vector", lambda e: e.memset(aout[:, :], 0.0), writes=[aout_b], tag="ao0")
    x_in_b = drams["_xin_buf"]

    gsc, gsc_b = env.sb(key + "gs", [1, 16], F32)
    gin, gin_b = env.sb(key + "gi", [1, 16], F32)
    scd = nc.dram_tensor(key + "scd", [1, 16], F32, kind="Internal").ap()
    scd_b = P.buf(key + "scd")
    scc, scc_b = env.sb(key + "sc", [128, 16], F32)

    def global_scale(src_key, rows, out_cols):
        P.op("sync", lambda e, rows=rows: e.dma_start(out=gin[0:1, 0:8 * rows], in_=drams[src_key]),
             writes=[gin_b], dma=True, tag="gld")
        for r, (c_s, c_i) in enumerate(out_cols):
            P.op("vector", lambda e, r=r, c_s=c_s: e.reduce_max(
                gsc[0:1, c_s:c_s + 1], gin[0:1, r * 8:(r + 1) * 8], axis=AX),
                reads=[gin_b], writes=[gsc_b], tag="gmax")
            P.op("vector", lambda e, c_s=c_s: e.tensor_scalar(
                gsc[0:1, c_s:c_s + 1], gsc[0:1, c_s:c_s + 1], 1.0 / QMAXI, 1e-12, ALU.mult, ALU.max),
                reads=[gsc_b], writes=[gsc_b], tag="gscale")
            P.op("vector", lambda e, c_s=c_s, c_i=c_i: e.reciprocal(gsc[0:1, c_i:c_i + 1], gsc[0:1, c_s:c_s + 1]),
                 reads=[gsc_b], writes=[gsc_b], tag="ginv")

    if phase >= 2:
        global_scale("l2f_am0g", 1, [(0, 1)])
    if phase >= 3:
        global_scale("l2f_am1g", 2, [(2, 3), (4, 5)])
    if phase >= 4:
        global_scale("l2f_am2g", 1, [(6, 7)])
    if phase == 5:
        global_scale("l2f_am3g", 1, [(8, 9)])
    if phase >= 2:
        P.op("sync", lambda e: e.dma_start(out=scd, in_=gsc[0:1, :]),
             reads=[gsc_b], writes=[scd_b], dma=True, tag="scspill")
        P.op("sync", lambda e: e.dma_start(out=scc[:, :], in_=scd[0:1, :].broadcast_to([128, 16])),
             reads=[scd_b], writes=[scc_b], dma=True, tag="screp")

    xc, xc_b = env.sb(key + "xc", [128, DT * FC], F32)
    if phase == 1:
        for f0 in range(0, TOK, FC):
            P.op("sync", lambda e, f0=f0: e.dma_start(out=xc[:, :], in_=x_chunk_src(x_in, f0, FC)),
                 reads=[x_in_b], writes=[xc_b], dma=True, tag="xload")
            _emit_amax_update(env, acc, acc_b, xc[:, :], [xc_b], "amx")
        _emit_amax_final(env, acc, acc_b, identf, identf_b, aout, aout_b, 0)
        P.op("sync", lambda e: e.dma_start(out=drams["_amout"], in_=aout[:, :]),
             reads=[aout_b], writes=[drams["_amout_buf"]], dma=True, tag="amst")
        return

    wslab, wslab_b = env.sb(key + "ws", [128, 2 * DT * II], BF16)
    wt, wt_b = env.sb(key + "wt", [128, 1024], F32)
    wtb, wtb_b = env.sb(key + "wb", [128, 1024], BF16)
    bmax, bmax_b = env.sb(key + "bm", [128, 96], F32)
    need_up = phase in (2, 4, 5)
    need_gate = True

    def dequant_to_slab(wkey, slab_off, tagp):
        for ot in range(NIT):
            P.op("sync", lambda e, ot=ot: e.dma_start(out=wt[:, :], in_=drams[wkey][ot * 128:(ot + 1) * 128, :]),
                 reads=[], writes=[wt_b], dma=True, tag=tagp + "ld")
            P.op("vector", lambda e: e.reduce_max(
                bmax[:, 0:32], wt.ap().rearrange("p (b k) -> p b k", k=32), axis=AX,
                apply_absolute_value=True),
                reads=[wt_b], writes=[bmax_b], tag=tagp + "bm")
            P.op("vector", lambda e: e.tensor_scalar(
                bmax[:, 32:64], bmax[:, 0:32], 1.0 / 7.0, 1e-12, ALU.mult, ALU.max),
                reads=[bmax_b], writes=[bmax_b], tag=tagp + "s")
            P.op("vector", lambda e: e.reciprocal(bmax[:, 64:96], bmax[:, 32:64]),
                 reads=[bmax_b], writes=[bmax_b], tag=tagp + "rs")
            P.op("vector", lambda e: e.tensor_mul(
                wt.ap().rearrange("p (b k) -> p b k", k=32),
                wt.ap().rearrange("p (b k) -> p b k", k=32),
                bmax[:, 64:96].rearrange("p (b o) -> p b o", o=1).broadcast_to([128, 32, 32])),
                reads=[wt_b, bmax_b], writes=[wt_b], tag=tagp + "div")
            P.op("vector", lambda e: e.tensor_scalar(wt[:, :], wt[:, :], MAGIC, MAGIC, ALU.add, ALU.subtract),
                 reads=[wt_b], writes=[wt_b], tag=tagp + "rnd")
            P.op("vector", lambda e: e.tensor_mul(
                wt.ap().rearrange("p (b k) -> p b k", k=32),
                wt.ap().rearrange("p (b k) -> p b k", k=32),
                bmax[:, 32:64].rearrange("p (b o) -> p b o", o=1).broadcast_to([128, 32, 32])),
                reads=[wt_b, bmax_b], writes=[wt_b], tag=tagp + "mul")
            P.op("gpsimd", lambda e: e.dma_start(out=wtb[:, :], in_=wt[:, :]),
                 reads=[wt_b], writes=[wtb_b], dma=True, tag=tagp + "cast")
            for k in range(DT):
                bank, bk_b = env.bank_bf()
                P.op("tensor", lambda e, k=k, bank=bank: e.transpose(
                    bank[0:128, 0:128], wtb[:, k * 128:(k + 1) * 128], identb[:, :]),
                    reads=[wtb_b, identb_b], writes=[bk_b], tag=tagp + "tr")
                P.op("scalar", lambda e, k=k, ot=ot, bank=bank: e.activation(
                    wslab[:, slab_off + k * II + ot * 128: slab_off + k * II + (ot + 1) * 128],
                    bank[0:128, 0:128], AF.Copy),
                    reads=[bk_b], writes=[wslab_b], tag=tagp + "ev")

    if need_up:
        dequant_to_slab("l2f_wup", 0, "du")
    dequant_to_slab("l2f_wgate", DT * II, "dg")

    xqb, xqb_b = env.sb(key + "xq", [128, DT * FC], BF16)
    upq, upq_b = env.sb(key + "uq", [128, FC], F32)
    gq, gq_b = env.sb(key + "gq", [128, FC], F32)
    sig, sig_b = env.sb(key + "sg", [128, FC], F32)
    oqb, oqb_b = env.sb(key + "oq", [128, NIT * FC], BF16)
    od = nc.dram_tensor(key + "od", [128, NFC * NIT * FC], BF16, kind="Internal").ap() if phase == 5 else None
    od_b = P.buf(key + "od") if phase == 5 else None

    for ci in range(NFC):
        f0 = ci * FC
        P.op("sync", lambda e, f0=f0: e.dma_start(out=xc[:, :], in_=x_chunk_src(x_in, f0, FC)),
             reads=[x_in_b], writes=[xc_b], dma=True, tag="xload")
        for dt in range(DT):
            sl = slice(dt * FC, (dt + 1) * FC)
            P.op("vector", lambda e, sl=sl: e.tensor_scalar(
                xc[:, sl], xc[:, sl], scc[:, 1:2], MAGIC, ALU.mult, ALU.add),
                reads=[xc_b, scc_b], writes=[xc_b], tag="xq1")
            P.op("vector", lambda e, sl=sl: e.tensor_scalar(
                xqb[:, sl], xc[:, sl], MAGIC, scc[:, 0:1], ALU.subtract, ALU.mult),
                reads=[xc_b, scc_b], writes=[xqb_b], tag="xq2")
        for mt in range(NIT):
            bank_u, bu_b = env.bank()
            bank_g, bg_b = env.bank()
            if need_up:
                def mmu(e, mt=mt, bank_u=bank_u):
                    last = None
                    for k in range(DT):
                        last = e.matmul(bank_u[:, 0:FC], wslab[:, k * II + mt * 128:k * II + (mt + 1) * 128],
                                        xqb[:, k * FC:(k + 1) * FC], start=(k == 0), stop=(k == DT - 1))
                    return last
                P.op("tensor", mmu, reads=[wslab_b, xqb_b], writes=[bu_b], tag="upmm")

            def mmg(e, mt=mt, bank_g=bank_g):
                last = None
                for k in range(DT):
                    last = e.matmul(bank_g[:, 0:FC],
                                    wslab[:, DT * II + k * II + mt * 128:DT * II + k * II + (mt + 1) * 128],
                                    xqb[:, k * FC:(k + 1) * FC], start=(k == 0), stop=(k == DT - 1))
                return last
            P.op("tensor", mmg, reads=[wslab_b, xqb_b], writes=[bg_b], tag="gmm")
            if phase == 2:
                _emit_amax_update(env, acc, acc_b, bank_u[:, 0:FC], [bu_b], "amu")
                P.op("vector", lambda e, bank_g=bank_g: e.reduce_max(
                    acc[:, 3:4], bank_g[:, 0:FC], axis=AX, apply_absolute_value=True),
                    reads=[bg_b, acc_b], writes=[acc_b], tag="amg")
                P.op("vector", lambda e: e.tensor_max(acc[:, 2:3], acc[:, 2:3], acc[:, 3:4]),
                     reads=[acc_b], writes=[acc_b], tag="amgm")
                continue
            P.op("vector", lambda e, bank_g=bank_g: e.tensor_scalar(
                gq[:, :], bank_g[:, 0:FC], scc[:, 5:6], MAGIC, ALU.mult, ALU.add),
                reads=[bg_b, scc_b], writes=[gq_b], tag="gq1")
            P.op("vector", lambda e: e.tensor_scalar(
                gq[:, :], gq[:, :], MAGIC, scc[:, 4:5], ALU.subtract, ALU.mult),
                reads=[gq_b, scc_b], writes=[gq_b], tag="gq2")
            P.op("scalar", lambda e: e.activation(sig[:, :], gq[:, :], AF.Sigmoid),
                 reads=[gq_b], writes=[sig_b], tag="sig")
            P.op("vector", lambda e: e.tensor_scalar(
                sig[:, :], sig[:, :], 65536.0, MAGIC, ALU.mult, ALU.add),
                reads=[sig_b], writes=[sig_b], tag="sq1")
            P.op("vector", lambda e: e.tensor_scalar(
                sig[:, :], sig[:, :], MAGIC, 65535.0, ALU.subtract, ALU.min),
                reads=[sig_b], writes=[sig_b], tag="sq2")
            P.op("vector", lambda e: e.scalar_tensor_tensor(
                gq[:, :], sig[:, :], 1.0 / 65536.0, gq[:, :], ALU.mult, ALU.mult),
                reads=[sig_b, gq_b], writes=[gq_b], tag="g2pre")
            if phase == 3:
                _emit_amax_update(env, acc, acc_b, gq[:, :], [gq_b], "amg2")
                continue
            P.op("vector", lambda e: e.tensor_scalar(
                gq[:, :], gq[:, :], scc[:, 7:8], MAGIC, ALU.mult, ALU.add),
                reads=[gq_b, scc_b], writes=[gq_b], tag="g2q1")
            P.op("vector", lambda e: e.tensor_scalar(
                gq[:, :], gq[:, :], MAGIC, scc[:, 6:7], ALU.subtract, ALU.mult),
                reads=[gq_b, scc_b], writes=[gq_b], tag="g2q2")
            P.op("vector", lambda e, bank_u=bank_u: e.tensor_scalar(
                upq[:, :], bank_u[:, 0:FC], scc[:, 3:4], MAGIC, ALU.mult, ALU.add),
                reads=[bu_b, scc_b], writes=[upq_b], tag="uq1")
            P.op("vector", lambda e: e.tensor_scalar(
                upq[:, :], upq[:, :], MAGIC, scc[:, 2:3], ALU.subtract, ALU.mult),
                reads=[upq_b, scc_b], writes=[upq_b], tag="uq2")
            P.op("vector", lambda e: e.tensor_mul(gq[:, :], gq[:, :], upq[:, :]),
                 reads=[gq_b, upq_b], writes=[gq_b], tag="opre")
            if phase == 4:
                _emit_amax_update(env, acc, acc_b, gq[:, :], [gq_b], "amo")
                continue
            P.op("vector", lambda e: e.tensor_scalar(
                gq[:, :], gq[:, :], scc[:, 9:10], MAGIC, ALU.mult, ALU.add),
                reads=[gq_b, scc_b], writes=[gq_b], tag="oq1")
            P.op("vector", lambda e, mt=mt: e.tensor_scalar(
                oqb[:, mt * FC:(mt + 1) * FC], gq[:, :], MAGIC, scc[:, 8:9], ALU.subtract, ALU.mult),
                reads=[gq_b, scc_b], writes=[oqb_b], tag="oq2")
        if phase == 5:
            P.op("sync", lambda e, ci=ci: e.dma_start(
                out=od[:, ci * NIT * FC:(ci + 1) * NIT * FC], in_=oqb[:, :]),
                reads=[oqb_b], writes=[od_b], dma=True, tag="ospill")

    if phase == 2:
        _emit_amax_final(env, acc, acc_b, identf, identf_b, aout, aout_b, 0)
        P.op("vector", lambda e: e.tensor_copy(acc[:, 0:1], acc[:, 2:3]),
             reads=[acc_b], writes=[acc_b], tag="swapg")
        _emit_amax_final(env, acc, acc_b, identf, identf_b, aout, aout_b, 1)
    if phase in (3, 4):
        _emit_amax_final(env, acc, acc_b, identf, identf_b, aout, aout_b, 0)
    if phase in (2, 3, 4):
        P.op("sync", lambda e: e.dma_start(out=drams["_amout"], in_=aout[:, :]),
             reads=[aout_b], writes=[drams["_amout_buf"]], dma=True, tag="amst")
        return

    def dequant_down():
        for ot in range(DT):
            for part in range(3):
                P.op("sync", lambda e, ot=ot, part=part: e.dma_start(
                    out=wt[:, :], in_=drams["l2f_wdown"][ot * 128:(ot + 1) * 128, part * 1024:(part + 1) * 1024]),
                    writes=[wt_b], dma=True, tag="ddld")
                P.op("vector", lambda e: e.reduce_max(
                    bmax[:, 0:32], wt.ap().rearrange("p (b k) -> p b k", k=32), axis=AX,
                    apply_absolute_value=True),
                    reads=[wt_b], writes=[bmax_b], tag="ddbm")
                P.op("vector", lambda e: e.tensor_scalar(
                    bmax[:, 32:64], bmax[:, 0:32], 1.0 / 7.0, 1e-12, ALU.mult, ALU.max),
                    reads=[bmax_b], writes=[bmax_b], tag="dds")
                P.op("vector", lambda e: e.reciprocal(bmax[:, 64:96], bmax[:, 32:64]),
                     reads=[bmax_b], writes=[bmax_b], tag="ddrs")
                P.op("vector", lambda e: e.tensor_mul(
                    wt.ap().rearrange("p (b k) -> p b k", k=32),
                    wt.ap().rearrange("p (b k) -> p b k", k=32),
                    bmax[:, 64:96].rearrange("p (b o) -> p b o", o=1).broadcast_to([128, 32, 32])),
                    reads=[wt_b, bmax_b], writes=[wt_b], tag="dddiv")
                P.op("vector", lambda e: e.tensor_scalar(wt[:, :], wt[:, :], MAGIC, MAGIC, ALU.add, ALU.subtract),
                     reads=[wt_b], writes=[wt_b], tag="ddrnd")
                P.op("vector", lambda e: e.tensor_mul(
                    wt.ap().rearrange("p (b k) -> p b k", k=32),
                    wt.ap().rearrange("p (b k) -> p b k", k=32),
                    bmax[:, 32:64].rearrange("p (b o) -> p b o", o=1).broadcast_to([128, 32, 32])),
                    reads=[wt_b, bmax_b], writes=[wt_b], tag="ddmul")
                P.op("gpsimd", lambda e: e.dma_start(out=wtb[:, :], in_=wt[:, :]),
                     reads=[wt_b], writes=[wtb_b], dma=True, tag="ddcast")
                for k in range(8):
                    kt = part * 8 + k
                    bank, bk_b = env.bank_bf()
                    P.op("tensor", lambda e, k=k, bank=bank: e.transpose(
                        bank[0:128, 0:128], wtb[:, k * 128:(k + 1) * 128], identb[:, :]),
                        reads=[wtb_b, identb_b], writes=[bk_b], tag="ddtr")
                    P.op("scalar", lambda e, kt=kt, ot=ot, bank=bank: e.activation(
                        wslab[:, kt * D + ot * 128: kt * D + (ot + 1) * 128],
                        bank[0:128, 0:128], AF.Copy),
                        reads=[bk_b], writes=[wslab_b], tag="ddev")

    dequant_down()
    for ci in range(NFC):
        f0 = ci * FC
        P.op("sync", lambda e, ci=ci: e.dma_start(
            out=oqb[:, :], in_=od[:, ci * NIT * FC:(ci + 1) * NIT * FC]),
            reads=[od_b], writes=[oqb_b], dma=True, tag="oload")
        P.op("sync", lambda e, f0=f0: e.dma_start(out=xc[:, :], in_=x_chunk_src(x_in, f0, FC)),
             reads=[x_in_b], writes=[xc_b], dma=True, tag="xload2")
        for mt in range(DT):
            bank, bk_b = env.bank()

            def mmd(e, mt=mt, bank=bank):
                last = None
                for k in range(NIT):
                    last = e.matmul(bank[:, 0:FC], wslab[:, k * D + mt * 128:k * D + (mt + 1) * 128],
                                    oqb[:, k * FC:(k + 1) * FC], start=(k == 0), stop=(k == NIT - 1))
                return last
            P.op("tensor", mmd, reads=[wslab_b, oqb_b], writes=[bk_b], tag="dmm")
            P.op("vector", lambda e, mt=mt, bank=bank: e.tensor_add(
                xc[:, mt * FC:(mt + 1) * FC], xc[:, mt * FC:(mt + 1) * FC], bank[:, 0:FC]),
                reads=[bk_b, xc_b], writes=[xc_b], tag="rad")
        emit_ln_chunk(env, xc, xc_b, DT, FC)
        P.op("sync", lambda e, f0=f0: e.dma_start(out=x_chunk_src(x_out, f0, FC), in_=xc[:, :]),
             reads=[xc_b], writes=[drams["_xout_buf"]], dma=True, tag="xstore")


STAGES = {"l3_ffn": emit_l3_ffn, "l0_ffn": emit_l0_ffn, "l3_attn": emit_l3_attn,
          "l0_attn": emit_l0_attn, "l1_attn": emit_l1_attn, "l1_ffn": emit_l1_ffn,
          "l2_attn": emit_l2_attn,
          "l2f_p5": lambda env, xi, xo, drams: emit_l2_ffn_phase(env, xi, xo, drams, 5)}



def build_l2f_program(phase, weights_meta):
    nc = bass.Bass("TRN2", target_bir_lowering=False, debug=False)
    es = ExitStack()
    P = Prog(nc, es)
    env = Env(nc, es, P)
    x_in = nc.dram_tensor("x_io", [D, TOK], F32, kind="ExternalInput").ap()
    drams = {}
    for name, shape in weights_meta.items():
        drams[name] = nc.dram_tensor(name, list(shape), F32, kind="ExternalInput").ap()
    drams["_xin_buf"] = P.buf("xin")
    if phase == 5:
        x_out = nc.dram_tensor("x_res", [D, TOK], F32, kind="ExternalOutput").ap()
        drams["_xout_buf"] = P.buf("xout")
        emit_l2_ffn_phase(env, x_in, x_out, drams, phase)
        last = drams["_xout_buf"].last_w
    else:
        x_out = None
        drams["_amout"] = nc.dram_tensor("amout", [1, 2], F32, kind="ExternalOutput").ap()
        drams["_amout_buf"] = P.buf("amout")
        emit_l2_ffn_phase(env, x_in, x_out, drams, phase)
        last = drams["_amout_buf"].last_w
    P.op("sync", lambda e: e.nop(), extra_waits=[last], tag="final")
    P.emit()
    es.close()
    return nc


def l2_ffn_weights(inputs):
    w = {
        "l2f_wup": np.ascontiguousarray(np.asarray(inputs["L2_ffn_w_up"], np.float32)),
        "l2f_wgate": np.ascontiguousarray(np.asarray(inputs["L2_ffn_w_gate"], np.float32)),
        "l2f_wdown": np.ascontiguousarray(np.asarray(inputs["L2_ffn_w_down"], np.float32)),
    }
    w.update(_consts())
    return w


def run_l2_ffn(x, wmap, sim=False):
    shards = _shard_x(x)
    amg = {}
    for phase in range(1, 6):
        pw = dict(wmap)
        for k, v in amg.items():
            pw[k] = v
        meta = {k: v.shape for k, v in pw.items()}
        nc = build_l2f_program(phase, meta)
        in_maps = [dict(pw, x_io=s) for s in shards]
        res = run_bass_kernel_spmd(nc, in_maps, core_ids=list(range(NCORES)))
        if phase < 5:
            outs = [r["amout"] for r in res.results]
            if phase == 1:
                amg["l2f_am0g"] = np.concatenate([o[0:1, 0:1] for o in outs], axis=1)
            elif phase == 2:
                amg["l2f_am1g"] = np.concatenate(
                    [np.concatenate([o[0:1, 0:1] for o in outs], axis=1),
                     np.concatenate([o[0:1, 1:2] for o in outs], axis=1)], axis=1)
            elif phase == 3:
                amg["l2f_am2g"] = np.concatenate([o[0:1, 0:1] for o in outs], axis=1)
            elif phase == 4:
                amg["l2f_am3g"] = np.concatenate([o[0:1, 0:1] for o in outs], axis=1)
        else:
            return _unshard_x([r["x_res"] for r in res.results])


SIDE_STAGES = {"l2f_p1": 1, "l2f_p2": 2, "l2f_p3": 3, "l2f_p4": 4}


def build_program(stage_names, weights_meta):
    nc = bass.Bass("TRN2", target_bir_lowering=False, debug=False)
    es = ExitStack()
    P = Prog(nc, es)
    env = Env(nc, es, P)

    n_x = sum(1 for s in stage_names if s not in SIDE_STAGES)
    xs = []
    for i in range(n_x + 1):
        if i == 0:
            t = nc.dram_tensor("x_io", [D, TOK], F32, kind="ExternalInput")
        elif i == n_x:
            t = nc.dram_tensor("x_res", [D, TOK], F32, kind="ExternalOutput")
        else:
            t = nc.dram_tensor(f"x_mid{i}", [D, TOK], F32, kind="Internal")
        xs.append(t.ap())

    drams = {}
    for name, shape in weights_meta.items():
        drams[name] = nc.dram_tensor(name, list(shape), F32, kind="ExternalInput").ap()
    if any(s in SIDE_STAGES for s in stage_names):
        drams["_amout"] = nc.dram_tensor("amout", [1, 2], F32, kind="ExternalOutput").ap()
        drams["_amout_buf"] = P.buf("amout")

    env.ln_scratch(FC)
    env.ln_scratch(128)
    env.const_col(LN_EPS)
    env.const_col(1.0)
    xi = 0
    last_out_token = None
    final_tokens = []
    for i, sname in enumerate(stage_names):
        drams["_xin_buf"] = P.buf(f"xin{i}")
        drams["_xout_buf"] = P.buf(f"xout{i}")
        if xi > 0:
            drams["_xin_buf"].last_w = last_out_token
        scope = ExitStack()
        env._scope = scope
        if sname in SIDE_STAGES:
            emit_l2_ffn_phase(env, xs[xi], None, drams, SIDE_STAGES[sname])
            final_tokens.append(drams["_amout_buf"].last_w)
        else:
            STAGES[sname](env, xs[xi], xs[xi + 1], drams)
            xi += 1
            last_out_token = drams["_xout_buf"].last_w
        scope.close()
        env._scope = None
        P.barrier()

    if n_x > 0:
        final_tokens.append(last_out_token)
    P.op("sync", lambda e: e.nop(), extra_waits=[t for t in final_tokens if t], tag="final")
    P.emit()
    es.close()
    return nc



def _shard_x(x):
    shards = []
    for c in range(NCORES):
        s = x[c * BPC:(c + 1) * BPC]
        shards.append(np.ascontiguousarray(s.transpose(2, 0, 1).reshape(D, TOK)))
    return shards


def _unshard_x(outs):
    full = np.empty((NCORES * BPC, T, D), np.float32)
    for c, o in enumerate(outs):
        full[c * BPC:(c + 1) * BPC] = o.reshape(D, BPC, T).transpose(1, 2, 0)
    return full


def _bias_tiles(b):
    n = b.shape[0] // 128
    return np.ascontiguousarray(b.reshape(n, 128).T)


def l0_ffn_weights(inputs):
    return {
        "l0f_u1": np.ascontiguousarray(np.asarray(inputs["L0_ffn_U1"], np.float32)),
        "l0f_v1": np.ascontiguousarray(np.asarray(inputs["L0_ffn_V1"], np.float32)),
        "l0f_b1": _bias_tiles(np.asarray(inputs["L0_ffn_b1"], np.float32)),
        "l0f_u2": np.ascontiguousarray(np.asarray(inputs["L0_ffn_U2"], np.float32)),
        "l0f_v2": np.ascontiguousarray(np.asarray(inputs["L0_ffn_V2"], np.float32)),
        "l0f_b2": _bias_tiles(np.asarray(inputs["L0_ffn_b2"], np.float32)),
    }


def _consts():
    tri = (np.arange(128)[:, None] <= np.arange(128)[None, :]).astype(np.float32)
    return {
        "ident_f32": np.eye(128, dtype=np.float32),
        "triu01": np.ascontiguousarray(tri),
        "ramp2048": np.arange(T, dtype=np.float32).reshape(1, T),
    }


def l0_attn_weights(inputs):
    import ml_dtypes
    k = np.arange(T, dtype=np.float64)
    ak = np.zeros((16, 4, T), np.float32)
    aq = np.zeros((16, 4, T), np.float32)
    for h in range(16):
        s8 = 8.0 * 2.0 ** (-(h + 1) / 2.0)
        for dst, v in ((ak, s8 * k), (aq, -s8 * k)):
            vf = v.astype(np.float32)
            hi = vf.astype(ml_dtypes.bfloat16).astype(np.float32)
            lo = (vf - hi).astype(ml_dtypes.bfloat16).astype(np.float32)
            if dst is ak:
                dst[h, 0], dst[h, 1], dst[h, 2], dst[h, 3] = hi, lo, 1.0, 1.0
            else:
                dst[h, 0], dst[h, 1], dst[h, 2], dst[h, 3] = 1.0, 1.0, hi, lo
    w = {
        "l0a_wqt": np.ascontiguousarray(np.asarray(inputs["L0_attn_Wq"], np.float32).T),
        "l0a_wkt": np.ascontiguousarray(np.asarray(inputs["L0_attn_Wk"], np.float32).T),
        "l0a_wvt": np.ascontiguousarray(np.asarray(inputs["L0_attn_Wv"], np.float32).T),
        "l0a_wot": np.ascontiguousarray(np.asarray(inputs["L0_attn_Wo"], np.float32).T),
        "alibi_k": ak,
        "alibi_q": aq,
        "trineg": np.where(np.arange(128)[:, None] <= np.arange(128)[None, :],
                           0.0, -1e30).astype(np.float32),
    }
    w.update(_consts())
    return w


def l1_ffn_weights(inputs):
    w = {
        "l1f_routert": np.ascontiguousarray(np.asarray(inputs["L1_ffn_router_w"], np.float32).T),
        "l1f_gatet": np.ascontiguousarray(np.asarray(inputs["L1_ffn_gate_w"], np.float32).transpose(0, 2, 1)),
        "l1f_upt": np.ascontiguousarray(np.asarray(inputs["L1_ffn_up_w"], np.float32).transpose(0, 2, 1)),
        "l1f_downt": np.ascontiguousarray(np.asarray(inputs["L1_ffn_down_w"], np.float32).transpose(0, 2, 1)),
    }
    w.update(_consts())
    return w


def l2_attn_weights(inputs):
    convw = np.asarray(inputs["L2_attn_conv_w"], np.float32)[:, 0, :]
    w = {
        "l2a_inprojt": np.ascontiguousarray(np.asarray(inputs["L2_attn_in_proj_w"], np.float32).T),
        "l2a_convw": np.ascontiguousarray(convw.reshape(16, 128, 4).transpose(1, 0, 2).reshape(128, 64)),
        "l2a_convb": np.ascontiguousarray(np.asarray(inputs["L2_attn_conv_b"], np.float32).reshape(16, 128).T),
        "l2a_xprojt": np.ascontiguousarray(np.asarray(inputs["L2_attn_x_proj_w"], np.float32).T),
        "l2a_dtprojt": np.ascontiguousarray(np.asarray(inputs["L2_attn_dt_proj_w"], np.float32).T),
        "l2a_dtb": np.ascontiguousarray(np.asarray(inputs["L2_attn_dt_proj_b"], np.float32).reshape(16, 128).T),
        "l2a_alog": np.ascontiguousarray(np.asarray(inputs["L2_attn_A_log"], np.float32)
                                         .reshape(16, 128, 64).transpose(1, 0, 2).reshape(128, 1024)),
        "l2a_outprojt": np.ascontiguousarray(np.asarray(inputs["L2_attn_out_proj_w"], np.float32).T),
    }
    w.update(_consts())
    return w


def l1_attn_weights(inputs):
    w = {
        "l1a_wqkvt": np.ascontiguousarray(np.asarray(inputs["L1_attn_w_qkv"], np.float32)),
        "l1a_wot": np.ascontiguousarray(np.asarray(inputs["L1_attn_w_out"], np.float32)),
        "l1a_bout": _bias_tiles(np.asarray(inputs["L1_attn_b_out"], np.float32)),
    }
    w.update(_consts())
    return w


def l3_attn_weights(inputs):
    w = {
        "l3a_wqt": np.ascontiguousarray(np.asarray(inputs["L3_attn_WQ"], np.float32).T),
        "l3a_wkt": np.ascontiguousarray(np.asarray(inputs["L3_attn_WK"], np.float32).T),
        "l3a_wvt": np.ascontiguousarray(np.asarray(inputs["L3_attn_WV"], np.float32).T),
    }
    w.update(_consts())
    return w


def l3_ffn_weights(inputs):
    return {
        "l3f_w1t": np.ascontiguousarray(np.asarray(inputs["L3_ffn_w1"], np.float32)),
        "l3f_b1": _bias_tiles(np.asarray(inputs["L3_ffn_b1"], np.float32)),
        "l3f_w2t": np.ascontiguousarray(np.asarray(inputs["L3_ffn_w2"], np.float32)),
        "l3f_b2": _bias_tiles(np.asarray(inputs["L3_ffn_b2"], np.float32)),
    }


def run_stages(stage_names, x, wmap, sim=False):
    meta = {k: v.shape for k, v in wmap.items()}
    nc = build_program(stage_names, meta)
    shards = _shard_x(x)
    in_maps = [dict(wmap, x_io=s) for s in shards]
    if sim:
        import concourse.bass_interp as bass_interp
        simm = bass_interp.MultiCoreSim(nc, len(in_maps))
        for i, m in enumerate(in_maps):
            for k, v in m.items():
                simm.cores[i].tensor(k)[:] = v
        simm.simulate()
        outs = [np.array(simm.cores[i].mem_tensor("x_res")) for i in range(len(in_maps))]
        return _unshard_x(outs), None
    res = run_bass_kernel_spmd(nc, in_maps, core_ids=list(range(NCORES)))
    has_x = any(s not in SIDE_STAGES for s in stage_names)
    xo = _unshard_x([r["x_res"] for r in res.results]) if has_x else None
    ams = [r.get("amout") for r in res.results] if "amout" in res.results[0] else None
    return xo, ams


def _cat_ams(ams, col):
    return np.ascontiguousarray(np.concatenate([a[0:1, col:col + 1] for a in ams], axis=1))


def kernel(**inputs):
    x = np.asarray(inputs["L0_attn_x"], np.float32)
    l2fw = l2_ffn_weights(inputs)
    w1 = {}
    w1.update(l0_attn_weights(inputs))
    w1.update(l0_ffn_weights(inputs))
    w1.update(l1_attn_weights(inputs))
    w1.update(l1_ffn_weights(inputs))
    w1.update(l2_attn_weights(inputs))
    w1.update(l2fw)
    x, ams = run_stages(
        ["l0_attn", "l0_ffn", "l1_attn", "l1_ffn", "l2_attn", "l2f_p1"], x, w1)
    am = {"l2f_am0g": _cat_ams(ams, 0)}
    _, ams = run_stages(["l2f_p2"], x, dict(l2fw, **am))
    am["l2f_am1g"] = np.ascontiguousarray(
        np.concatenate([_cat_ams(ams, 0), _cat_ams(ams, 1)], axis=1))
    _, ams = run_stages(["l2f_p3"], x, dict(l2fw, **am))
    am["l2f_am2g"] = _cat_ams(ams, 0)
    _, ams = run_stages(["l2f_p4"], x, dict(l2fw, **am))
    am["l2f_am3g"] = _cat_ams(ams, 0)
    w5 = dict(l2fw, **am)
    w5.update(l3_attn_weights(inputs))
    w5.update(l3_ffn_weights(inputs))
    x, _ = run_stages(["l2f_p5", "l3_attn", "l3_ffn"], x, w5)
    return x


# version 06 (final)
# speedup vs baseline: 1.2750x; 1.0122x over previous
import numpy as np
from contextlib import ExitStack

import concourse.bass as bass
import concourse.mybir as mybir
from concourse.bass_utils import run_bass_kernel_spmd

F32 = mybir.dt.float32
BF16 = mybir.dt.bfloat16
AF = mybir.ActivationFunctionType
ALU = mybir.AluOpType
AX = mybir.AxisListType.X

NCORES = 8
BPC = 2
T = 2048
TOK = BPC * T
D = 1024
DT = D // 128
FC = 512
NFC = TOK // FC
LN_EPS = 1e-5



class _Buf:
    __slots__ = ("name", "last_w", "readers")

    def __init__(self, name):
        self.name = name
        self.last_w = None
        self.readers = []


class _Sem:
    __slots__ = ("h", "v")

    def __init__(self, h):
        self.h = h
        self.v = 0


class Prog:

    ENGINES = ("sync", "gpsimd", "tensor", "vector", "scalar")

    def __init__(self, nc, es):
        self.nc = nc
        self.es = es
        self.ops = {e: [] for e in self.ENGINES}
        self._semn = 0
        self.esem = {e: self.new_sem() for e in self.ENGINES}
        self.dsem = {e: self.new_sem() for e in ("sync", "gpsimd", "scalar", "vector")}
        self.nbuf = 0

    def new_sem(self):
        h = self.es.enter_context(self.nc.semaphore(f"S{self._semn}"))
        self._semn += 1
        return _Sem(h)

    def buf(self, name=None):
        self.nbuf += 1
        return _Buf(name or f"b{self.nbuf}")

    def barrier(self):
        snap = []
        for s in list(self.esem.values()) + list(self.dsem.values()):
            if s.v > 0:
                snap.append((s, s.v))
        self._pending_barrier = {e: list(snap) for e in self.ENGINES}

    def op(self, eng, fn, reads=(), writes=(), dma=False, tag="", extra_waits=()):
        waits = {}

        def add_wait(sv):
            if sv is None:
                return
            s, v = sv
            if waits.get(s, 0) < v:
                waits[s] = v

        pend = getattr(self, "_pending_barrier", None)
        if pend and pend.get(eng):
            for sv in pend[eng]:
                add_wait(sv)
            pend[eng] = []
        for b in reads:
            add_wait(b.last_w)
        for b in writes:
            add_wait(b.last_w)
            for r in b.readers:
                add_wait(r)
        for w in extra_waits:
            add_wait(w)
        if dma:
            sem = self.dsem[eng]
            n = 16
        else:
            sem = self.esem[eng]
            n = 1
        sem.v += n
        tok = (sem, sem.v)
        self.ops[eng].append((list(waits.items()), fn, (sem, n), tag))
        for b in reads:
            b.readers.append(tok)
        for b in writes:
            b.last_w = tok
            b.readers = []
        return tok

    def emit(self):
        waited = {e: {} for e in self.ENGINES}

        def mk(ename):
            def body(engobj):
                w8 = waited[ename]
                for waits, fn, (sem, n), tag in self.ops[ename]:
                    for s, v in waits:
                        if w8.get(s, 0) >= v:
                            continue
                        engobj.wait_ge(s.h, v)
                        w8[s] = v
                    instr = fn(engobj)
                    if instr is None:
                        raise RuntimeError(f"op '{tag}' on {ename} returned None")
                    instr.then_inc(sem.h, n)
            return body

        with self.nc.Block() as block:
            block.sync(mk("sync"))
            block.gpsimd(mk("gpsimd"))
            block.tensor(mk("tensor"))
            block.vector(mk("vector"))
            block.scalar(mk("scalar"))



def ceil_div(a, b):
    return (a + b - 1) // b


class Env:
    def __init__(self, nc, es, P):
        self.nc = nc
        self.es = es
        self.P = P
        self._uid = 0
        self.psum = []
        for i in range(6):
            t = es.enter_context(nc.psum_tensor(f"pb{i}", [128, FC], F32))
            self.psum.append((t, P.buf(f"pb{i}")))
        self._psum_rr = 0
        self.psum_bf = []
        for i in range(2):
            t = es.enter_context(nc.psum_tensor(f"pbf{i}", [128, FC], BF16))
            self.psum_bf.append((t, P.buf(f"pbf{i}")))
        self._psum_bf_rr = 0

    def uid(self):
        self._uid += 1
        return self._uid

    def sb(self, name, shape, dtype):
        scope = getattr(self, "_scope", None) or self.es
        t = scope.enter_context(self.nc.sbuf_tensor(f"{name}{self.uid()}", shape, dtype))
        return t, self.P.buf(name)

    def sb_shared(self, name, shape, dtype):
        t = self.es.enter_context(self.nc.sbuf_tensor(f"{name}{self.uid()}", shape, dtype))
        return t, self.P.buf(name)

    def bank(self):
        t, b = self.psum[self._psum_rr]
        self._psum_rr = (self._psum_rr + 1) % len(self.psum)
        return t, b

    def bank_bf(self):
        t, b = self.psum_bf[self._psum_bf_rr]
        self._psum_bf_rr = (self._psum_bf_rr + 1) % len(self.psum_bf)
        return t, b

    def const_col(self, val):
        if not hasattr(self, "_ccol"):
            self._ccol = {}
        if val not in self._ccol:
            t, b = self.sb_shared(f"cc{len(self._ccol)}_", [128, 1], F32)
            self.P.op("vector", lambda e: e.memset(t[:, :], float(val)), writes=[b], tag="const")
            self._ccol[val] = (t, b)
        return self._ccol[val]

    def ln_scratch(self, fc):
        if not hasattr(self, "_ln_scr"):
            self._ln_scr = {}
        if fc not in self._ln_scr:
            key = f"lnS{fc}_"
            ones, ones_b = self.sb_shared(key + "o", [128, 1], F32)
            onesr, onesr_b = self.sb_shared(key + "r", [1, 128], F32)
            self.P.op("vector", lambda e: e.memset(ones[:, :], 1.0), writes=[ones_b], tag="lno")
            self.P.op("vector", lambda e: e.memset(onesr[:, :], 1.0), writes=[onesr_b], tag="lnr")
            self._ln_scr[fc] = {
                "ones": (ones, ones_b), "onesr": (onesr, onesr_b),
                "xsq": self.sb_shared(key + "q", [128, fc], F32),
                "mean": self.sb_shared(key + "m", [1, fc], F32),
                "msq": self.sb_shared(key + "s", [1, fc], F32),
                "rstd": self.sb_shared(key + "d", [1, fc], F32),
                "shift": self.sb_shared(key + "h", [1, fc], F32),
                "rep": self.sb_shared(key + "p", [128, 2 * fc], F32),
            }
        return self._ln_scr[fc]


def load_wt_bf16(env, name, w_dram, rows, cols):
    P = env.P
    kt = ceil_div(rows, 128)
    slab, buf = env.sb(name, [128, kt * cols], BF16)
    src = w_dram.rearrange("(a p) c -> p a c", p=128)
    P.op("gpsimd", lambda e: e.dma_start(out=slab[:, :], in_=src), writes=[buf],
         dma=True, tag=f"load_{name}")
    return (slab, buf, kt, cols)


def wt_tile(slabinfo, k, m0, m1):
    slab, buf, kt, cols = slabinfo
    return slab[:, k * cols + m0: k * cols + m1]



def emit_ln_chunk(env, x_sb, x_buf, dtiles, fc):
    P = env.P
    scr = env.ln_scratch(fc)
    ones, ones_b = scr["ones"]
    onesr, onesr_b = scr["onesr"]
    xsq, xsq_b = scr["xsq"]
    mean, mean_b = scr["mean"]
    msq, msq_b = scr["msq"]
    rstd, rstd_b = scr["rstd"]
    shift, shift_b = scr["shift"]
    rep, rep_b = scr["rep"]

    bank_s, bs_b = env.bank()
    bank_q, bq_b = env.bank()

    def mm_sum(e):
        last = None
        for dt in range(dtiles):
            last = e.matmul(bank_s[0:1, 0:fc], ones[:, :], x_sb[:, dt * fc:(dt + 1) * fc],
                            start=(dt == 0), stop=(dt == dtiles - 1))
        return last
    P.op("tensor", mm_sum, reads=[ones_b, x_buf], writes=[bs_b], tag="ln_sum")

    for dt in range(dtiles):
        P.op("scalar", lambda e, dt=dt: e.activation(
            xsq[:, 0:fc], x_sb[:, dt * fc:(dt + 1) * fc], AF.Square),
            reads=[x_buf], writes=[xsq_b], tag="ln_sq")
        P.op("tensor", lambda e, dt=dt: e.matmul(
            bank_q[0:1, 0:fc], ones[:, :], xsq[:, 0:fc],
            start=(dt == 0), stop=(dt == dtiles - 1)),
            reads=[ones_b, xsq_b], writes=[bq_b], tag="ln_sumsq")

    ndiv = 1.0 / (dtiles * 128)
    P.op("scalar", lambda e: e.activation(mean[0:1, :], bank_s[0:1, 0:fc], AF.Copy, scale=ndiv),
         reads=[bs_b], writes=[mean_b], tag="ln_mean")
    P.op("scalar", lambda e: e.activation(msq[0:1, :], bank_q[0:1, 0:fc], AF.Copy, scale=ndiv),
         reads=[bq_b], writes=[msq_b], tag="ln_msq")
    P.op("scalar", lambda e: e.activation(rstd[0:1, :], mean[0:1, :], AF.Square),
         reads=[mean_b], writes=[rstd_b], tag="ln_m2")
    P.op("vector", lambda e: e.tensor_sub(rstd[0:1, :], msq[0:1, :], rstd[0:1, :]),
         reads=[msq_b, rstd_b], writes=[rstd_b], tag="ln_var")
    epsc, epsc_b = env.const_col(LN_EPS)
    P.op("scalar", lambda e: e.activation(rstd[0:1, :], rstd[0:1, :], AF.Ln, bias=epsc[0:1, 0:1]),
         reads=[rstd_b, epsc_b], writes=[rstd_b], tag="ln_ln")
    P.op("scalar", lambda e: e.activation(rstd[0:1, :], rstd[0:1, :], AF.Exp, scale=-0.5),
         reads=[rstd_b], writes=[rstd_b], tag="ln_rstd")
    P.op("vector", lambda e: e.tensor_mul(shift[0:1, :], mean[0:1, :], rstd[0:1, :]),
         reads=[mean_b, rstd_b], writes=[shift_b], tag="ln_mr")
    P.op("scalar", lambda e: e.activation(shift[0:1, :], shift[0:1, :], AF.Copy, scale=-1.0),
         reads=[shift_b], writes=[shift_b], tag="ln_neg")
    for r, (src, src_b) in enumerate(((rstd, rstd_b), (shift, shift_b))):
        bank_r, br_b = env.bank()
        P.op("tensor", lambda e, src=src, bank_r=bank_r: e.matmul(
            bank_r[:, 0:fc], onesr[:, :], src[0:1, :], start=True, stop=True),
            reads=[onesr_b, src_b], writes=[br_b], tag="ln_rep")
        P.op("scalar", lambda e, r=r, bank_r=bank_r: e.activation(
            rep[:, r * fc:(r + 1) * fc], bank_r[:, 0:fc], AF.Copy),
            reads=[br_b], writes=[rep_b], tag="ln_repc")
    for dt in range(dtiles):
        sl = slice(dt * fc, (dt + 1) * fc)
        P.op("vector", lambda e, sl=sl: e.tensor_mul(x_sb[:, sl], x_sb[:, sl], rep[:, 0:fc]),
             reads=[x_buf, rep_b], writes=[x_buf], tag="ln_scl")
        P.op("vector", lambda e, sl=sl: e.tensor_add(x_sb[:, sl], x_sb[:, sl], rep[:, fc:2 * fc]),
             reads=[x_buf, rep_b], writes=[x_buf], tag="ln_sft")



def x_chunk_src(x_dram, f0, fc):
    return x_dram.rearrange("(a p) c -> p a c", p=128)[:, :, f0:f0 + fc]


def emit_l3_ffn(env, x_in, x_out, drams):
    P = env.P
    key = f"l3f{env.uid()}"
    w1 = load_wt_bf16(env, key + "w1", drams["l3f_w1t"], 1024, 4096)
    w2 = load_wt_bf16(env, key + "w2", drams["l3f_w2t"], 4096, 1024)
    b1, b1_b = env.sb(key + "b1", [128, 32], F32)
    b2, b2_b = env.sb(key + "b2", [128, 8], F32)
    P.op("sync", lambda e: e.dma_start(out=b1[:, :], in_=drams["l3f_b1"]), writes=[b1_b], dma=True, tag="b1")
    P.op("sync", lambda e: e.dma_start(out=b2[:, :], in_=drams["l3f_b2"]), writes=[b2_b], dma=True, tag="b2")

    xc, xc_b = env.sb(key + "xc", [128, DT * FC], F32)
    xb, xb_b = env.sb(key + "xb", [128, DT * FC], BF16)
    hb, hb_b = env.sb(key + "hb", [128, 32 * FC], BF16)
    x_in_b = drams["_xin_buf"]
    x_out_b = drams["_xout_buf"]

    for f0 in range(0, TOK, FC):
        P.op("sync", lambda e, f0=f0: e.dma_start(out=xc[:, :], in_=x_chunk_src(x_in, f0, FC)),
             reads=[x_in_b], writes=[xc_b], dma=True, tag="xload")
        P.op("gpsimd", lambda e: e.dma_start(out=xb[:, :], in_=xc[:, :]),
             reads=[xc_b], writes=[xb_b], dma=True, tag="xcast")
        for mt in range(32):
            bank, bk_b = env.bank()

            def mm1(e, mt=mt, bank=bank):
                last = None
                for k in range(DT):
                    last = e.matmul(bank[:, 0:FC], wt_tile(w1, k, mt * 128, (mt + 1) * 128),
                                    xb[:, k * FC:(k + 1) * FC], start=(k == 0), stop=(k == DT - 1))
                return last
            P.op("tensor", mm1, reads=[w1[1], xb_b], writes=[bk_b], tag="mm1")
            P.op("scalar", lambda e, mt=mt, bank=bank: e.activation(
                hb[:, mt * FC:(mt + 1) * FC], bank[:, 0:FC], AF.Relu, bias=b1[:, mt:mt + 1]),
                reads=[bk_b, b1_b], writes=[hb_b], tag="relu")
        for mt in range(DT):
            bank, bk_b = env.bank()

            def mm2(e, mt=mt, bank=bank):
                last = None
                for k in range(32):
                    last = e.matmul(bank[:, 0:FC], wt_tile(w2, k, mt * 128, (mt + 1) * 128),
                                    hb[:, k * FC:(k + 1) * FC], start=(k == 0), stop=(k == 31))
                return last
            P.op("tensor", mm2, reads=[w2[1], hb_b], writes=[bk_b], tag="mm2")
            P.op("vector", lambda e, mt=mt, bank=bank: e.scalar_tensor_tensor(
                xc[:, mt * FC:(mt + 1) * FC], bank[:, 0:FC], b2[:, mt:mt + 1],
                xc[:, mt * FC:(mt + 1) * FC], ALU.add, ALU.add),
                reads=[bk_b, b2_b, xc_b], writes=[xc_b], tag="residadd")
        emit_ln_chunk(env, xc, xc_b, DT, FC)
        P.op("sync", lambda e, f0=f0: e.dma_start(out=x_chunk_src(x_out, f0, FC), in_=xc[:, :]),
             reads=[xc_b], writes=[x_out_b], dma=True, tag="xstore")


def emit_l0_ffn(env, x_in, x_out, drams):
    P = env.P
    key = f"l0f{env.uid()}"
    u1 = load_wt_bf16(env, key + "u1", drams["l0f_u1"], 1024, 256)
    v1 = load_wt_bf16(env, key + "v1", drams["l0f_v1"], 256, 4096)
    u2 = load_wt_bf16(env, key + "u2", drams["l0f_u2"], 4096, 256)
    v2 = load_wt_bf16(env, key + "v2", drams["l0f_v2"], 256, 1024)
    b1, b1_b = env.sb(key + "b1", [128, 32], F32)
    b2, b2_b = env.sb(key + "b2", [128, 8], F32)
    P.op("sync", lambda e: e.dma_start(out=b1[:, :], in_=drams["l0f_b1"]), writes=[b1_b], dma=True, tag="b1")
    P.op("sync", lambda e: e.dma_start(out=b2[:, :], in_=drams["l0f_b2"]), writes=[b2_b], dma=True, tag="b2")

    xc, xc_b = env.sb(key + "xc", [128, DT * FC], F32)
    xb, xb_b = env.sb(key + "xb", [128, DT * FC], BF16)
    pb, pb_b = env.sb(key + "pb", [128, 2 * FC], BF16)
    z, z_b = env.sb(key + "z", [128, FC], F32)
    z2, z2_b = env.sb(key + "z2", [128, FC], F32)
    hbuf, hb_b = env.sb(key + "hbuf", [128, 32 * FC], BF16)
    qb, qb_b = env.sb(key + "qb", [128, 2 * FC], BF16)
    x_in_b = drams["_xin_buf"]
    x_out_b = drams["_xout_buf"]

    for f0 in range(0, TOK, FC):
        P.op("sync", lambda e, f0=f0: e.dma_start(out=xc[:, :], in_=x_chunk_src(x_in, f0, FC)),
             reads=[x_in_b], writes=[xc_b], dma=True, tag="xload")
        P.op("gpsimd", lambda e: e.dma_start(out=xb[:, :], in_=xc[:, :]),
             reads=[xc_b], writes=[xb_b], dma=True, tag="xcast")
        for mt in range(2):
            bank, bk_b = env.bank()

            def mmp(e, mt=mt, bank=bank):
                last = None
                for k in range(DT):
                    last = e.matmul(bank[:, 0:FC], wt_tile(u1, k, mt * 128, (mt + 1) * 128),
                                    xb[:, k * FC:(k + 1) * FC], start=(k == 0), stop=(k == DT - 1))
                return last
            P.op("tensor", mmp, reads=[u1[1], xb_b], writes=[bk_b], tag="u1mm")
            P.op("scalar", lambda e, mt=mt, bank=bank: e.activation(
                pb[:, mt * FC:(mt + 1) * FC], bank[:, 0:FC], AF.Copy),
                reads=[bk_b], writes=[pb_b], tag="pcast")
        for mt in range(32):
            bank, bk_b = env.bank()

            def mmh(e, mt=mt, bank=bank):
                last = None
                for k in range(2):
                    last = e.matmul(bank[:, 0:FC], wt_tile(v1, k, mt * 128, (mt + 1) * 128),
                                    pb[:, k * FC:(k + 1) * FC], start=(k == 0), stop=(k == 1))
                return last
            P.op("tensor", mmh, reads=[v1[1], pb_b], writes=[bk_b], tag="v1mm")
            P.op("vector", lambda e, mt=mt, bank=bank: e.tensor_scalar(
                z[:, :], bank[:, 0:FC], b1[:, mt:mt + 1], None, ALU.add),
                reads=[bk_b, b1_b], writes=[z_b], tag="gz")
            P.op("scalar", lambda e: e.activation(z2[:, :], z[:, :], AF.Square),
                 reads=[z_b], writes=[z2_b], tag="gz2")
            P.op("vector", lambda e: e.tensor_scalar(z2[:, :], z2[:, :], 0.044715, 1.0, ALU.mult, ALU.add),
                 reads=[z2_b], writes=[z2_b], tag="gz2b")
            P.op("vector", lambda e: e.tensor_mul(z2[:, :], z[:, :], z2[:, :]),
                 reads=[z_b, z2_b], writes=[z2_b], tag="gz3")
            P.op("scalar", lambda e: e.activation(z2[:, :], z2[:, :], AF.Tanh, scale=0.7978845608028654),
                 reads=[z2_b], writes=[z2_b], tag="gth")
            P.op("vector", lambda e: e.scalar_tensor_tensor(z2[:, :], z2[:, :], 1.0, z[:, :], ALU.add, ALU.mult),
                 reads=[z2_b, z_b], writes=[z2_b], tag="gmul")
            P.op("scalar", lambda e, mt=mt: e.activation(
                hbuf[:, mt * FC:(mt + 1) * FC], z2[:, :], AF.Copy, scale=0.5),
                reads=[z2_b], writes=[hb_b], tag="gh")
        for mt in range(2):
            bank, bk_b = env.bank()

            def mmq(e, mt=mt, bank=bank):
                last = None
                for k in range(32):
                    last = e.matmul(bank[:, 0:FC], wt_tile(u2, k, mt * 128, (mt + 1) * 128),
                                    hbuf[:, k * FC:(k + 1) * FC], start=(k == 0), stop=(k == 31))
                return last
            P.op("tensor", mmq, reads=[u2[1], hb_b], writes=[bk_b], tag="u2mm")
            P.op("scalar", lambda e, mt=mt, bank=bank: e.activation(
                qb[:, mt * FC:(mt + 1) * FC], bank[:, 0:FC], AF.Copy),
                reads=[bk_b], writes=[qb_b], tag="qcast")
        for mt in range(DT):
            bank, bk_b = env.bank()

            def mmy(e, mt=mt, bank=bank):
                last = None
                for k in range(2):
                    last = e.matmul(bank[:, 0:FC], wt_tile(v2, k, mt * 128, (mt + 1) * 128),
                                    qb[:, k * FC:(k + 1) * FC], start=(k == 0), stop=(k == 1))
                return last
            P.op("tensor", mmy, reads=[v2[1], qb_b], writes=[bk_b], tag="v2mm")
            P.op("vector", lambda e, mt=mt, bank=bank: e.scalar_tensor_tensor(
                xc[:, mt * FC:(mt + 1) * FC], bank[:, 0:FC], b2[:, mt:mt + 1],
                xc[:, mt * FC:(mt + 1) * FC], ALU.add, ALU.add),
                reads=[bk_b, b2_b, xc_b], writes=[xc_b], tag="residadd")
        emit_ln_chunk(env, xc, xc_b, DT, FC)
        P.op("sync", lambda e, f0=f0: e.dma_start(out=x_chunk_src(x_out, f0, FC), in_=xc[:, :]),
             reads=[xc_b], writes=[x_out_b], dma=True, tag="xstore")


def emit_l3_attn(env, x_in, x_out, drams):
    P = env.P
    key = f"l3a{env.uid()}"
    wq = load_wt_bf16(env, key + "wq", drams["l3a_wqt"], 1024, 1024)
    wk = load_wt_bf16(env, key + "wk", drams["l3a_wkt"], 1024, 1024)
    wv = load_wt_bf16(env, key + "wv", drams["l3a_wvt"], 1024, 1024)
    identf, identf_b = env.sb(key + "idf", [128, 128], F32)
    P.op("sync", lambda e: e.dma_start(out=identf[:, :], in_=drams["ident_f32"]),
         writes=[identf_b], dma=True, tag="idf")
    identb, identb_b = env.sb(key + "idb", [128, 128], BF16)
    P.op("gpsimd", lambda e: e.dma_start(out=identb[:, :], in_=drams["ident_f32"]),
         writes=[identb_b], dma=True, tag="idb")
    triu, triu_b = env.sb(key + "tri", [128, 128], BF16)
    P.op("gpsimd", lambda e: e.dma_start(out=triu[:, :], in_=drams["triu01"]),
         writes=[triu_b], dma=True, tag="tri")
    onesc, onesc_b = env.sb(key + "onc", [128, 1], BF16)
    P.op("vector", lambda e: e.memset(onesc[:, :], 1.0), writes=[onesc_b], tag="onc")

    xc, xc_b = env.sb(key + "xc", [128, DT * FC], F32)
    xb, xb_b = env.sb(key + "xb", [128, DT * FC], BF16)
    NT = T // 128
    q_sb, q_b = env.sb(key + "q", [128, DT * T], BF16)
    k_sb, k_b = env.sb(key + "k", [128, DT * T], BF16)
    vc, vc_b = env.sb(key + "v", [128, DT * FC], BF16)
    vT, vT_b = env.sb(key + "vt", [128, NT * D], BF16)
    att_tok, att_tok_b = env.sb(key + "at", [128, D], F32)
    recip, recip_b = env.sb(key + "rc", [128, 1], F32)
    xq, xq_b = env.sb(key + "xq", [128, DT * 128], F32)
    attnT, attnT_b = env.sb(key + "an", [128, 128], BF16)
    x_in_b = drams["_xin_buf"]
    x_out_b = drams["_xout_buf"]

    for b in range(BPC):
        toff = b * T
        for ci in range(T // FC):
            f0 = toff + ci * FC
            P.op("sync", lambda e, f0=f0: e.dma_start(out=xc[:, :], in_=x_chunk_src(x_in, f0, FC)),
                 reads=[x_in_b], writes=[xc_b], dma=True, tag="xload")
            P.op("gpsimd", lambda e: e.dma_start(out=xb[:, :], in_=xc[:, :]),
                 reads=[xc_b], writes=[xb_b], dma=True, tag="xcast")
            for wslab, dst, dst_b, scl in ((wq, q_sb, q_b, None), (wk, k_sb, k_b, None), (wv, vc, vc_b, None)):
                for mt in range(DT):
                    bank, bk_b = env.bank()

                    def mm(e, wslab=wslab, mt=mt, bank=bank):
                        last = None
                        for k in range(DT):
                            last = e.matmul(bank[:, 0:FC], wt_tile(wslab, k, mt * 128, (mt + 1) * 128),
                                            xb[:, k * FC:(k + 1) * FC], start=(k == 0), stop=(k == DT - 1))
                        return last
                    P.op("tensor", mm, reads=[wslab[1], xb_b], writes=[bk_b], tag="proj")
                    if dst is vc:
                        sl = slice(mt * FC, (mt + 1) * FC)
                    else:
                        sl = slice(mt * T + ci * FC, mt * T + (ci + 1) * FC)
                    P.op("scalar", lambda e, sl=sl, dst=dst, bank=bank: e.activation(
                        dst[:, sl], bank[:, 0:FC], AF.Copy),
                        reads=[bk_b], writes=[dst_b], tag="pevac")
            for dt in range(DT):
                for ts in range(FC // 128):
                    tt = ci * (FC // 128) + ts
                    bank, bk_b = env.bank_bf()
                    P.op("tensor", lambda e, dt=dt, ts=ts, bank=bank: e.transpose(
                        bank[:, 0:128], vc[:, dt * FC + ts * 128: dt * FC + (ts + 1) * 128],
                        identb[:, :]), reads=[vc_b, identb_b], writes=[bk_b], tag="vtr")
                    P.op("scalar", lambda e, tt=tt, dt=dt, bank=bank: e.activation(
                        vT[:, tt * D + dt * 128: tt * D + (dt + 1) * 128],
                        bank[:, 0:128], AF.Copy),
                        reads=[bk_b], writes=[vT_b], tag="vtevac")
        for qc in range(NT):
            bank_av0, bav0_b = env.bank()
            bank_av1, bav1_b = env.bank()
            bank_nrm, bnrm_b = env.bank()
            bank_s, bs_b = env.bank()
            for kt in range(qc + 1):
                def mms(e, kt=kt, qc=qc, bank_s=bank_s):
                    last = None
                    for dt in range(DT):
                        last = e.matmul(bank_s[:, 0:128],
                                        k_sb[:, dt * T + kt * 128: dt * T + (kt + 1) * 128],
                                        q_sb[:, dt * T + qc * 128: dt * T + (qc + 1) * 128],
                                        start=(dt == 0), stop=(dt == DT - 1))
                    return last
                P.op("tensor", mms, reads=[k_b, q_b], writes=[bs_b], tag="smm")
                P.op("scalar", lambda e, bank_s=bank_s: e.activation(
                    attnT[:, :], bank_s[:, 0:128], AF.Exp, scale=float(1024 ** -0.5)),
                    reads=[bs_b], writes=[attnT_b], tag="sexp")
                if kt == qc:
                    P.op("vector", lambda e: e.tensor_mul(attnT[:, :], attnT[:, :], triu[:, :]),
                         reads=[attnT_b, triu_b], writes=[attnT_b], tag="smask")
                for half, (bk, bkb) in enumerate(((bank_av0, bav0_b), (bank_av1, bav1_b))):
                    P.op("tensor", lambda e, kt=kt, qc=qc, half=half, bk=bk: e.matmul(
                        bk[:, 0:FC], attnT[:, :],
                        vT[:, kt * D + half * FC: kt * D + (half + 1) * FC],
                        start=(kt == 0), stop=(kt == qc)),
                        reads=[attnT_b, vT_b], writes=[bkb], tag="avmm")
                P.op("tensor", lambda e, kt=kt, qc=qc, bank_nrm=bank_nrm: e.matmul(
                    bank_nrm[:, 0:1], attnT[:, :], onesc[:, :],
                    start=(kt == 0), stop=(kt == qc)),
                    reads=[attnT_b, onesc_b], writes=[bnrm_b], tag="nrmm")
            P.op("vector", lambda e, bank_nrm=bank_nrm: e.reciprocal(recip[:, 0:1], bank_nrm[:, 0:1]),
                 reads=[bnrm_b], writes=[recip_b], tag="recip")
            for half, (bk, bkb) in enumerate(((bank_av0, bav0_b), (bank_av1, bav1_b))):
                P.op("scalar", lambda e, half=half, bk=bk: e.activation(
                    att_tok[:, half * FC:(half + 1) * FC], bk[:, 0:FC], AF.Copy, scale=recip[:, 0:1]),
                    reads=[bkb, recip_b], writes=[att_tok_b], tag="avnorm")
            f0 = toff + qc * 128
            P.op("sync", lambda e, f0=f0: e.dma_start(out=xq[:, :], in_=x_chunk_src(x_in, f0, 128)),
                 reads=[x_in_b], writes=[xq_b], dma=True, tag="xload2")
            for dt in range(DT):
                bank_t, bt_b = env.bank()
                P.op("tensor", lambda e, dt=dt, bank_t=bank_t: e.transpose(
                    bank_t[:, 0:128], att_tok[:, dt * 128:(dt + 1) * 128], identf[:, :]),
                    reads=[att_tok_b, identf_b], writes=[bt_b], tag="otr")
                P.op("vector", lambda e, dt=dt, bank_t=bank_t: e.tensor_add(
                    xq[:, dt * 128:(dt + 1) * 128], xq[:, dt * 128:(dt + 1) * 128], bank_t[:, 0:128]),
                    reads=[bt_b, xq_b], writes=[xq_b], tag="rad")
            emit_ln_chunk(env, xq, xq_b, DT, 128)
            P.op("sync", lambda e, f0=f0: e.dma_start(out=x_chunk_src(x_out, f0, 128), in_=xq[:, :]),
                 reads=[xq_b], writes=[x_out_b], dma=True, tag="xstore")


def _emit_mha(env, x_in, x_out, drams, alibi):
    P = env.P
    H, DH = 16, 64
    NT = T // 128
    pre = "l0a" if alibi else "l1a"
    key = f"{pre}{env.uid()}"
    if alibi:
        wq = load_wt_bf16(env, key + "wq", drams[pre + "_wqt"], 1024, 1024)
        wk = load_wt_bf16(env, key + "wk", drams[pre + "_wkt"], 1024, 1024)
        wv = load_wt_bf16(env, key + "wv", drams[pre + "_wvt"], 1024, 1024)
    else:
        wqkv = load_wt_bf16(env, key + "wqkv", drams[pre + "_wqkvt"], 1024, 3072)
    wo = load_wt_bf16(env, key + "wo", drams[pre + "_wot"], 1024, 1024)
    if not alibi:
        bout, bout_b = env.sb(key + "bo", [128, 8], F32)
        P.op("sync", lambda e: e.dma_start(out=bout[:, :], in_=drams[pre + "_bout"]),
             writes=[bout_b], dma=True, tag="bout")

    identf, identf_b = env.sb(key + "idf", [128, 128], F32)
    P.op("sync", lambda e: e.dma_start(out=identf[:, :], in_=drams["ident_f32"]),
         writes=[identf_b], dma=True, tag="idf")
    identb, identb_b = env.sb(key + "idb", [128, 128], BF16)
    P.op("gpsimd", lambda e: e.dma_start(out=identb[:, :], in_=drams["ident_f32"]),
         writes=[identb_b], dma=True, tag="idb")
    triu, triu_b = env.sb(key + "tri", [128, 128], BF16)
    P.op("gpsimd", lambda e: e.dma_start(out=triu[:, :], in_=drams["triu01"]),
         writes=[triu_b], dma=True, tag="tri")
    if alibi:
        trineg, trineg_b = env.sb(key + "trn", [128, 128], F32)
        P.op("sync", lambda e: e.dma_start(out=trineg[:, :], in_=drams["trineg"]),
             writes=[trineg_b], dma=True, tag="trn")

    xb, xb_b = env.sb(key + "xb", [128, DT * T], BF16)
    NAUG = 68 if alibi else 64
    ksl = [env.sb(key + "ks", [NAUG, T], BF16) for _ in range(2)]
    qsl = [env.sb(key + "qs", [NAUG, T], BF16) for _ in range(2)]
    vhl = [env.sb(key + "vh", [64, T], BF16) for _ in range(2)]
    vTl = [env.sb(key + "vt", [128, NT * 65], BF16) for _ in range(2)]
    atl = [env.sb(key + "an", [128, 512], BF16) for _ in range(2)]
    att_tok, att_tok_b = env.sb(key + "at", [128, 64], F32)
    recip, recip_b = env.sb(key + "rc", [128, 1], F32)
    aall, aall_b = env.sb(key + "aa", [128, DT * T], BF16)
    xc, xc_b = env.sb(key + "xc", [128, DT * FC], F32)
    x_in_b = drams["_xin_buf"]
    x_out_b = drams["_xout_buf"]

    slopes = [2.0 ** (-(h + 1) / 2.0) for h in range(H)]

    for b in range(BPC):
        toff = b * T
        P.op("gpsimd", lambda e, toff=toff: e.dma_start(
            out=xb.ap().rearrange("p (a c) -> p a c", a=DT),
            in_=x_in.rearrange("(a p) c -> p a c", p=128)[:, :, toff:toff + T]),
            reads=[x_in_b], writes=[xb_b], dma=True, tag="xbcast")
        for h in range(H):
            kslab, kslab_b = ksl[h % 2]
            qslab, qslab_b = qsl[h % 2]
            vh, vh_b = vhl[h % 2]
            vT, vT_b = vTl[h % 2]
            if alibi:
                plist = ((wq, qslab, qslab_b), (wk, kslab, kslab_b), (wv, vh, vh_b))
                offs = (h * 64, h * 64, h * 64)
            else:
                plist = ((wqkv, qslab, qslab_b), (wqkv, kslab, kslab_b), (wqkv, vh, vh_b))
                offs = (h * 64, 1024 + h * 64, 2048 + h * 64)
            for (slab, dst, dst_b), m0 in zip(plist, offs):
                for ci in range(T // FC):
                    bank, bk_b = env.bank()

                    def mmp(e, slab=slab, m0=m0, ci=ci, bank=bank):
                        last = None
                        for k in range(DT):
                            last = e.matmul(bank[0:64, 0:FC], wt_tile(slab, k, m0, m0 + 64),
                                            xb[:, k * T + ci * FC: k * T + (ci + 1) * FC],
                                            start=(k == 0), stop=(k == DT - 1))
                        return last
                    P.op("tensor", mmp, reads=[slab[1], xb_b], writes=[bk_b], tag="hproj")
                    P.op("scalar", lambda e, dst=dst, ci=ci, bank=bank: e.activation(
                        dst[0:64, ci * FC:(ci + 1) * FC], bank[0:64, 0:FC], AF.Copy),
                        reads=[bk_b], writes=[dst_b], tag="hevac")
            if alibi:
                P.op("gpsimd", lambda e, h=h, kslab=kslab: e.dma_start(
                    out=kslab[64:68, :], in_=drams["alibi_k"][h]),
                    writes=[kslab_b], dma=True, tag="aug_k")
                P.op("gpsimd", lambda e, h=h, qslab=qslab: e.dma_start(
                    out=qslab[64:68, :], in_=drams["alibi_q"][h]),
                    writes=[qslab_b], dma=True, tag="aug_q")
            P.op("vector", lambda e, vT=vT: e.memset(vT[:, :], 1.0), writes=[vT_b], tag="vt_on")
            for tt in range(NT):
                bank, bk_b = env.bank_bf()
                P.op("tensor", lambda e, tt=tt, bank=bank, vh=vh: e.transpose(
                    bank[0:128, 0:64], vh[0:64, tt * 128:(tt + 1) * 128], identb[0:64, 0:64]),
                    reads=[vh_b, identb_b], writes=[bk_b], tag="vtr")
                P.op("scalar", lambda e, tt=tt, bank=bank, vT=vT: e.activation(
                    vT[:, tt * 65: tt * 65 + 64], bank[0:128, 0:64], AF.Copy),
                    reads=[bk_b], writes=[vT_b], tag="vtev")
            for qc in range(NT):
                bank_av, bav_b = env.bank()
                nkt = qc + 1
                pend = None
                for k0 in range(0, nkt, 4):
                    kn = min(4, nkt - k0)
                    bank_s, bs_b = env.bank()

                    def mms(e, k0=k0, kn=kn, qc=qc, bank_s=bank_s, kslab=kslab, qslab=qslab):
                        last = None
                        for j in range(kn):
                            last = e.matmul(bank_s[:, j * 128:(j + 1) * 128],
                                            kslab[0:NAUG, (k0 + j) * 128:(k0 + j + 1) * 128],
                                            qslab[0:NAUG, qc * 128:(qc + 1) * 128],
                                            start=True, stop=True)
                        return last
                    P.op("tensor", mms, reads=[kslab_b, qslab_b], writes=[bs_b], tag="smm")
                    diag_j = qc - k0 if (k0 <= qc < k0 + kn) else None
                    if alibi and diag_j is not None:
                        P.op("vector", lambda e, diag_j=diag_j, bank_s=bank_s: e.tensor_add(
                            bank_s[:, diag_j * 128:(diag_j + 1) * 128],
                            bank_s[:, diag_j * 128:(diag_j + 1) * 128], trineg[:, :]),
                            reads=[bs_b, trineg_b], writes=[bs_b], tag="spremask")
                    attnT, attnT_b = atl[(k0 // 4) % 2]
                    P.op("scalar", lambda e, kn=kn, bank_s=bank_s, attnT=attnT: e.activation(
                        attnT[:, 0:kn * 128], bank_s[:, 0:kn * 128], AF.Exp, scale=0.125),
                        reads=[bs_b], writes=[attnT_b], tag="sexp")
                    if (not alibi) and diag_j is not None:
                        P.op("vector", lambda e, diag_j=diag_j, attnT=attnT: e.tensor_mul(
                            attnT[:, diag_j * 128:(diag_j + 1) * 128],
                            attnT[:, diag_j * 128:(diag_j + 1) * 128], triu[:, :]),
                            reads=[attnT_b, triu_b], writes=[attnT_b], tag="smask")

                    if pend is not None:
                        pk0, pkn, pat, pat_b = pend

                        def mmavp(e, k0=pk0, kn=pkn, qc=qc, bank_av=bank_av, attnT=pat, vT=vT):
                            last = None
                            for j in range(kn):
                                last = e.matmul(bank_av[:, 0:65], attnT[:, j * 128:(j + 1) * 128],
                                                vT[:, (k0 + j) * 65:(k0 + j + 1) * 65],
                                                start=(k0 + j == 0), stop=(k0 + j == qc))
                            return last
                        P.op("tensor", mmavp, reads=[pat_b, vT_b], writes=[bav_b], tag="avmm")
                    pend = (k0, kn, attnT, attnT_b)
                pk0, pkn, pat, pat_b = pend

                def mmavl(e, k0=pk0, kn=pkn, qc=qc, bank_av=bank_av, attnT=pat, vT=vT):
                    last = None
                    for j in range(kn):
                        last = e.matmul(bank_av[:, 0:65], attnT[:, j * 128:(j + 1) * 128],
                                        vT[:, (k0 + j) * 65:(k0 + j + 1) * 65],
                                        start=(k0 + j == 0), stop=(k0 + j == qc))
                    return last
                P.op("tensor", mmavl, reads=[pat_b, vT_b], writes=[bav_b], tag="avmm")
                P.op("vector", lambda e, bank_av=bank_av: e.reciprocal(recip[:, 0:1], bank_av[:, 64:65]),
                     reads=[bav_b], writes=[recip_b], tag="recip")
                P.op("scalar", lambda e, bank_av=bank_av: e.activation(
                    att_tok[:, 0:64], bank_av[:, 0:64], AF.Copy, scale=recip[:, 0:1]),
                    reads=[bav_b, recip_b], writes=[att_tok_b], tag="avnorm")
                bank_t, bt_b = env.bank()
                P.op("tensor", lambda e, bank_t=bank_t: e.transpose(
                    bank_t[0:64, 0:128], att_tok[:, 0:64], identf[:, :]),
                    reads=[att_tok_b, identf_b], writes=[bt_b], tag="otr")
                dt0 = h // 2
                pbase = (h % 2) * 64
                P.op("scalar", lambda e, dt0=dt0, pbase=pbase, qc=qc, bank_t=bank_t: e.activation(
                    aall[pbase:pbase + 64, dt0 * T + qc * 128: dt0 * T + (qc + 1) * 128],
                    bank_t[0:64, 0:128], AF.Copy),
                    reads=[bt_b], writes=[aall_b], tag="oev")
        for ci in range(T // FC):
            f0 = toff + ci * FC
            P.op("sync", lambda e, f0=f0: e.dma_start(out=xc[:, :], in_=x_chunk_src(x_in, f0, FC)),
                 reads=[x_in_b], writes=[xc_b], dma=True, tag="xload2")
            for mt in range(DT):
                bank, bk_b = env.bank()

                def mmo(e, mt=mt, ci=ci, bank=bank):
                    last = None
                    for k in range(DT):
                        last = e.matmul(bank[:, 0:FC], wt_tile(wo, k, mt * 128, (mt + 1) * 128),
                                        aall[:, k * T + ci * FC: k * T + (ci + 1) * FC],
                                        start=(k == 0), stop=(k == DT - 1))
                    return last
                P.op("tensor", mmo, reads=[wo[1], aall_b], writes=[bk_b], tag="womm")
                if alibi:
                    P.op("vector", lambda e, mt=mt, bank=bank: e.tensor_add(
                        xc[:, mt * FC:(mt + 1) * FC], xc[:, mt * FC:(mt + 1) * FC], bank[:, 0:FC]),
                        reads=[bk_b, xc_b], writes=[xc_b], tag="rad")
                else:
                    P.op("vector", lambda e, mt=mt, bank=bank: e.scalar_tensor_tensor(
                        xc[:, mt * FC:(mt + 1) * FC], bank[:, 0:FC], bout[:, mt:mt + 1],
                        xc[:, mt * FC:(mt + 1) * FC], ALU.add, ALU.add),
                        reads=[bk_b, bout_b, xc_b], writes=[xc_b], tag="radb")
            emit_ln_chunk(env, xc, xc_b, DT, FC)
            P.op("sync", lambda e, f0=f0: e.dma_start(out=x_chunk_src(x_out, f0, FC), in_=xc[:, :]),
                 reads=[xc_b], writes=[x_out_b], dma=True, tag="xstore")


def emit_l0_attn(env, x_in, x_out, drams):
    _emit_mha(env, x_in, x_out, drams, alibi=True)


def emit_l1_attn(env, x_in, x_out, drams):
    _emit_mha(env, x_in, x_out, drams, alibi=False)


def emit_l1_ffn(env, x_in, x_out, drams):
    P = env.P
    E, I = 8, 2816
    NKI = I // 128
    GRP = 1024
    NG = TOK // GRP
    key = f"l1f{env.uid()}"

    rw, rw_b = env.sb(key + "rw", [128, DT * E], F32)
    P.op("sync", lambda e: e.dma_start(out=rw[:, :], in_=drams["l1f_routert"].rearrange("(a p) c -> p a c", p=128)),
         writes=[rw_b], dma=True, tag="rw")
    identf, identf_b = env.sb(key + "idf", [128, 128], F32)
    P.op("sync", lambda e: e.dma_start(out=identf[:, :], in_=drams["ident_f32"]),
         writes=[identf_b], dma=True, tag="idf")

    xc, xc_b = env.sb(key + "xc", [128, DT * FC], F32)
    xq, xq_b = env.sb(key + "xq", [128, DT * GRP], BF16)
    lg, lg_b = env.sb(key + "lg", [8, FC], F32)
    ptok, ptok_b = env.sb(key + "pt", [128, E], F32)
    ssc, ssc_b = env.sb(key + "ss", [128, 6], F32)
    eqm, eqm_b = env.sb(key + "eq", [128, E], F32)
    wte, wte_b = env.sb(key + "wt", [128, E], F32)
    wrep, wrep_b = env.sb(key + "wr", [128, GRP], F32)
    sg = [env.sb(key + "sg", [128, FC], F32) for _ in range(2)]
    hh = [env.sb(key + "hh", [128, FC], F32) for _ in range(2)]
    hbuf, hbuf_b = env.sb(key + "hb", [128, NKI * GRP], BF16)
    yacc, yacc_b = env.sb(key + "ya", [128, DT * GRP], F32)
    wall, wall_b = yacc, yacc_b
    wsl = [env.sb(key + "ws", [128, DT * 512], BF16) for _ in range(2)]
    dsl, dsl_b = env.sb(key + "ds", [128, NKI * 1024], BF16)
    wdram = env.nc.dram_tensor(key + "wdram", [8, TOK], F32, kind="Internal").ap()
    wdram_b = P.buf(key + "wdram")
    x_in_b = drams["_xin_buf"]
    x_out_b = drams["_xout_buf"]

    for ci in range(NFC):
        f0 = ci * FC
        P.op("sync", lambda e, f0=f0: e.dma_start(out=xc[:, :], in_=x_chunk_src(x_in, f0, FC)),
             reads=[x_in_b], writes=[xc_b], dma=True, tag="xload")
        bank, bk_b = env.bank()

        def mmr(e, bank=bank):
            last = None
            for k in range(DT):
                last = e.matmul(bank[0:8, 0:FC], rw[:, k * E:(k + 1) * E],
                                xc[:, k * FC:(k + 1) * FC], start=(k == 0), stop=(k == DT - 1))
            return last
        P.op("tensor", mmr, reads=[rw_b, xc_b], writes=[bk_b], tag="rmm")
        P.op("scalar", lambda e, bank=bank: e.activation(lg[:, :], bank[0:8, 0:FC], AF.Copy),
             reads=[bk_b], writes=[lg_b], tag="rev")
        for ts in range(FC // 128):
            bank_t, bt_b = env.bank()
            P.op("tensor", lambda e, ts=ts, bank_t=bank_t: e.transpose(
                bank_t[0:128, 0:8], lg[0:8, ts * 128:(ts + 1) * 128], identf[0:8, 0:8]),
                reads=[lg_b, identf_b], writes=[bt_b], tag="rtr")
            P.op("scalar", lambda e, bank_t=bank_t: e.activation(
                ptok[:, :], bank_t[0:128, 0:8], AF.Copy),
                reads=[bt_b], writes=[ptok_b], tag="rtev")
            P.op("vector", lambda e: e.reduce_max(ssc[:, 0:1], ptok[:, :], axis=AX, negate=True),
                 reads=[ptok_b], writes=[ssc_b], tag="rmax")
            P.op("scalar", lambda e: e.activation(ptok[:, :], ptok[:, :], AF.Exp, bias=ssc[:, 0:1]),
                 reads=[ptok_b, ssc_b], writes=[ptok_b], tag="rexp")
            P.op("vector", lambda e: e.reduce_sum(ssc[:, 1:2], ptok[:, :], axis=AX),
                 reads=[ptok_b], writes=[ssc_b], tag="rsum")
            P.op("vector", lambda e: e.reciprocal(ssc[:, 2:3], ssc[:, 1:2]),
                 reads=[ssc_b], writes=[ssc_b], tag="rrec")
            P.op("vector", lambda e: e.tensor_scalar(ptok[:, :], ptok[:, :], ssc[:, 2:3], None, ALU.mult),
                 reads=[ptok_b, ssc_b], writes=[ptok_b], tag="rnorm")
            P.op("vector", lambda e: e.reduce_max(ssc[:, 3:4], ptok[:, :], axis=AX),
                 reads=[ptok_b], writes=[ssc_b], tag="t2m1")
            P.op("vector", lambda e: e.tensor_scalar(eqm[:, :], ptok[:, :], ssc[:, 3:4], None, ALU.is_lt),
                 reads=[ptok_b, ssc_b], writes=[eqm_b], tag="t2lt")
            P.op("vector", lambda e: e.tensor_mul(eqm[:, :], eqm[:, :], ptok[:, :]),
                 reads=[eqm_b, ptok_b], writes=[eqm_b], tag="t2p2")
            P.op("vector", lambda e: e.reduce_max(ssc[:, 4:5], eqm[:, :], axis=AX),
                 reads=[eqm_b], writes=[ssc_b], tag="t2m2")
            P.op("vector", lambda e: e.tensor_scalar(wte[:, :], ptok[:, :], ssc[:, 4:5], None, ALU.is_ge),
                 reads=[ptok_b, ssc_b], writes=[wte_b], tag="t2ge")
            P.op("vector", lambda e: e.tensor_mul(wte[:, :], wte[:, :], ptok[:, :]),
                 reads=[wte_b, ptok_b], writes=[wte_b], tag="t2w")
            bank_w, bw_b = env.bank()
            P.op("tensor", lambda e, bank_w=bank_w: e.transpose(
                bank_w[0:8, 0:128], wte[:, :], identf[:, :]),
                reads=[wte_b, identf_b], writes=[bw_b], tag="wtr")
            P.op("scalar", lambda e, f0=f0, ts=ts, bank_w=bank_w: e.activation(
                wall[0:8, f0 + ts * 128: f0 + (ts + 1) * 128], bank_w[0:8, 0:128], AF.Copy),
                reads=[bw_b], writes=[wall_b], tag="wev")
    P.op("sync", lambda e: e.dma_start(out=wdram, in_=wall[0:8, 0:TOK]),
         reads=[wall_b], writes=[wdram_b], dma=True, tag="wst")

    IC = [(i * 256, 256) for i in range(11)]
    li = 0
    for g in range(NG):
        t0 = g * GRP
        P.op("vector", lambda e: e.memset(yacc[:, :], 0.0), writes=[yacc_b], tag="yz")
        P.op("gpsimd", lambda e, t0=t0: e.dma_start(
            out=xq.ap().rearrange("p (a c) -> p a c", a=DT),
            in_=x_in.rearrange("(a p) c -> p a c", p=128)[:, :, t0:t0 + GRP]),
            reads=[x_in_b], writes=[xq_b], dma=True, tag="xqcast")
        for ei in range(E):
            P.op("sync", lambda e, ei=ei, t0=t0: e.dma_start(
                out=wrep[:, :],
                in_=wdram[ei:ei + 1, t0:t0 + GRP].broadcast_to([128, GRP])),
                reads=[wdram_b], writes=[wrep_b], dma=True, tag="wrep")
            P.op("gpsimd", lambda e, ei=ei: e.dma_start(
                out=dsl.ap().rearrange("p (a c) -> p a c", a=NKI),
                in_=drams["l1f_downt"][ei].rearrange("(a p) c -> p a c", p=128)),
                writes=[dsl_b], dma=True, tag="dld")
            for ici, (i0, ilen) in enumerate(IC):
                nmt = ilen // 128
                wslab, wslab_b = wsl[li % 2]
                li += 1
                P.op("gpsimd", lambda e, ei=ei, i0=i0, ilen=ilen, wslab=wslab: e.dma_start(
                    out=wslab.ap()[:, 0:DT * ilen].rearrange("p (a c) -> p a c", a=DT),
                    in_=drams["l1f_gatet"][ei].rearrange("(a p) c -> p a c", p=128)[:, :, i0:i0 + ilen]),
                    writes=[wslab_b], dma=True, tag="gld")
                P.op("gpsimd", lambda e, ei=ei, i0=i0, ilen=ilen, wslab=wslab: e.dma_start(
                    out=wslab.ap()[:, DT * ilen:2 * DT * ilen].rearrange("p (a c) -> p a c", a=DT),
                    in_=drams["l1f_upt"][ei].rearrange("(a p) c -> p a c", p=128)[:, :, i0:i0 + ilen]),
                    writes=[wslab_b], dma=True, tag="uld")
                for tc in range(GRP // FC):
                    for mt in range(nmt):
                        bank_g, bg_b = env.bank()
                        bank_u, bu_b = env.bank()
                        sgt, sg_b = sg[mt % 2]
                        hht, hh_b = hh[mt % 2]

                        def mmg(e, i0=i0, ilen=ilen, mt=mt, tc=tc, bank_g=bank_g, wslab=wslab):
                            last = None
                            for k in range(DT):
                                last = e.matmul(bank_g[:, 0:FC],
                                                wslab[:, k * ilen + mt * 128: k * ilen + (mt + 1) * 128],
                                                xq[:, k * GRP + tc * FC: k * GRP + (tc + 1) * FC],
                                                start=(k == 0), stop=(k == DT - 1))
                            return last
                        P.op("tensor", mmg, reads=[wslab_b, xq_b], writes=[bg_b], tag="gmm")
                        P.op("scalar", lambda e, bank_g=bank_g, sgt=sgt: e.activation(
                            sgt[:, :], bank_g[:, 0:FC], AF.Silu),
                            reads=[bg_b], writes=[sg_b], tag="silu")

                        def mmu(e, i0=i0, ilen=ilen, mt=mt, tc=tc, bank_u=bank_u, wslab=wslab):
                            last = None
                            for k in range(DT):
                                last = e.matmul(bank_u[:, 0:FC],
                                                wslab[:, (DT + k) * ilen + mt * 128: (DT + k) * ilen + (mt + 1) * 128],
                                                xq[:, k * GRP + tc * FC: k * GRP + (tc + 1) * FC],
                                                start=(k == 0), stop=(k == DT - 1))
                            return last
                        P.op("tensor", mmu, reads=[wslab_b, xq_b], writes=[bu_b], tag="umm")
                        P.op("vector", lambda e, bank_u=bank_u, sgt=sgt, hht=hht: e.tensor_mul(
                            hht[:, :], sgt[:, :], bank_u[:, 0:FC]),
                            reads=[sg_b, bu_b], writes=[hh_b], tag="hmul")
                        it = (i0 // 128) + mt
                        P.op("vector", lambda e, it=it, tc=tc, hht=hht: e.tensor_mul(
                            hbuf[:, it * GRP + tc * FC: it * GRP + (tc + 1) * FC], hht[:, :],
                            wrep[:, tc * FC:(tc + 1) * FC]),
                            reads=[hh_b, wrep_b], writes=[hbuf_b], tag="hw")
            for tc in range(GRP // FC):
                for mt in range(DT):
                    bank_d, bd_b = env.bank()

                    def mmd(e, mt=mt, tc=tc, bank_d=bank_d):
                        last = None
                        for k in range(NKI):
                            last = e.matmul(bank_d[:, 0:FC],
                                            dsl[:, k * D + mt * 128:k * D + (mt + 1) * 128],
                                            hbuf[:, k * GRP + tc * FC: k * GRP + (tc + 1) * FC],
                                            start=(k == 0), stop=(k == NKI - 1))
                        return last
                    P.op("tensor", mmd, reads=[dsl_b, hbuf_b], writes=[bd_b], tag="dmm")
                    sl = slice(mt * GRP + tc * FC, mt * GRP + (tc + 1) * FC)
                    P.op("vector", lambda e, sl=sl, bank_d=bank_d: e.tensor_add(
                        yacc[:, sl], yacc[:, sl], bank_d[:, 0:FC]),
                        reads=[bd_b, yacc_b], writes=[yacc_b], tag="yadd")
        for tc in range(GRP // FC):
            f0 = t0 + tc * FC
            P.op("sync", lambda e, f0=f0: e.dma_start(out=xc[:, :], in_=x_chunk_src(x_in, f0, FC)),
                 reads=[x_in_b], writes=[xc_b], dma=True, tag="xload2")
            for dt in range(DT):
                sl = slice(dt * FC, (dt + 1) * FC)
                sla = slice(dt * GRP + tc * FC, dt * GRP + (tc + 1) * FC)
                P.op("vector", lambda e, sl=sl, sla=sla: e.tensor_add(xc[:, sl], xc[:, sl], yacc[:, sla]),
                     reads=[xc_b, yacc_b], writes=[xc_b], tag="rad")
            emit_ln_chunk(env, xc, xc_b, DT, FC)
            P.op("sync", lambda e, f0=f0: e.dma_start(out=x_chunk_src(x_out, f0, FC), in_=xc[:, :]),
                 reads=[xc_b], writes=[x_out_b], dma=True, tag="xstore")


def emit_l2_attn(env, x_in, x_out, drams):
    P = env.P
    nc = env.nc
    DI = 2048
    NDT = 16
    NST = 64
    CW = 515
    key = f"l2m{env.uid()}"

    convw, convw_b = env.sb(key + "cw", [128, NDT * 4], F32)
    convb, convb_b = env.sb(key + "cb", [128, NDT], F32)
    dtb, dtb_b = env.sb(key + "db", [128, NDT], F32)
    alog, alog_b = env.sb(key + "al", [128, NDT * NST], F32)
    P.op("sync", lambda e: e.dma_start(out=convw[:, :], in_=drams["l2a_convw"]), writes=[convw_b], dma=True, tag="cw")
    P.op("sync", lambda e: e.dma_start(out=convb[:, :], in_=drams["l2a_convb"]), writes=[convb_b], dma=True, tag="cb")
    P.op("sync", lambda e: e.dma_start(out=dtb[:, :], in_=drams["l2a_dtb"]), writes=[dtb_b], dma=True, tag="db")
    P.op("sync", lambda e: e.dma_start(out=alog[:, :], in_=drams["l2a_alog"]), writes=[alog_b], dma=True, tag="al")
    P.op("scalar", lambda e: e.activation(alog[:, :], alog[:, :], AF.Exp),
         reads=[alog_b], writes=[alog_b], tag="aexp")
    P.op("vector", lambda e: e.tensor_scalar(alog[:, :], alog[:, :], -1.0, None, ALU.mult),
         reads=[alog_b], writes=[alog_b], tag="aneg")

    wsl2 = [env.sb(key + "ws", [128, 4096], BF16) for _ in range(2)]

    delta, delta_b = env.sb(key + "de", [128, NDT * FC], F32)
    xsw, xsw_b = env.sb(key + "xw", [128, NDT * FC], F32)
    stx, stx_b = env.sb(key + "sx", [128, NDT * CW], F32)
    cvd, cvd_b = stx, stx_b
    halo, halo_b = env.sb(key + "hl", [128, NDT * 3], F32)
    yacc, yacc_b = env.sb(key + "ya", [128, NDT * FC], F32)
    xsb, xsb_b = env.sb(key + "xb", [128, NDT * FC], BF16)
    bcr2 = [env.sb(key + "bc", [128, 2 * FC], F32) for _ in range(2)]
    carry, carry_b = env.sb(key + "ca", [128, NST * NDT], F32)
    sp1, sp1_b = env.sb(key + "s1", [128, FC], F32)
    sp2, sp2_b = env.sb(key + "s2", [128, FC], F32)
    sp3, sp3_b = env.sb(key + "s3", [128, FC], F32)
    sp4, sp4_b = env.sb(key + "s4", [128, FC], F32)
    dabs = [(sp1, sp1_b), (sp3, sp3_b)]
    dxbs = [(sp2, sp2_b), (sp4, sp4_b)]
    xcb, xcb_b = xsb, xsb_b

    zd = nc.dram_tensor(key + "zd", [128, 8 * NDT * FC], F32, kind="Internal").ap()
    zd_b = P.buf(key + "zd")
    bcd = nc.dram_tensor(key + "bcd", [128, FC], F32, kind="Internal").ap()
    bcd_b = P.buf(key + "bcd")
    one_c, one_c_b = env.const_col(1.0)
    x_in_b = drams["_xin_buf"]
    x_out_b = drams["_xout_buf"]

    NTC = T // FC
    for b in range(BPC):
        P.op("vector", lambda e: e.memset(carry[:, :], 0.0), writes=[carry_b], tag="car0")
        for tc in range(NTC):
            cidx = b * NTC + tc
            f0 = b * T + tc * FC
            P.op("gpsimd", lambda e, f0=f0: e.dma_start(
                out=xcb.ap()[:, 0:DT * FC].rearrange("p (a c) -> p a c", a=DT),
                in_=x_in.rearrange("(a p) c -> p a c", p=128)[:, :, f0:f0 + FC]),
                reads=[x_in_b], writes=[xcb_b], dma=True, tag="xcast")
            if tc == 0:
                P.op("vector", lambda e: e.memset(halo[:, :], 0.0), writes=[halo_b], tag="halo0")
            for mt in range(NDT):
                P.op("vector", lambda e, mt=mt: e.tensor_copy(
                    cvd[:, mt * CW:mt * CW + 3], halo[:, mt * 3:(mt + 1) * 3]),
                    reads=[halo_b], writes=[cvd_b], tag="halocp")
            for q in range(8):
                wslab, wslab_b = wsl2[q % 2]
                P.op("gpsimd", lambda e, q=q, wslab=wslab: e.dma_start(
                    out=wslab.ap().rearrange("p (a c) -> p a c", a=DT),
                    in_=drams["l2a_inprojt"].rearrange("(a p) c -> p a c", p=128)[:, :, q * 512:(q + 1) * 512]),
                    writes=[wslab_b], dma=True, tag="ipld")
                for m in range(4):
                    mt = q * 4 + m
                    bank, bk_b = env.bank()

                    def mmi(e, m=m, bank=bank, wslab=wslab):
                        last = None
                        for k in range(DT):
                            last = e.matmul(bank[:, 0:FC], wslab[:, k * 512 + m * 128:(k * 512) + (m + 1) * 128],
                                            xcb[:, k * FC:(k + 1) * FC], start=(k == 0), stop=(k == DT - 1))
                        return last
                    P.op("tensor", mmi, reads=[wslab_b, xcb_b], writes=[bk_b], tag="ipmm")
                    if mt < NDT:
                        P.op("scalar", lambda e, mt=mt, bank=bank: e.activation(
                            cvd[:, mt * CW + 3:mt * CW + CW], bank[:, 0:FC], AF.Copy),
                            reads=[bk_b], writes=[cvd_b], tag="ipev")
                    else:
                        zt = mt - NDT
                        P.op("scalar", lambda e, zt=zt, bank=bank: e.activation(
                            delta[:, zt * FC:(zt + 1) * FC], bank[:, 0:FC], AF.Silu),
                            reads=[bk_b], writes=[delta_b], tag="zsilu")
            P.op("sync", lambda e, cidx=cidx: e.dma_start(
                out=zd[:, cidx * NDT * FC:(cidx + 1) * NDT * FC], in_=delta[:, :]),
                reads=[delta_b], writes=[zd_b], dma=True, tag="zspill")
            for mt in range(NDT):
                sl = slice(mt * FC, (mt + 1) * FC)
                P.op("vector", lambda e, mt=mt, sl=sl: e.tensor_scalar(
                    xsw[:, sl], cvd[:, mt * CW + 0:mt * CW + FC], convw[:, mt * 4:mt * 4 + 1], None, ALU.mult),
                    reads=[cvd_b, convw_b], writes=[xsw_b], tag="cv0")
                for j in range(1, 4):
                    P.op("vector", lambda e, mt=mt, sl=sl, j=j: e.scalar_tensor_tensor(
                        xsw[:, sl], cvd[:, mt * CW + j:mt * CW + j + FC], convw[:, mt * 4 + j:mt * 4 + j + 1],
                        xsw[:, sl], ALU.mult, ALU.add),
                        reads=[cvd_b, convw_b, xsw_b], writes=[xsw_b], tag="cvj")
                P.op("scalar", lambda e, mt=mt, sl=sl: e.activation(
                    xsw[:, sl], xsw[:, sl], AF.Silu, bias=convb[:, mt:mt + 1]),
                    reads=[xsw_b, convb_b], writes=[xsw_b], tag="cvsilu")
            for mt in range(NDT):
                P.op("vector", lambda e, mt=mt: e.tensor_copy(
                    halo[:, mt * 3:(mt + 1) * 3], cvd[:, mt * CW + FC:mt * CW + FC + 3]),
                    reads=[cvd_b], writes=[halo_b], tag="halosv")
            P.op("gpsimd", lambda e: e.dma_start(out=xsb[:, :], in_=xsw[:, :]),
                 reads=[xsw_b], writes=[xsb_b], dma=True, tag="xscast")
            wslab, wslab_b = wsl2[0]
            P.op("gpsimd", lambda e, wslab=wslab: e.dma_start(
                out=wslab.ap()[:, 0:NDT * 128].rearrange("p (a c) -> p a c", a=NDT),
                in_=drams["l2a_xprojt"].rearrange("(a p) c -> p a c", p=128)),
                writes=[wslab_b], dma=True, tag="xpld")
            bank, bk_b = env.bank()

            def mmx(e, bank=bank, wslab=wslab):
                last = None
                for k in range(NDT):
                    last = e.matmul(bank[:, 0:FC], wslab[:, k * 128:(k + 1) * 128],
                                    xsb[:, k * FC:(k + 1) * FC], start=(k == 0), stop=(k == NDT - 1))
                return last
            P.op("tensor", mmx, reads=[wslab_b, xsb_b], writes=[bk_b], tag="xpmm")
            P.op("scalar", lambda e, bank=bank: e.activation(yacc[:, 0:FC], bank[:, 0:FC], AF.Copy),
                 reads=[bk_b], writes=[yacc_b], tag="xpev")
            P.op("sync", lambda e: e.dma_start(out=bcd, in_=yacc[:, 0:FC]),
                 reads=[yacc_b], writes=[bcd_b], dma=True, tag="bcspill")
            for q8 in range(8):
                wslab, wslab_b = wsl2[(q8 + 1) % 2]
                P.op("gpsimd", lambda e, q8=q8, wslab=wslab: e.dma_start(
                    out=wslab.ap().rearrange("p (a c) -> p a c", a=NDT),
                    in_=drams["l2a_dtprojt"].rearrange("(a p) c -> p a c", p=128)[:, :, q8 * 256:(q8 + 1) * 256]),
                    writes=[wslab_b], dma=True, tag="dtld")
                for m in range(2):
                    mt = q8 * 2 + m
                    bank, bk_b = env.bank()

                    def mmd(e, m=m, bank=bank, wslab=wslab):
                        last = None
                        for k in range(NDT):
                            last = e.matmul(bank[:, 0:FC], wslab[:, k * 256 + m * 128:k * 256 + (m + 1) * 128],
                                            xsb[:, k * FC:(k + 1) * FC], start=(k == 0), stop=(k == NDT - 1))
                        return last
                    P.op("tensor", mmd, reads=[wslab_b, xsb_b], writes=[bk_b], tag="dtmm")
                    P.op("scalar", lambda e, mt=mt, bank=bank: e.activation(
                        sp1[:, :], bank[:, 0:FC], AF.Abs, bias=dtb[:, mt:mt + 1]),
                        reads=[bk_b, dtb_b], writes=[sp1_b], tag="spabs")
                    P.op("scalar", lambda e: e.activation(sp1[:, :], sp1[:, :], AF.Exp, scale=-1.0),
                         reads=[sp1_b], writes=[sp1_b], tag="spexp")
                    P.op("scalar", lambda e: e.activation(sp1[:, :], sp1[:, :], AF.Ln, bias=one_c[:, 0:1]),
                         reads=[sp1_b, one_c_b], writes=[sp1_b], tag="spln")
                    P.op("scalar", lambda e, mt=mt, bank=bank: e.activation(
                        sp2[:, :], bank[:, 0:FC], AF.Relu, bias=dtb[:, mt:mt + 1]),
                        reads=[bk_b, dtb_b], writes=[sp2_b], tag="sprelu")
                    P.op("vector", lambda e, mt=mt: e.tensor_add(
                        delta[:, mt * FC:(mt + 1) * FC], sp2[:, :], sp1[:, :]),
                        reads=[sp1_b, sp2_b], writes=[delta_b], tag="spadd")
            for mt in range(NDT):
                sl = slice(mt * FC, (mt + 1) * FC)
                P.op("vector", lambda e, sl=sl: e.tensor_mul(xsw[:, sl], delta[:, sl], xsw[:, sl]),
                     reads=[delta_b, xsw_b], writes=[xsw_b], tag="wmul")
            P.op("vector", lambda e: e.memset(yacc[:, :], 0.0), writes=[yacc_b], tag="yz")
            for n in range(NST):
                bcrep, bcrep_b = bcr2[n % 2]
                P.op("sync", lambda e, n=n, bcrep=bcrep: e.dma_start(
                    out=bcrep.ap().rearrange("p (a c) -> p a c", a=2),
                    in_=bcd.rearrange("(j r) c -> j r c", j=2)[:, n:n + 1, :].transpose([1, 0, 2])
                        .rearrange("o j c -> (o j) c")[None].broadcast_to([128, 2, FC])),
                    reads=[bcd_b], writes=[bcrep_b], dma=True, tag="bcrep")
                if True:
                    for dt in range(NDT):
                        dab, dab_b = dabs[dt % 2]
                        dxb, dxb_b = dxbs[dt % 2]
                        P.op("scalar", lambda e, dt=dt, n=n, dab=dab: e.activation(
                            dab[:, :], delta[:, dt * FC:(dt + 1) * FC], AF.Exp,
                            scale=alog[:, dt * NST + n:dt * NST + n + 1]),
                            reads=[delta_b, alog_b], writes=[dab_b], tag="dA")
                        P.op("vector", lambda e, dt=dt, dxb=dxb, bcrep=bcrep: e.tensor_mul(
                            dxb[:, :], xsw[:, dt * FC:(dt + 1) * FC], bcrep[:, 0:FC]),
                            reads=[xsw_b, bcrep_b], writes=[dxb_b], tag="dBx")
                        P.op("vector", lambda e, dt=dt, n=n, dab=dab, dxb=dxb: e.tensor_tensor_scan(
                            stx[:, dt * FC:(dt + 1) * FC], dab[:, :], dxb[:, :],
                            carry[:, n * NDT + dt:n * NDT + dt + 1], ALU.mult, ALU.add),
                            reads=[dab_b, dxb_b, carry_b], writes=[stx_b], tag="scan")
                    P.op("vector", lambda e, n=n: e.tensor_copy(
                        carry[:, n * NDT:(n + 1) * NDT],
                        stx.ap()[:, 0:NDT * FC].rearrange("p (d t) -> p d t", d=NDT)[:, :, FC - 1:FC]),
                        reads=[stx_b], writes=[carry_b], tag="carry")
                    P.op("vector", lambda e, bcrep=bcrep: e.tensor_mul(
                        stx.ap()[:, 0:NDT * FC].rearrange("p (d t) -> p d t", d=NDT),
                        stx.ap()[:, 0:NDT * FC].rearrange("p (d t) -> p d t", d=NDT),
                        bcrep[:, FC:2 * FC].rearrange(
                            "p (o t) -> p o t", o=1).broadcast_to([128, NDT, FC])),
                        reads=[stx_b, bcrep_b], writes=[stx_b], tag="zC")
                    P.op("vector", lambda e: e.tensor_add(yacc[:, :], yacc[:, :], stx[:, 0:NDT * FC]),
                         reads=[yacc_b, stx_b], writes=[yacc_b], tag="yadd")
            P.op("sync", lambda e, cidx=cidx: e.dma_start(
                out=stx[:, 0:NDT * FC], in_=zd[:, cidx * NDT * FC:(cidx + 1) * NDT * FC]),
                reads=[zd_b], writes=[stx_b], dma=True, tag="zload")
            for mt in range(NDT):
                sl = slice(mt * FC, (mt + 1) * FC)
                P.op("vector", lambda e, sl=sl: e.tensor_mul(xsw[:, sl], yacc[:, sl], stx[:, sl]),
                     reads=[yacc_b, stx_b], writes=[xsw_b], tag="y2")
            P.op("gpsimd", lambda e: e.dma_start(out=xsb[:, :], in_=xsw[:, :]),
                 reads=[xsw_b], writes=[xsb_b], dma=True, tag="y2cast")
            P.op("sync", lambda e, f0=f0: e.dma_start(
                out=stx.ap()[:, 0:DT * FC].rearrange("p (a c) -> p a c", a=DT),
                in_=x_in.rearrange("(a p) c -> p a c", p=128)[:, :, f0:f0 + FC]),
                reads=[x_in_b], writes=[stx_b], dma=True, tag="xload2")
            for q4 in range(4):
                wslab, wslab_b = wsl2[q4 % 2]
                P.op("gpsimd", lambda e, q4=q4, wslab=wslab: e.dma_start(
                    out=wslab.ap()[:, 0:NDT * 256].rearrange("p (a c) -> p a c", a=NDT),
                    in_=drams["l2a_outprojt"].rearrange("(a p) c -> p a c", p=128)[:, :, q4 * 256:(q4 + 1) * 256]),
                    writes=[wslab_b], dma=True, tag="opld")
                for m in range(2):
                    mt = q4 * 2 + m
                    bank, bk_b = env.bank()

                    def mmo(e, m=m, bank=bank, wslab=wslab):
                        last = None
                        for k in range(NDT):
                            last = e.matmul(bank[:, 0:FC], wslab[:, k * 256 + m * 128:k * 256 + (m + 1) * 128],
                                            xsb[:, k * FC:(k + 1) * FC], start=(k == 0), stop=(k == NDT - 1))
                        return last
                    P.op("tensor", mmo, reads=[wslab_b, xsb_b], writes=[bk_b], tag="opmm")
                    P.op("vector", lambda e, mt=mt, bank=bank: e.tensor_add(
                        stx[:, mt * FC:(mt + 1) * FC], stx[:, mt * FC:(mt + 1) * FC], bank[:, 0:FC]),
                        reads=[bk_b, stx_b], writes=[stx_b], tag="rad")
            emit_ln_chunk(env, stx, stx_b, DT, FC)
            P.op("sync", lambda e, f0=f0: e.dma_start(
                out=x_chunk_src(x_out, f0, FC), in_=stx.ap()[:, 0:DT * FC].rearrange("p (a c) -> p a c", a=DT)),
                reads=[stx_b], writes=[x_out_b], dma=True, tag="xstore")


MAGIC = 12582912.0
QMAXI = 32767.0


def _emit_amax_update(env, acc, acc_b, src_ap, src_bufs, tag):
    P = env.P
    P.op("vector", lambda e: e.reduce_max(acc[:, 1:2], src_ap, axis=AX, apply_absolute_value=True),
         reads=src_bufs + [acc_b], writes=[acc_b], tag=tag)
    P.op("vector", lambda e: e.tensor_max(acc[:, 0:1], acc[:, 0:1], acc[:, 1:2]),
         reads=[acc_b], writes=[acc_b], tag=tag + "m")


def _emit_amax_final(env, acc, acc_b, identf, identf_b, out_sb, out_sb_b, col):
    P = env.P
    bank, bk_b = env.bank()
    P.op("tensor", lambda e, bank=bank: e.transpose(bank[0:1, 0:128], acc[:, 0:1], identf[:, :]),
         reads=[acc_b, identf_b], writes=[bk_b], tag="amtr")
    P.op("scalar", lambda e, bank=bank: e.activation(acc[0:1, 4:4 + 128], bank[0:1, 0:128], AF.Copy),
         reads=[bk_b], writes=[acc_b], tag="amev")
    P.op("vector", lambda e, col=col: e.reduce_max(out_sb[0:1, col:col + 1], acc[0:1, 4:4 + 128], axis=AX),
         reads=[acc_b], writes=[out_sb_b], tag="amred")


def emit_l2_ffn_phase(env, x_in, x_out, drams, phase):
    P = env.P
    nc = env.nc
    II = 3072
    NIT = II // 128
    key = f"l2f{env.uid()}p{phase}"

    identf, identf_b = env.sb(key + "idf", [128, 128], F32)
    P.op("sync", lambda e: e.dma_start(out=identf[:, :], in_=drams["ident_f32"]),
         writes=[identf_b], dma=True, tag="idf")
    identb, identb_b = env.sb(key + "idb", [128, 128], BF16)
    P.op("gpsimd", lambda e: e.dma_start(out=identb[:, :], in_=drams["ident_f32"]),
         writes=[identb_b], dma=True, tag="idb")

    acc, acc_b = env.sb(key + "ac", [128, 134], F32)
    aout, aout_b = env.sb(key + "ao", [1, 2], F32)
    P.op("vector", lambda e: e.memset(acc[:, :], 0.0), writes=[acc_b], tag="acc0")
    P.op("vector", lambda e: e.memset(aout[:, :], 0.0), writes=[aout_b], tag="ao0")
    x_in_b = drams["_xin_buf"]

    gsc, gsc_b = env.sb(key + "gs", [1, 16], F32)
    gin, gin_b = env.sb(key + "gi", [1, 16], F32)
    scd = nc.dram_tensor(key + "scd", [1, 16], F32, kind="Internal").ap()
    scd_b = P.buf(key + "scd")
    scc, scc_b = env.sb(key + "sc", [128, 16], F32)

    def global_scale(src_key, rows, out_cols):
        P.op("sync", lambda e, rows=rows: e.dma_start(out=gin[0:1, 0:8 * rows], in_=drams[src_key]),
             writes=[gin_b], dma=True, tag="gld")
        for r, (c_s, c_i) in enumerate(out_cols):
            P.op("vector", lambda e, r=r, c_s=c_s: e.reduce_max(
                gsc[0:1, c_s:c_s + 1], gin[0:1, r * 8:(r + 1) * 8], axis=AX),
                reads=[gin_b], writes=[gsc_b], tag="gmax")
            P.op("vector", lambda e, c_s=c_s: e.tensor_scalar(
                gsc[0:1, c_s:c_s + 1], gsc[0:1, c_s:c_s + 1], 1.0 / QMAXI, 1e-12, ALU.mult, ALU.max),
                reads=[gsc_b], writes=[gsc_b], tag="gscale")
            P.op("vector", lambda e, c_s=c_s, c_i=c_i: e.reciprocal(gsc[0:1, c_i:c_i + 1], gsc[0:1, c_s:c_s + 1]),
                 reads=[gsc_b], writes=[gsc_b], tag="ginv")

    if phase >= 2:
        global_scale("l2f_am0g", 1, [(0, 1)])
    if phase >= 3:
        global_scale("l2f_am1g", 2, [(2, 3), (4, 5)])
    if phase >= 4:
        global_scale("l2f_am2g", 1, [(6, 7)])
    if phase == 5:
        global_scale("l2f_am3g", 1, [(8, 9)])
    if phase >= 2:
        P.op("sync", lambda e: e.dma_start(out=scd, in_=gsc[0:1, :]),
             reads=[gsc_b], writes=[scd_b], dma=True, tag="scspill")
        P.op("sync", lambda e: e.dma_start(out=scc[:, :], in_=scd[0:1, :].broadcast_to([128, 16])),
             reads=[scd_b], writes=[scc_b], dma=True, tag="screp")

    xc, xc_b = env.sb(key + "xc", [128, DT * FC], F32)
    if phase == 1:
        for f0 in range(0, TOK, FC):
            P.op("sync", lambda e, f0=f0: e.dma_start(out=xc[:, :], in_=x_chunk_src(x_in, f0, FC)),
                 reads=[x_in_b], writes=[xc_b], dma=True, tag="xload")
            _emit_amax_update(env, acc, acc_b, xc[:, :], [xc_b], "amx")
        _emit_amax_final(env, acc, acc_b, identf, identf_b, aout, aout_b, 0)
        P.op("sync", lambda e: e.dma_start(out=drams["_amout"], in_=aout[:, :]),
             reads=[aout_b], writes=[drams["_amout_buf"]], dma=True, tag="amst")
        return

    wslab, wslab_b = env.sb(key + "ws", [128, 2 * DT * II], BF16)
    wtl = [env.sb(key + "wt" + str(j), [128, 1024], F32) for j in range(2)]
    wtbl = [env.sb(key + "wb" + str(j), [128, 1024], BF16) for j in range(2)]
    bml = [env.sb(key + "bm" + str(j), [128, 96], F32) for j in range(2)]
    need_up = phase in (2, 4, 5)
    need_gate = True

    def dequant_to_slab(wkey, slab_off, tagp):
        def one_tile(ot, wt, wt_b, wtb, wtb_b, bmax, bmax_b):
            P.op("sync", lambda e, ot=ot: e.dma_start(out=wt[:, :], in_=drams[wkey][ot * 128:(ot + 1) * 128, :]),
                 reads=[], writes=[wt_b], dma=True, tag=tagp + "ld")
            P.op("vector", lambda e: e.reduce_max(
                bmax[:, 0:32], wt.ap().rearrange("p (b k) -> p b k", k=32), axis=AX,
                apply_absolute_value=True),
                reads=[wt_b], writes=[bmax_b], tag=tagp + "bm")
            P.op("vector", lambda e: e.tensor_scalar(
                bmax[:, 32:64], bmax[:, 0:32], 1.0 / 7.0, 1e-12, ALU.mult, ALU.max),
                reads=[bmax_b], writes=[bmax_b], tag=tagp + "s")
            P.op("vector", lambda e: e.reciprocal(bmax[:, 64:96], bmax[:, 32:64]),
                 reads=[bmax_b], writes=[bmax_b], tag=tagp + "rs")
            P.op("vector", lambda e: e.tensor_mul(
                wt.ap().rearrange("p (b k) -> p b k", k=32),
                wt.ap().rearrange("p (b k) -> p b k", k=32),
                bmax[:, 64:96].rearrange("p (b o) -> p b o", o=1).broadcast_to([128, 32, 32])),
                reads=[wt_b, bmax_b], writes=[wt_b], tag=tagp + "div")
            P.op("vector", lambda e: e.tensor_scalar(wt[:, :], wt[:, :], MAGIC, MAGIC, ALU.add, ALU.subtract),
                 reads=[wt_b], writes=[wt_b], tag=tagp + "rnd")
            P.op("vector", lambda e: e.tensor_mul(
                wt.ap().rearrange("p (b k) -> p b k", k=32),
                wt.ap().rearrange("p (b k) -> p b k", k=32),
                bmax[:, 32:64].rearrange("p (b o) -> p b o", o=1).broadcast_to([128, 32, 32])),
                reads=[wt_b, bmax_b], writes=[wt_b], tag=tagp + "mul")
            P.op("gpsimd", lambda e: e.dma_start(out=wtb[:, :], in_=wt[:, :]),
                 reads=[wt_b], writes=[wtb_b], dma=True, tag=tagp + "cast")
            for k in range(DT):
                bank, bk_b = env.bank_bf()
                P.op("tensor", lambda e, k=k, bank=bank: e.transpose(
                    bank[0:128, 0:128], wtb[:, k * 128:(k + 1) * 128], identb[:, :]),
                    reads=[wtb_b, identb_b], writes=[bk_b], tag=tagp + "tr")
                P.op("scalar", lambda e, k=k, ot=ot, bank=bank: e.activation(
                    wslab[:, slab_off + k * II + ot * 128: slab_off + k * II + (ot + 1) * 128],
                    bank[0:128, 0:128], AF.Copy),
                    reads=[bk_b], writes=[wslab_b], tag=tagp + "ev")

        for ot in range(NIT):
            one_tile(ot, *wtl[ot % 2], *wtbl[ot % 2], *bml[ot % 2])

    if need_up:
        dequant_to_slab("l2f_wup", 0, "du")
    dequant_to_slab("l2f_wgate", DT * II, "dg")

    xqb, xqb_b = env.sb(key + "xq", [128, DT * FC], BF16)
    upq, upq_b = env.sb(key + "uq", [128, FC], F32)
    gq, gq_b = env.sb(key + "gq", [128, FC], F32)
    sig, sig_b = env.sb(key + "sg", [128, FC], F32)
    oqb, oqb_b = env.sb(key + "oq", [128, NIT * FC], BF16)
    od = nc.dram_tensor(key + "od", [128, NFC * NIT * FC], BF16, kind="Internal").ap() if phase == 5 else None
    od_b = P.buf(key + "od") if phase == 5 else None

    for ci in range(NFC):
        f0 = ci * FC
        P.op("sync", lambda e, f0=f0: e.dma_start(out=xc[:, :], in_=x_chunk_src(x_in, f0, FC)),
             reads=[x_in_b], writes=[xc_b], dma=True, tag="xload")
        for dt in range(DT):
            sl = slice(dt * FC, (dt + 1) * FC)
            P.op("vector", lambda e, sl=sl: e.tensor_scalar(
                xc[:, sl], xc[:, sl], scc[:, 1:2], MAGIC, ALU.mult, ALU.add),
                reads=[xc_b, scc_b], writes=[xc_b], tag="xq1")
            P.op("vector", lambda e, sl=sl: e.tensor_scalar(
                xqb[:, sl], xc[:, sl], MAGIC, scc[:, 0:1], ALU.subtract, ALU.mult),
                reads=[xc_b, scc_b], writes=[xqb_b], tag="xq2")
        for mt in range(NIT):
            bank_u, bu_b = env.bank()
            bank_g, bg_b = env.bank()
            if need_up:
                def mmu(e, mt=mt, bank_u=bank_u):
                    last = None
                    for k in range(DT):
                        last = e.matmul(bank_u[:, 0:FC], wslab[:, k * II + mt * 128:k * II + (mt + 1) * 128],
                                        xqb[:, k * FC:(k + 1) * FC], start=(k == 0), stop=(k == DT - 1))
                    return last
                P.op("tensor", mmu, reads=[wslab_b, xqb_b], writes=[bu_b], tag="upmm")

            def mmg(e, mt=mt, bank_g=bank_g):
                last = None
                for k in range(DT):
                    last = e.matmul(bank_g[:, 0:FC],
                                    wslab[:, DT * II + k * II + mt * 128:DT * II + k * II + (mt + 1) * 128],
                                    xqb[:, k * FC:(k + 1) * FC], start=(k == 0), stop=(k == DT - 1))
                return last
            P.op("tensor", mmg, reads=[wslab_b, xqb_b], writes=[bg_b], tag="gmm")
            if phase == 2:
                _emit_amax_update(env, acc, acc_b, bank_u[:, 0:FC], [bu_b], "amu")
                P.op("vector", lambda e, bank_g=bank_g: e.reduce_max(
                    acc[:, 3:4], bank_g[:, 0:FC], axis=AX, apply_absolute_value=True),
                    reads=[bg_b, acc_b], writes=[acc_b], tag="amg")
                P.op("vector", lambda e: e.tensor_max(acc[:, 2:3], acc[:, 2:3], acc[:, 3:4]),
                     reads=[acc_b], writes=[acc_b], tag="amgm")
                continue
            P.op("vector", lambda e, bank_g=bank_g: e.tensor_scalar(
                gq[:, :], bank_g[:, 0:FC], scc[:, 5:6], MAGIC, ALU.mult, ALU.add),
                reads=[bg_b, scc_b], writes=[gq_b], tag="gq1")
            P.op("vector", lambda e: e.tensor_scalar(
                gq[:, :], gq[:, :], MAGIC, scc[:, 4:5], ALU.subtract, ALU.mult),
                reads=[gq_b, scc_b], writes=[gq_b], tag="gq2")
            P.op("scalar", lambda e: e.activation(sig[:, :], gq[:, :], AF.Sigmoid),
                 reads=[gq_b], writes=[sig_b], tag="sig")
            P.op("vector", lambda e: e.tensor_scalar(
                sig[:, :], sig[:, :], 65536.0, MAGIC, ALU.mult, ALU.add),
                reads=[sig_b], writes=[sig_b], tag="sq1")
            P.op("vector", lambda e: e.tensor_scalar(
                sig[:, :], sig[:, :], MAGIC, 65535.0, ALU.subtract, ALU.min),
                reads=[sig_b], writes=[sig_b], tag="sq2")
            P.op("vector", lambda e: e.scalar_tensor_tensor(
                gq[:, :], sig[:, :], 1.0 / 65536.0, gq[:, :], ALU.mult, ALU.mult),
                reads=[sig_b, gq_b], writes=[gq_b], tag="g2pre")
            if phase == 3:
                _emit_amax_update(env, acc, acc_b, gq[:, :], [gq_b], "amg2")
                continue
            P.op("vector", lambda e: e.tensor_scalar(
                gq[:, :], gq[:, :], scc[:, 7:8], MAGIC, ALU.mult, ALU.add),
                reads=[gq_b, scc_b], writes=[gq_b], tag="g2q1")
            P.op("vector", lambda e: e.tensor_scalar(
                gq[:, :], gq[:, :], MAGIC, scc[:, 6:7], ALU.subtract, ALU.mult),
                reads=[gq_b, scc_b], writes=[gq_b], tag="g2q2")
            P.op("vector", lambda e, bank_u=bank_u: e.tensor_scalar(
                upq[:, :], bank_u[:, 0:FC], scc[:, 3:4], MAGIC, ALU.mult, ALU.add),
                reads=[bu_b, scc_b], writes=[upq_b], tag="uq1")
            P.op("vector", lambda e: e.tensor_scalar(
                upq[:, :], upq[:, :], MAGIC, scc[:, 2:3], ALU.subtract, ALU.mult),
                reads=[upq_b, scc_b], writes=[upq_b], tag="uq2")
            P.op("vector", lambda e: e.tensor_mul(gq[:, :], gq[:, :], upq[:, :]),
                 reads=[gq_b, upq_b], writes=[gq_b], tag="opre")
            if phase == 4:
                _emit_amax_update(env, acc, acc_b, gq[:, :], [gq_b], "amo")
                continue
            P.op("vector", lambda e: e.tensor_scalar(
                gq[:, :], gq[:, :], scc[:, 9:10], MAGIC, ALU.mult, ALU.add),
                reads=[gq_b, scc_b], writes=[gq_b], tag="oq1")
            P.op("vector", lambda e, mt=mt: e.tensor_scalar(
                oqb[:, mt * FC:(mt + 1) * FC], gq[:, :], MAGIC, scc[:, 8:9], ALU.subtract, ALU.mult),
                reads=[gq_b, scc_b], writes=[oqb_b], tag="oq2")
        if phase == 5:
            P.op("sync", lambda e, ci=ci: e.dma_start(
                out=od[:, ci * NIT * FC:(ci + 1) * NIT * FC], in_=oqb[:, :]),
                reads=[oqb_b], writes=[od_b], dma=True, tag="ospill")

    if phase == 2:
        _emit_amax_final(env, acc, acc_b, identf, identf_b, aout, aout_b, 0)
        P.op("vector", lambda e: e.tensor_copy(acc[:, 0:1], acc[:, 2:3]),
             reads=[acc_b], writes=[acc_b], tag="swapg")
        _emit_amax_final(env, acc, acc_b, identf, identf_b, aout, aout_b, 1)
    if phase in (3, 4):
        _emit_amax_final(env, acc, acc_b, identf, identf_b, aout, aout_b, 0)
    if phase in (2, 3, 4):
        P.op("sync", lambda e: e.dma_start(out=drams["_amout"], in_=aout[:, :]),
             reads=[aout_b], writes=[drams["_amout_buf"]], dma=True, tag="amst")
        return

    def dequant_down():
        def one_dtile(ot, part, wt, wt_b, wtb, wtb_b, bmax, bmax_b):
            if True:
                P.op("sync", lambda e, ot=ot, part=part: e.dma_start(
                    out=wt[:, :], in_=drams["l2f_wdown"][ot * 128:(ot + 1) * 128, part * 1024:(part + 1) * 1024]),
                    writes=[wt_b], dma=True, tag="ddld")
                P.op("vector", lambda e: e.reduce_max(
                    bmax[:, 0:32], wt.ap().rearrange("p (b k) -> p b k", k=32), axis=AX,
                    apply_absolute_value=True),
                    reads=[wt_b], writes=[bmax_b], tag="ddbm")
                P.op("vector", lambda e: e.tensor_scalar(
                    bmax[:, 32:64], bmax[:, 0:32], 1.0 / 7.0, 1e-12, ALU.mult, ALU.max),
                    reads=[bmax_b], writes=[bmax_b], tag="dds")
                P.op("vector", lambda e: e.reciprocal(bmax[:, 64:96], bmax[:, 32:64]),
                     reads=[bmax_b], writes=[bmax_b], tag="ddrs")
                P.op("vector", lambda e: e.tensor_mul(
                    wt.ap().rearrange("p (b k) -> p b k", k=32),
                    wt.ap().rearrange("p (b k) -> p b k", k=32),
                    bmax[:, 64:96].rearrange("p (b o) -> p b o", o=1).broadcast_to([128, 32, 32])),
                    reads=[wt_b, bmax_b], writes=[wt_b], tag="dddiv")
                P.op("vector", lambda e: e.tensor_scalar(wt[:, :], wt[:, :], MAGIC, MAGIC, ALU.add, ALU.subtract),
                     reads=[wt_b], writes=[wt_b], tag="ddrnd")
                P.op("vector", lambda e: e.tensor_mul(
                    wt.ap().rearrange("p (b k) -> p b k", k=32),
                    wt.ap().rearrange("p (b k) -> p b k", k=32),
                    bmax[:, 32:64].rearrange("p (b o) -> p b o", o=1).broadcast_to([128, 32, 32])),
                    reads=[wt_b, bmax_b], writes=[wt_b], tag="ddmul")
                P.op("gpsimd", lambda e: e.dma_start(out=wtb[:, :], in_=wt[:, :]),
                     reads=[wt_b], writes=[wtb_b], dma=True, tag="ddcast")
                for k in range(8):
                    kt = part * 8 + k
                    bank, bk_b = env.bank_bf()
                    P.op("tensor", lambda e, k=k, bank=bank: e.transpose(
                        bank[0:128, 0:128], wtb[:, k * 128:(k + 1) * 128], identb[:, :]),
                        reads=[wtb_b, identb_b], writes=[bk_b], tag="ddtr")
                    P.op("scalar", lambda e, kt=kt, ot=ot, bank=bank: e.activation(
                        wslab[:, kt * D + ot * 128: kt * D + (ot + 1) * 128],
                        bank[0:128, 0:128], AF.Copy),
                        reads=[bk_b], writes=[wslab_b], tag="ddev")

        for ot in range(DT):
            for part in range(3):
                j = (ot * 3 + part) % 2
                one_dtile(ot, part, *wtl[j], *wtbl[j], *bml[j])

    dequant_down()
    for ci in range(NFC):
        f0 = ci * FC
        P.op("sync", lambda e, ci=ci: e.dma_start(
            out=oqb[:, :], in_=od[:, ci * NIT * FC:(ci + 1) * NIT * FC]),
            reads=[od_b], writes=[oqb_b], dma=True, tag="oload")
        P.op("sync", lambda e, f0=f0: e.dma_start(out=xc[:, :], in_=x_chunk_src(x_in, f0, FC)),
             reads=[x_in_b], writes=[xc_b], dma=True, tag="xload2")
        for mt in range(DT):
            bank, bk_b = env.bank()

            def mmd(e, mt=mt, bank=bank):
                last = None
                for k in range(NIT):
                    last = e.matmul(bank[:, 0:FC], wslab[:, k * D + mt * 128:k * D + (mt + 1) * 128],
                                    oqb[:, k * FC:(k + 1) * FC], start=(k == 0), stop=(k == NIT - 1))
                return last
            P.op("tensor", mmd, reads=[wslab_b, oqb_b], writes=[bk_b], tag="dmm")
            P.op("vector", lambda e, mt=mt, bank=bank: e.tensor_add(
                xc[:, mt * FC:(mt + 1) * FC], xc[:, mt * FC:(mt + 1) * FC], bank[:, 0:FC]),
                reads=[bk_b, xc_b], writes=[xc_b], tag="rad")
        emit_ln_chunk(env, xc, xc_b, DT, FC)
        P.op("sync", lambda e, f0=f0: e.dma_start(out=x_chunk_src(x_out, f0, FC), in_=xc[:, :]),
             reads=[xc_b], writes=[drams["_xout_buf"]], dma=True, tag="xstore")


STAGES = {"l3_ffn": emit_l3_ffn, "l0_ffn": emit_l0_ffn, "l3_attn": emit_l3_attn,
          "l0_attn": emit_l0_attn, "l1_attn": emit_l1_attn, "l1_ffn": emit_l1_ffn,
          "l2_attn": emit_l2_attn,
          "l2f_p5": lambda env, xi, xo, drams: emit_l2_ffn_phase(env, xi, xo, drams, 5)}



def build_l2f_program(phase, weights_meta):
    nc = bass.Bass("TRN2", target_bir_lowering=False, debug=False)
    es = ExitStack()
    P = Prog(nc, es)
    env = Env(nc, es, P)
    x_in = nc.dram_tensor("x_io", [D, TOK], F32, kind="ExternalInput").ap()
    drams = {}
    for name, shape in weights_meta.items():
        drams[name] = nc.dram_tensor(name, list(shape), F32, kind="ExternalInput").ap()
    drams["_xin_buf"] = P.buf("xin")
    if phase == 5:
        x_out = nc.dram_tensor("x_res", [D, TOK], F32, kind="ExternalOutput").ap()
        drams["_xout_buf"] = P.buf("xout")
        emit_l2_ffn_phase(env, x_in, x_out, drams, phase)
        last = drams["_xout_buf"].last_w
    else:
        x_out = None
        drams["_amout"] = nc.dram_tensor("amout", [1, 2], F32, kind="ExternalOutput").ap()
        drams["_amout_buf"] = P.buf("amout")
        emit_l2_ffn_phase(env, x_in, x_out, drams, phase)
        last = drams["_amout_buf"].last_w
    P.op("sync", lambda e: e.nop(), extra_waits=[last], tag="final")
    P.emit()
    es.close()
    return nc


def l2_ffn_weights(inputs):
    w = {
        "l2f_wup": np.ascontiguousarray(np.asarray(inputs["L2_ffn_w_up"], np.float32)),
        "l2f_wgate": np.ascontiguousarray(np.asarray(inputs["L2_ffn_w_gate"], np.float32)),
        "l2f_wdown": np.ascontiguousarray(np.asarray(inputs["L2_ffn_w_down"], np.float32)),
    }
    w.update(_consts())
    return w


def run_l2_ffn(x, wmap, sim=False):
    shards = _shard_x(x)
    amg = {}
    for phase in range(1, 6):
        pw = dict(wmap)
        for k, v in amg.items():
            pw[k] = v
        meta = {k: v.shape for k, v in pw.items()}
        nc = build_l2f_program(phase, meta)
        in_maps = [dict(pw, x_io=s) for s in shards]
        res = run_bass_kernel_spmd(nc, in_maps, core_ids=list(range(NCORES)))
        if phase < 5:
            outs = [r["amout"] for r in res.results]
            if phase == 1:
                amg["l2f_am0g"] = np.concatenate([o[0:1, 0:1] for o in outs], axis=1)
            elif phase == 2:
                amg["l2f_am1g"] = np.concatenate(
                    [np.concatenate([o[0:1, 0:1] for o in outs], axis=1),
                     np.concatenate([o[0:1, 1:2] for o in outs], axis=1)], axis=1)
            elif phase == 3:
                amg["l2f_am2g"] = np.concatenate([o[0:1, 0:1] for o in outs], axis=1)
            elif phase == 4:
                amg["l2f_am3g"] = np.concatenate([o[0:1, 0:1] for o in outs], axis=1)
        else:
            return _unshard_x([r["x_res"] for r in res.results])


SIDE_STAGES = {"l2f_p1": 1, "l2f_p2": 2, "l2f_p3": 3, "l2f_p4": 4}


def build_program(stage_names, weights_meta):
    nc = bass.Bass("TRN2", target_bir_lowering=False, debug=False)
    es = ExitStack()
    P = Prog(nc, es)
    env = Env(nc, es, P)

    n_x = sum(1 for s in stage_names if s not in SIDE_STAGES)
    xs = []
    for i in range(n_x + 1):
        if i == 0:
            t = nc.dram_tensor("x_io", [D, TOK], F32, kind="ExternalInput")
        elif i == n_x:
            t = nc.dram_tensor("x_res", [D, TOK], F32, kind="ExternalOutput")
        else:
            t = nc.dram_tensor(f"x_mid{i}", [D, TOK], F32, kind="Internal")
        xs.append(t.ap())

    drams = {}
    for name, shape in weights_meta.items():
        drams[name] = nc.dram_tensor(name, list(shape), F32, kind="ExternalInput").ap()
    if any(s in SIDE_STAGES for s in stage_names):
        drams["_amout"] = nc.dram_tensor("amout", [1, 2], F32, kind="ExternalOutput").ap()
        drams["_amout_buf"] = P.buf("amout")

    env.ln_scratch(FC)
    env.ln_scratch(128)
    env.const_col(LN_EPS)
    env.const_col(1.0)
    xi = 0
    last_out_token = None
    final_tokens = []
    for i, sname in enumerate(stage_names):
        drams["_xin_buf"] = P.buf(f"xin{i}")
        drams["_xout_buf"] = P.buf(f"xout{i}")
        if xi > 0:
            drams["_xin_buf"].last_w = last_out_token
        scope = ExitStack()
        env._scope = scope
        if sname in SIDE_STAGES:
            emit_l2_ffn_phase(env, xs[xi], None, drams, SIDE_STAGES[sname])
            final_tokens.append(drams["_amout_buf"].last_w)
        else:
            STAGES[sname](env, xs[xi], xs[xi + 1], drams)
            xi += 1
            last_out_token = drams["_xout_buf"].last_w
        scope.close()
        env._scope = None
        P.barrier()

    if n_x > 0:
        final_tokens.append(last_out_token)
    P.op("sync", lambda e: e.nop(), extra_waits=[t for t in final_tokens if t], tag="final")
    P.emit()
    es.close()
    return nc



def _shard_x(x):
    shards = []
    for c in range(NCORES):
        s = x[c * BPC:(c + 1) * BPC]
        shards.append(np.ascontiguousarray(s.transpose(2, 0, 1).reshape(D, TOK)))
    return shards


def _unshard_x(outs):
    full = np.empty((NCORES * BPC, T, D), np.float32)
    for c, o in enumerate(outs):
        full[c * BPC:(c + 1) * BPC] = o.reshape(D, BPC, T).transpose(1, 2, 0)
    return full


def _bias_tiles(b):
    n = b.shape[0] // 128
    return np.ascontiguousarray(b.reshape(n, 128).T)


def l0_ffn_weights(inputs):
    return {
        "l0f_u1": np.ascontiguousarray(np.asarray(inputs["L0_ffn_U1"], np.float32)),
        "l0f_v1": np.ascontiguousarray(np.asarray(inputs["L0_ffn_V1"], np.float32)),
        "l0f_b1": _bias_tiles(np.asarray(inputs["L0_ffn_b1"], np.float32)),
        "l0f_u2": np.ascontiguousarray(np.asarray(inputs["L0_ffn_U2"], np.float32)),
        "l0f_v2": np.ascontiguousarray(np.asarray(inputs["L0_ffn_V2"], np.float32)),
        "l0f_b2": _bias_tiles(np.asarray(inputs["L0_ffn_b2"], np.float32)),
    }


def _consts():
    tri = (np.arange(128)[:, None] <= np.arange(128)[None, :]).astype(np.float32)
    return {
        "ident_f32": np.eye(128, dtype=np.float32),
        "triu01": np.ascontiguousarray(tri),
        "ramp2048": np.arange(T, dtype=np.float32).reshape(1, T),
    }


def l0_attn_weights(inputs):
    import ml_dtypes
    k = np.arange(T, dtype=np.float64)
    ak = np.zeros((16, 4, T), np.float32)
    aq = np.zeros((16, 4, T), np.float32)
    for h in range(16):
        s8 = 8.0 * 2.0 ** (-(h + 1) / 2.0)
        for dst, v in ((ak, s8 * k), (aq, -s8 * k)):
            vf = v.astype(np.float32)
            hi = vf.astype(ml_dtypes.bfloat16).astype(np.float32)
            lo = (vf - hi).astype(ml_dtypes.bfloat16).astype(np.float32)
            if dst is ak:
                dst[h, 0], dst[h, 1], dst[h, 2], dst[h, 3] = hi, lo, 1.0, 1.0
            else:
                dst[h, 0], dst[h, 1], dst[h, 2], dst[h, 3] = 1.0, 1.0, hi, lo
    w = {
        "l0a_wqt": np.ascontiguousarray(np.asarray(inputs["L0_attn_Wq"], np.float32).T),
        "l0a_wkt": np.ascontiguousarray(np.asarray(inputs["L0_attn_Wk"], np.float32).T),
        "l0a_wvt": np.ascontiguousarray(np.asarray(inputs["L0_attn_Wv"], np.float32).T),
        "l0a_wot": np.ascontiguousarray(np.asarray(inputs["L0_attn_Wo"], np.float32).T),
        "alibi_k": ak,
        "alibi_q": aq,
        "trineg": np.where(np.arange(128)[:, None] <= np.arange(128)[None, :],
                           0.0, -1e30).astype(np.float32),
    }
    w.update(_consts())
    return w


def l1_ffn_weights(inputs):
    w = {
        "l1f_routert": np.ascontiguousarray(np.asarray(inputs["L1_ffn_router_w"], np.float32).T),
        "l1f_gatet": np.ascontiguousarray(np.asarray(inputs["L1_ffn_gate_w"], np.float32).transpose(0, 2, 1)),
        "l1f_upt": np.ascontiguousarray(np.asarray(inputs["L1_ffn_up_w"], np.float32).transpose(0, 2, 1)),
        "l1f_downt": np.ascontiguousarray(np.asarray(inputs["L1_ffn_down_w"], np.float32).transpose(0, 2, 1)),
    }
    w.update(_consts())
    return w


def l2_attn_weights(inputs):
    convw = np.asarray(inputs["L2_attn_conv_w"], np.float32)[:, 0, :]
    w = {
        "l2a_inprojt": np.ascontiguousarray(np.asarray(inputs["L2_attn_in_proj_w"], np.float32).T),
        "l2a_convw": np.ascontiguousarray(convw.reshape(16, 128, 4).transpose(1, 0, 2).reshape(128, 64)),
        "l2a_convb": np.ascontiguousarray(np.asarray(inputs["L2_attn_conv_b"], np.float32).reshape(16, 128).T),
        "l2a_xprojt": np.ascontiguousarray(np.asarray(inputs["L2_attn_x_proj_w"], np.float32).T),
        "l2a_dtprojt": np.ascontiguousarray(np.asarray(inputs["L2_attn_dt_proj_w"], np.float32).T),
        "l2a_dtb": np.ascontiguousarray(np.asarray(inputs["L2_attn_dt_proj_b"], np.float32).reshape(16, 128).T),
        "l2a_alog": np.ascontiguousarray(np.asarray(inputs["L2_attn_A_log"], np.float32)
                                         .reshape(16, 128, 64).transpose(1, 0, 2).reshape(128, 1024)),
        "l2a_outprojt": np.ascontiguousarray(np.asarray(inputs["L2_attn_out_proj_w"], np.float32).T),
    }
    w.update(_consts())
    return w


def l1_attn_weights(inputs):
    w = {
        "l1a_wqkvt": np.ascontiguousarray(np.asarray(inputs["L1_attn_w_qkv"], np.float32)),
        "l1a_wot": np.ascontiguousarray(np.asarray(inputs["L1_attn_w_out"], np.float32)),
        "l1a_bout": _bias_tiles(np.asarray(inputs["L1_attn_b_out"], np.float32)),
    }
    w.update(_consts())
    return w


def l3_attn_weights(inputs):
    w = {
        "l3a_wqt": np.ascontiguousarray(np.asarray(inputs["L3_attn_WQ"], np.float32).T),
        "l3a_wkt": np.ascontiguousarray(np.asarray(inputs["L3_attn_WK"], np.float32).T),
        "l3a_wvt": np.ascontiguousarray(np.asarray(inputs["L3_attn_WV"], np.float32).T),
    }
    w.update(_consts())
    return w


def l3_ffn_weights(inputs):
    return {
        "l3f_w1t": np.ascontiguousarray(np.asarray(inputs["L3_ffn_w1"], np.float32)),
        "l3f_b1": _bias_tiles(np.asarray(inputs["L3_ffn_b1"], np.float32)),
        "l3f_w2t": np.ascontiguousarray(np.asarray(inputs["L3_ffn_w2"], np.float32)),
        "l3f_b2": _bias_tiles(np.asarray(inputs["L3_ffn_b2"], np.float32)),
    }


def run_stages(stage_names, x, wmap, sim=False):
    meta = {k: v.shape for k, v in wmap.items()}
    nc = build_program(stage_names, meta)
    shards = _shard_x(x)
    in_maps = [dict(wmap, x_io=s) for s in shards]
    if sim:
        import concourse.bass_interp as bass_interp
        simm = bass_interp.MultiCoreSim(nc, len(in_maps))
        for i, m in enumerate(in_maps):
            for k, v in m.items():
                simm.cores[i].tensor(k)[:] = v
        simm.simulate()
        outs = [np.array(simm.cores[i].mem_tensor("x_res")) for i in range(len(in_maps))]
        return _unshard_x(outs), None
    res = run_bass_kernel_spmd(nc, in_maps, core_ids=list(range(NCORES)))
    has_x = any(s not in SIDE_STAGES for s in stage_names)
    xo = _unshard_x([r["x_res"] for r in res.results]) if has_x else None
    ams = [r.get("amout") for r in res.results] if "amout" in res.results[0] else None
    return xo, ams


def _cat_ams(ams, col):
    return np.ascontiguousarray(np.concatenate([a[0:1, col:col + 1] for a in ams], axis=1))


def kernel(**inputs):
    x = np.asarray(inputs["L0_attn_x"], np.float32)
    l2fw = l2_ffn_weights(inputs)
    w1 = {}
    w1.update(l0_attn_weights(inputs))
    w1.update(l0_ffn_weights(inputs))
    w1.update(l1_attn_weights(inputs))
    w1.update(l1_ffn_weights(inputs))
    w1.update(l2_attn_weights(inputs))
    w1.update(l2fw)
    x, ams = run_stages(
        ["l0_attn", "l0_ffn", "l1_attn", "l1_ffn", "l2_attn", "l2f_p1"], x, w1)
    am = {"l2f_am0g": _cat_ams(ams, 0)}
    _, ams = run_stages(["l2f_p2"], x, dict(l2fw, **am))
    am["l2f_am1g"] = np.ascontiguousarray(
        np.concatenate([_cat_ams(ams, 0), _cat_ams(ams, 1)], axis=1))
    _, ams = run_stages(["l2f_p3"], x, dict(l2fw, **am))
    am["l2f_am2g"] = _cat_ams(ams, 0)
    _, ams = run_stages(["l2f_p4"], x, dict(l2fw, **am))
    am["l2f_am3g"] = _cat_ams(ams, 0)
    w5 = dict(l2fw, **am)
    w5.update(l3_attn_weights(inputs))
    w5.update(l3_ffn_weights(inputs))
    x, _ = run_stages(["l2f_p5", "l3_attn", "l3_ffn"], x, w5)
    return x
